# version 03 (final)
# baseline (speedup 1.0000x reference)
import os
import numpy as np
import ml_dtypes
from contextlib import ExitStack

import concourse.bass as bass
import concourse.mybir as mybir
from concourse.bass_utils import run_bass_kernel_spmd

F32 = mybir.dt.float32
BF16 = mybir.dt.bfloat16
AF = mybir.ActivationFunctionType
ALU = mybir.AluOpType
AX = mybir.AxisListType

NCORES = 8
B = 16
T = 2048
D = 1024
B_LOC = B // NCORES
NTOK = B_LOC * T
D_STATE = 128
HEADDIM = 64
NHEADS = 32
D_INNER = 2048
CONV_DIM = 2304
D_IN_PROJ = 4384
D_CONV = 4
LCH = 128
GRP = 256
NGRP = NTOK // GRP
NGRP_EFF = int(os.environ.get("MK_NGRP", "0")) or NTOK // GRP
GPB = NGRP // B_LOC
CHG = GRP // LCH
POOL = 50
SUMD = 64
SCORER_HID = 256
EPS = 1e-5
NEXP = 16
DFF = 512
LN05 = float(np.log(0.5))

DBG = os.environ.get("MK_DBG", "")


def _consts():
    c = {}
    c["eye_bf"] = np.eye(128, dtype=ml_dtypes.bfloat16)
    c["eye_f32"] = np.eye(128, dtype=np.float32)
    j = np.arange(128)[:, None]
    i = np.arange(128)[None, :]
    c["maskc_bf"] = np.where(j > i, -1e30, 0.0).astype(ml_dtypes.bfloat16)
    c["ones2_bf"] = np.ones((2, 128), dtype=ml_dtypes.bfloat16)
    c["ones1_f"] = np.ones((1, 128), dtype=np.float32)
    c["onescol_bf"] = np.ones((128, 1), dtype=ml_dtypes.bfloat16)
    m64 = np.zeros((64, 32 * 128), dtype=ml_dtypes.bfloat16)
    for h in range(32):
        m64[h, h * 128:(h + 1) * 128] = 1.0
        m64[32 + h, h * 128:(h + 1) * 128] = 1.0
    c["mask64_bf"] = m64
    sele = np.zeros((16, 16 * 128), dtype=ml_dtypes.bfloat16)
    for ex in range(16):
        sele[ex, ex * 128:(ex + 1) * 128] = 1.0
    c["sele_bf"] = sele
    return c


CONSTS = _consts()


NDMASEM = 8

class Sched:
    def __init__(self):
        self.ops = []
        self.cnt = {k: 0 for k in ("T", "V", "S", "G", "CC")}
        for i in range(NDMASEM):
            self.cnt[f"DS{i}"] = 0
            self.cnt[f"DG{i}"] = 0
        self.rr = {"DS": 0, "DG": 0}

    def _dma_sem(self, base):
        i = self.rr[base]
        self.rr[base] = (i + 1) % NDMASEM
        return f"{base}{i}"

    def _add(self, eng, emit, deps, inc_sem, inc_amt, name):
        self.cnt[inc_sem] += inc_amt
        tok = (inc_sem, self.cnt[inc_sem])
        deps = tuple(d for d in deps if d is not None)
        self.ops.append((eng, emit, deps, inc_sem, inc_amt, name))
        return tok

    def t(self, emit, deps=(), name=""):
        return self._add("T", emit, deps, "T", 1, name)

    def v(self, emit, deps=(), name=""):
        return self._add("V", emit, deps, "V", 1, name)

    def s(self, emit, deps=(), name=""):
        return self._add("S", emit, deps, "S", 1, name)

    def g(self, emit, deps=(), name=""):
        return self._add("G", emit, deps, "G", 1, name)

    def dma(self, queue, emit, deps=(), name=""):
        sem = self._dma_sem("DS" if queue == "SYNC" else "DG")
        eng = "SYNC" if queue == "SYNC" else "G"
        deps = list(deps)
        if self.cnt[sem]:
            deps.append((sem, self.cnt[sem]))
        return self._add(eng, emit, deps, sem, 16, name)

    def dma_multi(self, queue, emit, n, deps=(), name=""):
        sem = self._dma_sem("DS" if queue == "SYNC" else "DG")
        eng = "SYNC" if queue == "SYNC" else "G"
        deps = list(deps)
        if self.cnt[sem]:
            deps.append((sem, self.cnt[sem]))
        self.cnt[sem] += 16 * n
        tok = (sem, self.cnt[sem])
        deps = tuple(d for d in deps if d is not None)
        self.ops.append((eng, ("MULTI", emit, sem), deps, sem, 0, name))
        return tok

    def cc(self, emit, deps=(), name=""):
        return self._add("G", emit, deps, "CC", 1, name)


_OWN = {"T": "T", "V": "V", "S": "S", "G": "G"}


def emit_engine_stream(sched, eng_name, handle, sems):
    hw = {k: 0 for k in sched.cnt}
    own_sem = _OWN.get(eng_name)
    for (eng, emit, deps, inc_sem, inc_amt, name) in sched.ops:
        if eng != eng_name:
            continue
        for (dsem, dval) in deps:
            if eng_name == "T" and dsem == "T":
                continue
            if hw[dsem] >= dval:
                continue
            handle.wait_ge(sems[dsem], dval)
            hw[dsem] = dval
        if isinstance(emit, tuple) and emit[0] == "MULTI":
            emit[1](handle, sems[emit[2]])
        else:
            last = emit(handle)
            if last is None:
                raise RuntimeError(f"op '{name}' on {eng} emitted nothing")
            last.then_inc(sems[inc_sem], inc_amt)


def build_program(dbg=()):
    nc = bass.Bass("TRN2", target_bir_lowering=False, debug=False)
    sc = Sched()

    XI = nc.dram_tensor("x", [B_LOC, T, D], F32, kind="ExternalInput").ap()
    W = {}
    for nm, shp in [
        ("norm_w", [D]), ("in_proj_w", [D, D_IN_PROJ]), ("conv_w", [CONV_DIM, D_CONV]),
        ("conv_b", [CONV_DIM]), ("dt_bias", [NHEADS]), ("A_log", [NHEADS]), ("Dp", [NHEADS]),
        ("ssm_norm_w", [D_INNER]), ("out_proj_w", [D_INNER, D]),
        ("scorer_w1", [D, SCORER_HID]), ("scorer_w2", [SCORER_HID, 1]),
        ("summ_w", [D, SUMD]), ("q_w", [D, SUMD]), ("k_w", [SUMD, SUMD]), ("v_w", [SUMD, D]),
        ("gate_w", [2 * D, D]),
        ("ffn_gate_w", [D, NEXP]), ("ffn_gate_b", [NEXP]),
        ("shared_wg", [D, DFF]), ("shared_wu", [D, DFF]), ("shared_wd", [DFF, D]),
        ("experts_wg", [NEXP, D, DFF]), ("experts_wu", [NEXP, D, DFF]), ("experts_wd", [NEXP, DFF, D]),
        ("normw_t", [128, 8]), ("ssmw_t", [128, 16]), ("convw_t", [128, 72]),
        ("convb_t", [128, 18]), ("dp_rep", [128, 16]),
    ]:
        W[nm] = nc.dram_tensor(nm, shp, F32, kind="ExternalInput").ap()
    C = {}
    for nm, arr in CONSTS.items():
        dt = BF16 if arr.dtype == ml_dtypes.bfloat16 else F32
        C[nm] = nc.dram_tensor("c_" + nm, list(arr.shape), dt, kind="ExternalInput").ap()

    OUT = nc.dram_tensor("out", [B_LOC, T, D], F32, kind="ExternalOutput").ap()
    YSC = nc.dram_tensor("y_scratch", [8, 128, NTOK], BF16)
    dbg_outs = {}

    def dbg_tensor(name, shp, dt=F32):
        if name in dbg:
            dbg_outs[name] = nc.dram_tensor("dbg_" + name, shp, dt, kind="ExternalOutput").ap()
            return dbg_outs[name]
        return None

    with ExitStack() as ctx:
        e = ctx.enter_context
        sb = lambda name, shp, dt=F32: e(nc.sbuf_tensor(name, shp, dt))
        ps = lambda name, shp: e(nc.psum_tensor(name, shp, F32))
        semnames = ["T", "V", "S", "G", "CC"] + [f"DS{i}" for i in range(8)] + [f"DG{i}" for i in range(8)]
        sems = {k: e(nc.semaphore("sem_" + k)) for k in semnames}

        eye_bf = sb("eye_bf", [128, 128], BF16)
        eye_f = sb("eye_f", [128, 128], F32)
        maskc_bf = sb("maskc_bf", [128, 128], BF16)
        ones2_bf = sb("ones2_bf", [2, 128], BF16)
        ones1_f = sb("ones1_f", [1, 128], F32)
        onescol_bf = sb("onescol_bf", [128, 1], BF16)
        mask64 = sb("mask64", [64, 32 * 128], BF16)

        wip = sb("wip", [128, 8 * D_IN_PROJ], BF16)
        wop = sb("wop", [128, 16 * D], BF16)
        convw = sb("convw", [128, 72], F32)
        convb = sb("convb", [128, 18], F32)
        dtb_col = sb("dtb_col", [32, 1], F32)
        negA_col = sb("negA_col", [32, 1], F32)
        d128 = sb("d128", [128, 16], F32)
        ln05_col = sb("ln05_col", [128, 1], F32)

        tok_const = []

        def ld(queue, dst, src, deps=(), name="ld"):
            return sc.dma(queue, lambda h, dst=dst, src=src: h.dma_start(out=dst, in_=src),
                          deps=deps, name=name)

        tok_const.append(ld("SYNC", eye_bf[:, :], C["eye_bf"], name="c_eye"))
        tok_const.append(ld("SYNC", eye_f[:, :], C["eye_f32"], name="c_eyef"))
        tok_const.append(ld("SYNC", maskc_bf[:, :], C["maskc_bf"], name="c_mask"))
        tok_const.append(ld("SYNC", ones2_bf[:, :], C["ones2_bf"], name="c_ones2"))
        tok_const.append(ld("SYNC", ones1_f[:, :], C["ones1_f"], name="c_ones1"))
        tok_const.append(ld("SYNC", onescol_bf[:, :], C["onescol_bf"], name="c_onesc"))
        tok_const.append(ld("SYNC", mask64[:, :], C["mask64_bf"], name="c_m64"))

        nrm_stage = sb("nrm_stage", [128, 8], F32)
        t_nw = ld("SYNC", nrm_stage[:, :], W["normw_t"], name="normw")
        toks_w = []
        for k in range(8):
            t1 = sc.dma("G", lambda h, k=k: h.dma_start(
                out=wip[:, k * D_IN_PROJ:(k + 1) * D_IN_PROJ],
                in_=W["in_proj_w"][k * 128:(k + 1) * 128, :]), name=f"wip{k}")
            toks_w.append(sc.v(lambda h, k=k: h.tensor_scalar(
                out=wip[:, k * D_IN_PROJ:(k + 1) * D_IN_PROJ],
                in0=wip[:, k * D_IN_PROJ:(k + 1) * D_IN_PROJ],
                scalar1=nrm_stage[:, k:k + 1], scalar2=None, op0=ALU.mult),
                deps=[t_nw, t1], name=f"wipsc{k}"))
        ssm_stage = sb("ssm_stage", [128, 16], F32)
        t_sw = ld("SYNC", ssm_stage[:, :], W["ssmw_t"], name="ssmw")
        for k in range(16):
            t1 = sc.dma("G", lambda h, k=k: h.dma_start(
                out=wop[:, k * D:(k + 1) * D],
                in_=W["out_proj_w"][k * 128:(k + 1) * 128, :]), name=f"wop{k}")
            toks_w.append(sc.v(lambda h, k=k: h.tensor_scalar(
                out=wop[:, k * D:(k + 1) * D], in0=wop[:, k * D:(k + 1) * D],
                scalar1=ssm_stage[:, k:k + 1], scalar2=None, op0=ALU.mult),
                deps=[t_sw, t1], name=f"wopsc{k}"))

        toks_w.append(ld("SYNC", convw[:, :], W["convw_t"], name="convw"))
        toks_w.append(ld("SYNC", convb[:, :], W["convb_t"], name="convb"))

        smallv = sb("smallv", [32, 4], F32)
        t_sv = [ld("SYNC", smallv[:, 0:1], W["dt_bias"].rearrange("(p one) -> p one", one=1), name="dtb"),
                ld("SYNC", smallv[:, 1:2], W["A_log"].rearrange("(p one) -> p one", one=1), name="alog")]
        toks_w.append(sc.v(lambda h: h.tensor_copy(out=dtb_col[:, :], in_=smallv[:, 0:1]), deps=t_sv, name="dtbcp"))
        t_na = sc.s(lambda h: h.activation(out=negA_col[:, :], in_=smallv[:, 1:2], func=AF.Exp), deps=t_sv, name="negAe")
        toks_w.append(sc.v(lambda h: h.tensor_scalar(out=negA_col[:, :], in0=negA_col[:, :],
                                                     scalar1=-1.0, scalar2=None, op0=ALU.mult),
                           deps=[t_na], name="negAn"))
        t_d1 = ld("SYNC", d128[:, :], W["dp_rep"], name="d128")
        toks_w.append(sc.v(lambda h: h.tensor_scalar(out=d128[:, :], in0=d128[:, :],
                                                     scalar1=0.5, scalar2=None, op0=ALU.mult),
                           deps=[t_d1], name="d128h"))
        toks_w.append(sc.v(lambda h: h.memset(ln05_col[:, :], LN05), name="ln05"))
        weights_ready = toks_w + tok_const

        xtok = [sb(f"xtok{i}", [128, D], BF16) for i in range(2)]
        ssq = sb("ssq", [128, 2], F32)
        rs_col = sb("rs_col", [128, 2], F32)
        diag_s = sb("diag_s", [128, 128], BF16)
        sq_dump = sb("sq_dump", [128, D], BF16)
        xnT = sb("xnT", [128, 8 * GRP], BF16)
        zs2 = [sb(f"zs2{i}", [128, 16 * GRP], BF16) for i in range(2)]
        zt_tmp = sb("zt_tmp", [128, GRP], BF16)
        xbcu = [sb(f"xbcu{i}", [128, 18 * (GRP + 3)], BF16) for i in range(2)]
        hsave = [sb(f"hsave{i}", [128, 18 * 3], BF16) for i in range(2)]
        convacc = [sb(f"cva{i}", [128, GRP], F32) for i in range(2)]
        convt_tmp = sb("convt_tmp", [128, GRP], BF16)
        dtg = sb("dtg", [32, GRP], F32)
        dt_abs = sb("dt_abs", [32, GRP], F32)

        dAc = sb("dAc", [32, LCH], F32)
        csc = sb("csc", [32, LCH], F32)
        cs32f = sb("cs32f", [32, LCH], F32)
        cshl = sb("cshl", [64, LCH], BF16)
        cshlneg = sb("cshlneg", [64, LCH], BF16)
        csi2 = [sb(f"csi2{i}", [2, 4 * LCH], BF16) for i in range(2)]
        dtT = sb("dtT", [128, 32], F32)
        decc = sb("decc", [32, LCH], F32)
        decT = sb("decT", [128, 32], F32)
        lamrow = sb("lamrow", [1, 32], F32)
        lamb = sb("lamb", [128, 32], F32)
        xdT = sb("xdT", [128, D_INNER], BF16)
        xdec = [sb(f"xdec{i}", [128, HEADDIM], BF16) for i in range(2)]
        BuT = sb("BuT", [128, 128], BF16)
        GT = sb("GT", [128, 128], BF16)
        exp05 = sb("exp05", [128, 4 * LCH], BF16)
        L_sb = sb("L_sb", [128, 4 * LCH], BF16)
        m_tok = sb("m_tok", [128, 4 * LCH], BF16)
        m_aug = sb("m_aug", [128, 4 * LCH], BF16)
        prevst = sb("prevst", [128, D_INNER], BF16)
        t1_sb = sb("t1_sb", [128, LCH], BF16)
        y2sq = sb("y2sq", [128, GRP], BF16)
        y2 = [sb(f"y2{i}", [128, 16 * GRP], BF16) for i in range(2)]
        srow = sb("srow", [1, GRP], F32)
        srow_bf = sb("srow_bf", [1, GRP], BF16)
        sb_bf = sb("sb_bf", [128, GRP], BF16)
        yfm = sb("yfm", [128, 8 * GRP], BF16)


        p_ip_t = ps("p_ip", [128, 1024])
        p_inproj = [p_ip_t[:, 0:GRP], p_ip_t[:, 512:512 + GRP]]
        p_T_t = ps("p_Tt", [128, 8 * LCH])
        p_T = [p_T_t[:, 0:4 * LCH], p_T_t[:, 4 * LCH:8 * LCH]]
        p_b4 = ps("p_b4", [128, 512])
        p_b7 = ps("p_b7", [128, 512])
        p_y = [p_b4[:, 0:128], p_b7[:, 0:128]]
        p_st = [p_b4[:, 128:192], p_b7[:, 128:192]]
        p_trb_t = e(nc.psum_tensor("p_trb", [128, 256], BF16))
        p_trb = p_trb_t[:, 0:128]
        p_b6 = ps("p_b6", [128, 512])
        p_ftr = [p_b6[:, 0:32], p_b6[:, 32:64]]
        p_lam = p_b6[:, 64:96]
        p_gt = p_b6[:, 128:256]
        p_sb = p_b6[:, 256:512]

        d_xnT = dbg_tensor("xnT", [8 * 128, GRP])
        d_dt = dbg_tensor("dt", [32, GRP])
        d_zs2 = dbg_tensor("zs2", [16 * 128, GRP])
        d_xs = dbg_tensor("xs", [18 * 128, GRP])
        d_y2 = dbg_tensor("y2", [16 * 128, GRP])
        d_L = dbg_tensor("Lmat", [128, 8 * LCH])
        d_e05 = dbg_tensor("e05", [128, 8 * LCH])
        d_cs = dbg_tensor("cs", [32, LCH])
        d_Tm = dbg_tensor("Tm", [128, 8 * LCH])
        d_csi = dbg_tensor("csi", [2, 8 * LCH])
        Tm_cap = None
        d_y = dbg_tensor("y", [8 * 128, GRP])

        chains = {k: None for k in ("ip0", "ip1", "T0", "T1", "b4", "b7", "b5", "b6")}

        def chn(key, deps):
            return list(deps) + ([chains[key]] if chains[key] is not None else [])

        xdec_slot_last = [None] * 2
        csi2_slot_last = [[], []]
        lam_last = []
        cva_slot_last = [None] * 2
        prev_zt = None
        prev_ct = None
        prev_xnT_readers = []
        prev_xtok_readers = [[], []]
        prev_xtok_writer = [None, None]
        prev_xs_group = {}
        prev_zs_group = {}
        prev_dt_readers = []
        prev_ssd_shared = []
        prev_upd = [None] * NHEADS
        prev_aug = [None] * NHEADS
        last_maug = None
        last_mtok = None
        y2_reuse = {}
        yfm_reuse = []
        prev_fin_ap = []
        y2_group = {}
        ga_done = []
        ysc_dmas = []

        def finish_group(gp):
            par = gp % 2
            sumsq_tok, y2_toks = y2_group[gp]
            tok_ms = sc.v(lambda h: h.tensor_scalar(
                out=srow[:, :], in0=p_sb[0:1, :], scalar1=1.0 / D_INNER, scalar2=EPS,
                op0=ALU.mult, op1=ALU.add), deps=chn("b6", [sumsq_tok]), name=f"fms{gp}")
            chains["b6"] = tok_ms
            tok_l = sc.s(lambda h: h.activation(out=srow[:, :], in_=srow[:, :], func=AF.Ln),
                         deps=[tok_ms], name=f"fln{gp}")
            tok_r = sc.s(lambda h: h.activation(out=srow[:, :], in_=srow[:, :], func=AF.Exp,
                                                scale=-0.5), deps=[tok_l], name=f"frs{gp}")
            tok_bf = sc.v(lambda h: h.tensor_copy(out=srow_bf[:, :], in_=srow[:, :]),
                          deps=[tok_r], name=f"fbf{gp}")
            tok_bc = sc.t(lambda h: h.matmul(p_sb[:, :], ones2_bf[0:1, :], srow_bf[0:1, :],
                                             start=True, stop=True),
                          deps=chn("b6", [tok_bf]), name=f"fbc{gp}")
            chains["b6"] = tok_bc
            tok_sb = sc.s(lambda h: h.copy(out=sb_bf[:, :], in_=p_sb[:, :]),
                          deps=chn("b6", prev_fin_ap), name=f"fsb{gp}")
            chains["b6"] = tok_sb
            ap_toks = []
            for ct in range(16):
                ap_toks.append(sc.v(lambda h, ct=ct: h.tensor_tensor(
                    out=y2[par][:, ct * GRP:(ct + 1) * GRP], in0=y2[par][:, ct * GRP:(ct + 1) * GRP],
                    in1=sb_bf[:, :], op=ALU.mult), deps=[tok_sb] + (y2_toks if ct == 0 else []),
                    name=f"fap{gp}_{ct}"))
            op_ev = []
            for m in range(8):
                si = m % 2

                def mk_op(h, m=m, par=par, si=si):
                    last = None
                    for k in range(16):
                        last = h.matmul(p_inproj[si][:, :],
                                        wop[:, k * D + m * 128:k * D + (m + 1) * 128],
                                        y2[par][:, k * GRP:(k + 1) * GRP],
                                        start=(k == 0), stop=(k == 15))
                    return last
                tok_mm = sc.t(mk_op, deps=chn(f"ip{si}", ap_toks), name=f"fop{gp}_{m}")
                chains[f"ip{si}"] = tok_mm
                tok_ev = sc.s(lambda h, m=m, si=si: h.copy(
                    out=yfm[:, m * GRP:(m + 1) * GRP], in_=p_inproj[si][:, :]),
                    deps=chn(f"ip{si}", [tok_mm] + list(yfm_reuse)), name=f"fev{gp}_{m}")
                chains[f"ip{si}"] = tok_ev
                op_ev.append(tok_ev)
            gt0 = gp * GRP
            tok_dma = sc.dma("SYNC", lambda h, gt0=gt0: h.dma_start(
                out=YSC.ap()[:, :, gt0:gt0 + GRP].rearrange("k p t -> p k t"),
                in_=yfm[:, :]), deps=op_ev, name=f"fyd{gp}")
            if d_y is not None and gp == 0:
                sc.dma("G", lambda h: h.dma_start(
                    out=d_y.rearrange("(k p) t -> p k t", k=8), in_=yfm[:, :]),
                    deps=op_ev, name="dbg_y")
            y2_reuse[gp % 2] = [op_ev[-1]]
            yfm_reuse[:] = [tok_dma]
            ysc_dmas.append(tok_dma)
            prev_fin_ap[:] = ap_toks
            return [tok_dma] + op_ev

        for g in range(NGRP_EFF):
            gpar = g % 2
            bloc = g // GPB
            t0 = (g % GPB) * GRP
            first_in_batch = (t0 == 0)

            tok_xnT = []
            for tt in range(2):
                t_x = sc.dma("G", lambda h, bloc=bloc, t0=t0, tt=tt: h.dma_start(
                    out=xtok[tt][:, :],
                    in_=XI[bloc, t0 + tt * 128:t0 + (tt + 1) * 128, :]),
                    deps=(weights_ready if g == 0 else prev_xtok_readers[tt] + [prev_xtok_writer[tt]]),
                    name=f"x{g}_{tt}")
                prev_xtok_writer[tt] = t_x
                tok_sq = sc.s(lambda h, tt=tt: h.activation(
                    out=sq_dump[:, :], in_=xtok[tt][:, :],
                    func=AF.Square, accum_out=ssq[:, tt:tt + 1]),
                    deps=[t_x], name=f"ssq{g}_{tt}")
                tok_m = sc.v(lambda h, tt=tt: h.tensor_scalar(
                    out=rs_col[:, tt:tt + 1], in0=ssq[:, tt:tt + 1],
                    scalar1=1.0 / D, scalar2=EPS, op0=ALU.mult, op1=ALU.add),
                    deps=[tok_sq], name=f"m{g}_{tt}")
                tok_ln = sc.s(lambda h, tt=tt: h.activation(
                    out=rs_col[:, tt:tt + 1], in_=rs_col[:, tt:tt + 1], func=AF.Ln),
                    deps=[tok_m], name=f"ln{g}_{tt}")
                tok_rs = sc.s(lambda h, tt=tt: h.activation(
                    out=rs_col[:, tt:tt + 1], in_=rs_col[:, tt:tt + 1], func=AF.Exp, scale=-0.5),
                    deps=[tok_ln], name=f"rs{g}_{tt}")
                tok_diag = sc.v(lambda h, tt=tt: h.tensor_scalar(
                    out=diag_s[:, :], in0=eye_bf[:, :], scalar1=rs_col[:, tt:tt + 1],
                    scalar2=None, op0=ALU.mult),
                    deps=[tok_rs] + ([t for t in tok_xnT[-8:]] if tt == 1 else prev_xnT_readers),
                    name=f"diag{g}_{tt}")
                readers = []
                for k in range(8):
                    si = k % 2
                    tok_mm = sc.t(lambda h, tt=tt, k=k, si=si: h.matmul(
                        p_inproj[si][:, 0:128], xtok[tt][:, k * 128:(k + 1) * 128],
                        diag_s[:, :], start=True, stop=True),
                        deps=chn(f"ip{si}", [tok_diag]), name=f"xnTmm{g}_{tt}_{k}")
                    chains[f"ip{si}"] = tok_mm
                    tok_ev = sc.s(lambda h, tt=tt, k=k, si=si: h.copy(
                        out=xnT[:, k * GRP + tt * 128:k * GRP + tt * 128 + 128],
                        in_=p_inproj[si][:, 0:128]), deps=[tok_mm], name=f"xnTev{g}_{tt}_{k}")
                    chains[f"ip{si}"] = tok_ev
                    readers.append(tok_mm)
                    tok_xnT.append(tok_ev)
                prev_xtok_readers[tt] = readers
            prev_xnT_readers = []
            if d_xnT is not None and g == 0:
                sc.dma("G", lambda h: h.dma_start(
                    out=d_xnT.rearrange("(k p) t -> p k t", k=8), in_=xnT[:, :]),
                    deps=tok_xnT, name="dbg_xnT")

            mlist = [34] + list(range(34))
            evac_toks = {}
            for mi, m in enumerate(mlist):
                si = mi % 2
                mw = 32 if m == 34 else 128

                def mk_ip(h, m=m, si=si, mw=mw):
                    last = None
                    for k in range(8):
                        last = h.matmul(p_inproj[si][0:mw, :],
                                        wip[:, k * D_IN_PROJ + m * 128:k * D_IN_PROJ + m * 128 + mw],
                                        xnT[:, k * GRP:(k + 1) * GRP],
                                        start=(k == 0), stop=(k == 7))
                    return last
                tok_mm = sc.t(mk_ip, deps=chn(f"ip{si}", tok_xnT), name=f"ip{g}_{m}")
                chains[f"ip{si}"] = tok_mm
                prev_xnT_readers.append(tok_mm)
                if m == 34:
                    tok_a1 = sc.v(lambda h, si=si: h.tensor_scalar(
                        out=dtg[:, :], in0=p_inproj[si][0:32, :], scalar1=dtb_col[:, :],
                        scalar2=None, op0=ALU.add), deps=chn(f"ip{si}", [tok_mm] + prev_dt_readers),
                        name=f"dta1_{g}")
                    chains[f"ip{si}"] = tok_a1
                    tok_aa = sc.s(lambda h: h.activation(
                        out=dt_abs[:, :], in_=dtg[:, :], func=AF.Abs),
                        deps=[tok_a1], name=f"dtab_{g}")
                    tok_e = sc.s(lambda h: h.activation(out=dt_abs[:, :], in_=dt_abs[:, :],
                                                        func=AF.Exp, scale=-1.0),
                                 deps=[tok_aa], name=f"dte_{g}")
                    tok_l = sc.s(lambda h: h.activation(out=dt_abs[:, :], in_=dt_abs[:, :],
                                                        func=AF.Ln, bias=1.0),
                                 deps=[tok_e], name=f"dtl_{g}")
                    tok_r = sc.s(lambda h: h.activation(
                        out=dtg[:, :], in_=dtg[:, :], func=AF.Relu),
                        deps=[tok_l], name=f"dtr_{g}")
                    tok_dt = sc.v(lambda h: h.tensor_add(out=dtg[:, :], in0=dtg[:, :], in1=dt_abs[:, :]),
                                  deps=[tok_r], name=f"dt_{g}")
                    evac_toks["dt"] = tok_dt
                    if d_dt is not None and g == 0:
                        evac_toks["dbg_dt"] = sc.dma("SYNC", lambda h: h.dma_start(
                            out=d_dt, in_=dtg[:, :]), deps=[tok_dt], name="dbg_dt")
                    if g > 0:
                        finish_group(g - 1)
                elif m < 16:
                    tok_t = sc.s(lambda h, si=si: h.activation(
                        out=zt_tmp[:, :], in_=p_inproj[si][:, :], func=AF.Tanh, scale=0.5),
                        deps=chn(f"ip{si}", [tok_mm, prev_zt]), name=f"zt{g}_{m}")
                    tok_z = sc.v(lambda h, m=m, gpar=gpar, si=si: h.scalar_tensor_tensor(
                        out=zs2[gpar][:, m * GRP:(m + 1) * GRP], in0=zt_tmp[:, :], scalar=1.0,
                        in1=p_inproj[si][:, :], op0=ALU.add, op1=ALU.mult),
                        deps=[tok_t] + (prev_zs_group.get(gpar, []) if m == 0 else []),
                        name=f"z{g}_{m}")
                    chains[f"ip{si}"] = tok_z
                    prev_zt = tok_z
                    evac_toks[m] = tok_z
                else:
                    ct = m - 16
                    tok_ev = sc.s(lambda h, ct=ct, gpar=gpar, si=si: h.copy(
                        out=xbcu[gpar][:, ct * (GRP + 3) + 3:ct * (GRP + 3) + 3 + GRP],
                        in_=p_inproj[si][:, :]),
                        deps=chn(f"ip{si}", [tok_mm] + (list(prev_xs_group.get(gpar, ())) if ct == 0 else [])),
                        name=f"xbc{g}_{ct}")
                    chains[f"ip{si}"] = tok_ev
                    evac_toks[m] = tok_ev

            xbc_toks = [evac_toks[m] for m in range(16, 34)]

            if first_in_batch:
                def mk_halo(h, gpar=gpar):
                    last = None
                    for ct in range(18):
                        last = h.memset(xbcu[gpar][:, ct * (GRP + 3):ct * (GRP + 3) + 3], 0.0)
                    return last
                tok_halo = sc.v(mk_halo, deps=list(prev_xs_group.get(gpar, ())), name=f"halo0_{g}")
            else:
                pbi = 1 - gpar

                def mk_halo(h, gpar=gpar, pbi=pbi):
                    last = None
                    for ct in range(18):
                        last = h.tensor_copy(
                            out=xbcu[gpar][:, ct * (GRP + 3):ct * (GRP + 3) + 3],
                            in_=hsave[pbi][:, ct * 3:ct * 3 + 3])
                    return last
                tok_halo = sc.v(mk_halo, deps=list(prev_xs_group.get(gpar, ())) + xbc_toks,
                                name=f"halo_{g}")

            def mk_hsave(h, gpar=gpar):
                last = None
                for ct in range(18):
                    last = h.tensor_copy(
                        out=hsave[gpar][:, ct * 3:ct * 3 + 3],
                        in_=xbcu[gpar][:, ct * (GRP + 3) + GRP:ct * (GRP + 3) + GRP + 3])
                return last
            tok_hsave = sc.v(mk_hsave, deps=xbc_toks, name=f"hsave_{g}")

            tok_xs_all = []
            for ct in range(18):
                si = ct % 2
                base = ct * (GRP + 3)
                tok_ck = sc.v(lambda h, ct=ct, gpar=gpar, si=si, base=base: h.tensor_scalar(
                    out=convacc[si][:, :], in0=xbcu[gpar][:, base + 0:base + 0 + GRP],
                    scalar1=convw[:, ct * 4:ct * 4 + 1], scalar2=convb[:, ct:ct + 1],
                    op0=ALU.mult, op1=ALU.add),
                    deps=[evac_toks[16 + ct], tok_halo, tok_hsave, cva_slot_last[si]],
                    name=f"cv0_{g}_{ct}")
                for k in range(1, 4):
                    tok_ck = sc.v(lambda h, ct=ct, gpar=gpar, si=si, base=base, k=k: h.scalar_tensor_tensor(
                        out=convacc[si][:, :], in0=xbcu[gpar][:, base + k:base + k + GRP],
                        scalar=convw[:, ct * 4 + k:ct * 4 + k + 1], in1=convacc[si][:, :],
                        op0=ALU.mult, op1=ALU.add), deps=[tok_ck], name=f"cv{k}_{g}_{ct}")
                tok_ct2 = sc.s(lambda h, si=si: h.activation(
                    out=convt_tmp[:, :], in_=convacc[si][:, :], func=AF.Tanh, scale=0.5),
                    deps=[tok_ck, prev_ct], name=f"cvt{g}_{ct}")
                tok_xs = sc.v(lambda h, ct=ct, gpar=gpar, si=si, base=base: h.scalar_tensor_tensor(
                    out=xbcu[gpar][:, base + 3:base + 3 + GRP],
                    in0=convt_tmp[:, :], scalar=1.0, in1=convacc[si][:, :],
                    op0=ALU.add, op1=ALU.mult),
                    deps=[tok_ct2], name=f"cxs{g}_{ct}")
                cva_slot_last[si] = tok_xs
                prev_ct = tok_xs
                tok_xs_all.append(tok_xs)

            xs_group_readers = []
            zs_group_readers = []
            if d_zs2 is not None and g == 0:
                zs_group_readers.append(sc.dma("G", lambda h: h.dma_start(
                    out=d_zs2.rearrange("(ct p) t -> p ct t", ct=16), in_=zs2[0][:, :]),
                    deps=[evac_toks[m] for m in range(16)], name="dbg_zs2"))
            if d_xs is not None and g == 0:
                def mk_dbg_xs(h, sem):
                    for ct in range(18):
                        h.dma_start(out=d_xs[ct * 128:(ct + 1) * 128, :],
                                    in_=xbcu[0][:, ct * (GRP + 3) + 3:ct * (GRP + 3) + 3 + GRP]
                                    ).then_inc(sem, 16)
                xs_group_readers.append(sc.dma_multi("G", mk_dbg_xs, 18, deps=tok_xs_all, name="dbg_xs"))

            dt_readers = []
            y2_toks = []
            sumsq_tok = None
            if first_in_batch:
                tok_pz = sc.v(lambda h: h.memset(prevst[:, :], 0.0),
                              deps=[t for t in prev_aug if t is not None] +
                                   [t for t in prev_upd if t is not None], name=f"pz{g}")
                prev_upd = [tok_pz] * NHEADS
                prev_aug = [None] * NHEADS

            for c in range(CHG):
                cbase = c * LCH
                shared_deps = list(prev_ssd_shared)
                tok_da = sc.v(lambda h, cbase=cbase: h.tensor_scalar(
                    out=dAc[:, :], in0=dtg[:, cbase:cbase + LCH], scalar1=negA_col[:, :],
                    scalar2=None, op0=ALU.mult), deps=[evac_toks["dt"]] + shared_deps,
                    name=f"da{g}_{c}")
                dt_readers.append(tok_da)
                tok_cs = sc.v(lambda h: h.tensor_tensor_scan(
                    out=csc[:, :], data0=dAc[:, :], data1=dAc[:, :], initial=0.0,
                    op0=ALU.add, op1=ALU.bypass), deps=[tok_da], name=f"cs{g}_{c}")
                if d_cs is not None and g == 0 and c == 0:
                    sc.dma("SYNC", lambda h: h.dma_start(out=d_cs, in_=csc[:, :]),
                           deps=[tok_cs], name="dbg_cs")
                tok_h1 = sc.v(lambda h: h.tensor_copy(out=cshl[0:32, :], in_=csc[:, :]),
                              deps=[tok_cs], name=f"hl1{g}_{c}")
                tok_h2 = sc.v(lambda h: h.tensor_copy(out=cs32f[:, :], in_=cshl[0:32, :]),
                              deps=[tok_h1], name=f"hl2{g}_{c}")
                tok_h3 = sc.v(lambda h: h.tensor_sub(out=cshl[32:64, :], in0=csc[:, :],
                                                     in1=cs32f[:, :]), deps=[tok_h2], name=f"hl3{g}_{c}")
                tok_n1 = sc.v(lambda h: h.tensor_scalar(
                    out=cshlneg[:, :], in0=cshl[:, :], scalar1=-1.0, scalar2=None, op0=ALU.mult),
                    deps=[tok_h3], name=f"neg{g}_{c}")

                tok_dtt_m = sc.t(lambda h, cbase=cbase: h.transpose(
                    p_ftr[0][:, :], dtg[0:32, cbase:cbase + LCH], eye_f[0:32, 0:32]),
                    deps=chn("b6", [evac_toks["dt"]]), name=f"dttm{g}_{c}")
                chains["b6"] = tok_dtt_m
                dt_readers.append(tok_dtt_m)
                tok_dtt = sc.s(lambda h: h.activation(out=dtT[:, :], in_=p_ftr[0][:, :],
                                                      func=AF.Copy, scale=0.5),
                               deps=chn("b6", [tok_dtt_m] + shared_deps), name=f"dtt{g}_{c}")
                chains["b6"] = tok_dtt
                tok_d1 = sc.v(lambda h: h.tensor_scalar(
                    out=decc[:, :], in0=csc[:, :], scalar1=csc[:, LCH - 1:LCH], scalar2=None,
                    op0=ALU.subtract), deps=[tok_cs], name=f"dc1{g}_{c}")
                tok_d2 = sc.s(lambda h: h.activation(out=decc[:, :], in_=decc[:, :],
                                                     func=AF.Exp, scale=-1.0, bias=ln05_col[0:32, :]),
                              deps=[tok_d1], name=f"dc2{g}_{c}")
                tok_d3 = sc.t(lambda h: h.transpose(p_ftr[1][:, :], decc[:, :], eye_f[0:32, 0:32]),
                              deps=chn("b6", [tok_d2]), name=f"dc3{g}_{c}")
                chains["b6"] = tok_d3
                tok_d4 = sc.s(lambda h: h.copy(out=decT[:, :], in_=p_ftr[1][:, :]),
                              deps=chn("b6", [tok_d3] + shared_deps), name=f"dc4{g}_{c}")
                chains["b6"] = tok_d4
                tok_lr = sc.dma("SYNC", lambda h: h.dma_start(
                    out=lamrow[:, :], in_=csc[:, LCH - 1:LCH]),
                    deps=[tok_cs] + lam_last, name=f"lr{g}_{c}")
                tok_le = sc.s(lambda h: h.activation(out=lamrow[:, :], in_=lamrow[:, :],
                                                     func=AF.Exp), deps=[tok_lr], name=f"le{g}_{c}")
                lam_last = [tok_lr, tok_le, tok_lb_m] if False else [tok_lr, tok_le]
                tok_lb_m = sc.t(lambda h: h.matmul(p_lam[:, :], ones1_f[:, :], lamrow[:, :],
                                                   start=True, stop=True),
                                deps=chn("b6", [tok_le]), name=f"lbm{g}_{c}")
                chains["b6"] = tok_lb_m
                tok_lb = sc.s(lambda h: h.copy(out=lamb[:, :], in_=p_lam[:, :]),
                              deps=chn("b6", [tok_lb_m] + shared_deps), name=f"lb{g}_{c}")
                chains["b6"] = tok_lb

                xdt_toks = []
                for ct in range(16):
                    xin = ct * (GRP + 3) + 3 + cbase
                    tok_tm = sc.t(lambda h, gpar=gpar, xin=xin: h.transpose(
                        p_trb[:, :], xbcu[gpar][:, xin:xin + LCH], eye_bf[:, :]),
                        deps=chn("b5", [tok_xs_all[ct]]), name=f"xdm{g}_{c}_{ct}")
                    chains["b5"] = tok_tm
                    xs_group_readers.append(tok_tm)

                    def mk_xde(h, ct=ct):
                        h.activation(out=xdT[:, ct * 128:ct * 128 + 64], in_=p_trb[:, 0:64],
                                     func=AF.Copy, scale=dtT[:, 2 * ct:2 * ct + 1])
                        return h.activation(out=xdT[:, ct * 128 + 64:ct * 128 + 128],
                                            in_=p_trb[:, 64:128],
                                            func=AF.Copy, scale=dtT[:, 2 * ct + 1:2 * ct + 2])
                    tok_te = sc.s(mk_xde, deps=chn("b5", [tok_tm, tok_dtt] + shared_deps),
                                  name=f"xde{g}_{c}_{ct}")
                    chains["b5"] = tok_te
                    xdt_toks.append(tok_te)
                bu_off = 16 * (GRP + 3) + 3 + cbase
                cu_off = 17 * (GRP + 3) + 3 + cbase
                tok_bm = sc.t(lambda h, gpar=gpar, bu_off=bu_off: h.transpose(
                    p_trb[:, :], xbcu[gpar][:, bu_off:bu_off + LCH], eye_bf[:, :]),
                    deps=chn("b5", [tok_xs_all[16]]), name=f"bum{g}_{c}")
                chains["b5"] = tok_bm
                xs_group_readers.append(tok_bm)
                tok_bu = sc.s(lambda h: h.copy(out=BuT[:, :], in_=p_trb[:, :]),
                              deps=chn("b5", [tok_bm] + shared_deps), name=f"bue{g}_{c}")
                chains["b5"] = tok_bu
                tok_gm = sc.t(lambda h, gpar=gpar, bu_off=bu_off, cu_off=cu_off: h.matmul(
                    p_gt[:, :], xbcu[gpar][:, bu_off:bu_off + LCH],
                    xbcu[gpar][:, cu_off:cu_off + LCH], start=True, stop=True),
                    deps=chn("b6", [tok_xs_all[16], tok_xs_all[17]]), name=f"gtm{g}_{c}")
                chains["b6"] = tok_gm
                xs_group_readers.append(tok_gm)
                tok_gt = sc.s(lambda h: h.activation(out=GT[:, :], in_=p_gt[:, :],
                                                     func=AF.Copy, scale=0.25),
                              deps=chn("b6", [tok_gm] + shared_deps), name=f"gte{g}_{c}")
                chains["b6"] = tok_gt

                new_shared = [tok_dtt, tok_d4, tok_lb]
                y_pair_mm = {}
                for bb in range(8):
                    csl = bb % 2
                    sl = bb % 2
                    Tk = f"T{sl}"
                    dep_m_reuse = [last_maug, last_mtok]

                    def mk_csi(h, sem, bb=bb, csl=csl):
                        h.dma_start(out=csi2[csl][0:1, :],
                                    in_=cshl[bb * 4:bb * 4 + 4, :]).then_inc(sem, 16)
                        h.dma_start(out=csi2[csl][1:2, :],
                                    in_=cshl[32 + bb * 4:32 + bb * 4 + 4, :]).then_inc(sem, 16)
                    tok_csi = sc.dma_multi("SYNC", mk_csi, 2,
                                           deps=[tok_h3] + csi2_slot_last[csl],
                                           name=f"csi{g}_{c}_{bb}")
                    csi2_slot_last[csl] = [tok_csi]

                    tok_rm = sc.t(lambda h, sl=sl, csl=csl: h.matmul(
                        p_T[sl][:, :], ones2_bf[:, :], csi2[csl][:, :], start=True, stop=True),
                        deps=chn(Tk, [tok_csi]), name=f"rm_{g}_{c}_{bb}")
                    chains[Tk] = tok_rm
                    tok_e05 = sc.s(lambda h, sl=sl: h.activation(
                        out=exp05[:, :], in_=p_T[sl][:, :], func=AF.Exp, bias=ln05_col[:, :]),
                        deps=chn(Tk, [tok_rm, last_maug]), name=f"e05_{g}_{c}_{bb}")
                    chains[Tk] = tok_e05
                    tok_maug = sc.v(lambda h, gpar=gpar, cu_off=cu_off: h.tensor_tensor(
                        out=m_aug[:, :], in0=exp05[:, :],
                        in1=xbcu[gpar][:, cu_off:cu_off + LCH]
                        .rearrange("p (o f) -> p o f", o=1).to_broadcast([128, 4, LCH]),
                        op=ALU.mult), deps=[tok_e05] + dep_m_reuse, name=f"maug_{g}_{c}_{bb}")
                    xs_group_readers.append(tok_maug)
                    last_maug = tok_maug

                    def mk_T(h, sl=sl, csl=csl, bb=bb):
                        h.matmul(p_T[sl][:, :], ones2_bf[:, :], csi2[csl][:, :],
                                 start=True, stop=False)
                        h.matmul(p_T[sl][:, :], cshlneg[:, :],
                                 mask64[:, bb * 512:(bb + 1) * 512], start=False, stop=False)
                        last = None
                        for v in range(4):
                            last = h.matmul(p_T[sl][:, v * 128:(v + 1) * 128], eye_bf[:, :],
                                            maskc_bf[:, :], start=False, stop=(v == 3))
                        return last
                    tok_mmT = sc.t(mk_T, deps=chn(Tk, [tok_e05, tok_n1, tok_csi]),
                                   name=f"mmT_{g}_{c}_{bb}")
                    chains[Tk] = tok_mmT
                    csi2_slot_last[csl] = csi2_slot_last[csl] + [tok_mmT]
                    tok_L = sc.s(lambda h, sl=sl: h.activation(
                        out=L_sb[:, :], in_=p_T[sl][:, :], func=AF.Exp),
                        deps=chn(Tk, [tok_mmT, last_mtok]), name=f"L_{g}_{c}_{bb}")
                    chains[Tk] = tok_L
                    tok_mtok = sc.v(lambda h: h.tensor_tensor(
                        out=m_tok[:, :], in0=L_sb[:, :],
                        in1=GT[:, :].rearrange("p (o f) -> p o f", o=1).to_broadcast([128, 4, LCH]),
                        op=ALU.mult), deps=[tok_L, tok_gt] + dep_m_reuse, name=f"mtok_{g}_{c}_{bb}")
                    new_shared.append(tok_mtok)
                    last_mtok = tok_mtok

                    for v in range(4):
                        hh = bb * 4 + v
                        pi = (hh // 2) % 2
                        bk = "b4" if pi == 0 else "b7"
                        off = 64 * (hh % 2)
                        tp = (0, 64) if off else None

                        def mk_yd(h, hh=hh, v=v, pi=pi, off=off, tp=tp):
                            h.matmul(p_y[pi][off:off + 64, :], xdT[:, hh * 64:(hh + 1) * 64],
                                     m_tok[:, v * 128:(v + 1) * 128], start=True, stop=False,
                                     tile_position=tp)
                            return h.matmul(p_y[pi][off:off + 64, :], prevst[:, hh * 64:(hh + 1) * 64],
                                            m_aug[:, v * 128:(v + 1) * 128], start=False, stop=True,
                                            tile_position=tp)
                        tok_yd = sc.t(mk_yd, deps=chn(bk, [tok_mtok, tok_maug, xdt_toks[hh // 2],
                                                           prev_upd[hh]]),
                                      name=f"yd{g}_{c}_{hh}")
                        chains[bk] = tok_yd
                        prev_aug[hh] = tok_yd
                        new_shared.append(tok_yd)
                        s2 = pi
                        tok_xd = sc.v(lambda h, hh=hh, s2=s2: h.tensor_scalar(
                            out=xdec[s2][:, :], in0=xdT[:, hh * 64:(hh + 1) * 64],
                            scalar1=decT[:, hh:hh + 1], scalar2=None, op0=ALU.mult),
                            deps=[xdt_toks[hh // 2], tok_d4, xdec_slot_last[s2]],
                            name=f"xdc{g}_{c}_{hh}")
                        new_shared.append(tok_xd)
                        tok_st = sc.t(lambda h, s2=s2: h.matmul(
                            p_st[s2][:, :], BuT[:, :], xdec[s2][:, :], start=True, stop=True),
                            deps=chn(bk, [tok_xd, tok_bu]), name=f"st{g}_{c}_{hh}")
                        chains[bk] = tok_st
                        xdec_slot_last[s2] = tok_st
                        new_shared.append(tok_st)
                        tok_up = sc.v(lambda h, hh=hh, s2=s2: h.scalar_tensor_tensor(
                            out=prevst[:, hh * 64:(hh + 1) * 64], in0=prevst[:, hh * 64:(hh + 1) * 64],
                            scalar=lamb[:, hh:hh + 1], in1=p_st[s2][:, :],
                            op0=ALU.mult, op1=ALU.add),
                            deps=chn(bk, [tok_st, tok_yd, tok_lb]), name=f"up{g}_{c}_{hh}")
                        chains[bk] = tok_up
                        prev_upd[hh] = tok_up
                        new_shared.append(tok_up)

                        if off == 64:
                            ct2 = hh // 2
                            xin = ct2 * (GRP + 3) + 3 + cbase
                            tok_t1 = sc.v(lambda h, ct2=ct2, gpar=gpar, xin=xin, pi=pi: h.scalar_tensor_tensor(
                                out=t1_sb[:, :], in0=xbcu[gpar][:, xin:xin + LCH],
                                scalar=d128[:, ct2:ct2 + 1], in1=p_y[pi][:, :],
                                op0=ALU.mult, op1=ALU.add),
                                deps=chn(bk, [tok_yd, y_pair_mm.get(hh - 1)]), name=f"t1_{g}_{c}_{ct2}")
                            chains[bk] = tok_t1
                            xs_group_readers.append(tok_t1)
                            tok_y2 = sc.v(lambda h, ct2=ct2, gpar=gpar, cbase=cbase: h.tensor_tensor(
                                out=y2[gpar][:, ct2 * GRP + cbase:ct2 * GRP + cbase + LCH],
                                in0=t1_sb[:, :],
                                in1=zs2[gpar][:, ct2 * GRP + cbase:ct2 * GRP + cbase + LCH],
                                op=ALU.mult), deps=[tok_t1] + y2_reuse.get(gpar, []),
                                name=f"y2_{g}_{c}_{ct2}")
                            zs_group_readers.append(tok_y2)
                            y2_toks.append(tok_y2)
                        else:
                            y_pair_mm[hh] = tok_yd
                prev_ssd_shared = new_shared

            for ct in range(16):
                tok_sq = sc.v(lambda h, ct=ct, gpar=gpar: h.tensor_tensor(
                    out=y2sq[:, :], in0=y2[gpar][:, ct * GRP:(ct + 1) * GRP],
                    in1=y2[gpar][:, ct * GRP:(ct + 1) * GRP], op=ALU.mult),
                    deps=([sumsq_tok] if ct > 0 else y2_toks), name=f"ysq{g}_{ct}")
                sumsq_tok = sc.t(lambda h, ct=ct: h.matmul(
                    p_sb[0:1, :], onescol_bf[:, :], y2sq[:, :],
                    start=(ct == 0), stop=(ct == 15)),
                    deps=chn("b6", [tok_sq]), name=f"ymm{g}_{ct}")
                chains["b6"] = sumsq_tok
            y2_group[g] = (sumsq_tok, y2_toks)

            if d_y2 is not None and g == 0:
                y2_toks.append(sc.dma("G", lambda h: h.dma_start(
                    out=d_y2.rearrange("(ct p) t -> p ct t", ct=16), in_=y2[0][:, :]),
                    deps=list(y2_toks), name="dbg_y2"))

            prev_xs_group[gpar] = tuple(xs_group_readers)
            prev_zs_group[gpar] = zs_group_readers
            prev_dt_readers = dt_readers + ([evac_toks["dbg_dt"]] if "dbg_dt" in evac_toks else [])
            ga_done.append(tok_xs_all[-1])

        fin_last = finish_group(NGRP_EFF - 1)

        RB = nc.lookup_mloc(wip).addr

        def at(name, shp, dt, off):
            return nc.alloc_sbuf_tensor_at(name, shp, dt, offset=RB + off)

        def mk_carver(base):
            off = [base]
            def carve(name, shp, dt):
                t = at(name, shp, dt, off[0])
                esz = 2 if dt == BF16 else 4
                off[0] += (shp[1] * esz + 31) // 32 * 32
                return t
            return carve, off

        carveA, oa = mk_carver(0)
        wsc1 = carveA("wsc1", [128, 8 * 256], BF16)
        wsumm = carveA("wsumm", [128, 8 * 64], BF16)
        wkv = carveA("wkv", [64, 64 + 1024], BF16)
        wq = carveA("wq", [128, 8 * 64], BF16)
        scst = carveA("scst", [32, 256], F32)
        scrow = carveA("scrow", [1, 256], F32)
        summ_sb = carveA("summ_sb", [128, 32 * 64], BF16)
        ybuf = carveA("ybuf", [128, 8 * 256], BF16)
        xbuf = carveA("xbuf", [128, 2 * 1024], F32)
        h1b = carveA("h1b", [128, 2 * 256], BF16)
        a3b = carveA("a3b", [128, 8 * 256], BF16)
        q_bf = carveA("q_bf", [64, 256], BF16)
        pT_bf = carveA("pT_bf", [64, 256], BF16)
        prow = carveA("prow", [128, 64], BF16)
        lgb = carveA("lgb", [128, 64], F32)
        gate_bf = carveA("gate_bf", [128, 256], BF16)
        t2b = carveA("t2b", [128, 256], F32)
        dump_b = carveA("dump_b", [128, 1024], BF16)
        sglob = carveA("sglob", [128, 256], F32)
        svm = carveA("svm", [128, 256], F32)
        cascA = carveA("cascA", [128, 896], F32)
        cascB = carveA("cascB", [128, 896], F32)
        cmask = carveA("cmask", [128, 896], F32)
        cands = carveA("cands", [128, 56], F32)
        vvals = carveA("vvals", [1, 56], F32)
        vb50 = carveA("vb50", [128, 50], F32)
        scloc = carveA("scloc", [128, 32], F32)
        sccols = carveA("sccols", [128, 12], F32)
        onescol_f = carveA("onescol_f", [128, 1], F32)
        cond05 = carveA("cond05", [128, 1], F32)
        pool_f = carveA("pool_f", [64, 64], F32)
        poolb = carveA("poolb", [64, 64], BF16)
        poolT_bf = carveA("poolT_bf", [64, 64], BF16)
        kk_bf = carveA("kk_bf", [64, 64], BF16)
        vv_bf = carveA("vv_bf", [64, 1024], BF16)
        vvc_bf = carveA("vvc_bf", [64, 1024], BF16)
        VG_bf = carveA("VG_bf", [64, 1024], BF16)
        vvT = carveA("vvT", [128, 8 * 64], BF16)
        maskb = carveA("maskb", [128, 64], F32)
        Sx = carveA("Sx", [128, 50], BF16)
        assert oa[0] <= 65536, f"regA overflow {oa[0]}"

        x1fm = at("x1fm", [128, 8 * 4096], BF16, 0)
        wg1 = at("wg1", [128, 8 * 1024], BF16, 65536)
        wg2 = at("wg2", [128, 8 * 1024], BF16, 81920)
        wslot = []
        for i in range(3):
            wb = 65536 + i * 24576
            wslot.append({"wg": at(f"cwg{i}", [128, 8 * 512], BF16, wb),
                          "wu": at(f"cwu{i}", [128, 8 * 512], BF16, wb + 8192),
                          "wd": at(f"cwd{i}", [128, 4 * 1024], BF16, wb + 16384)})

        carveP, op_ = mk_carver(139264)
        mucol = carveP("mucol", [128, 32], F32)
        sqcol = carveP("sqcol", [128, 32], F32)
        sscol = carveP("sscol", [128, 32], F32)
        diag2 = carveP("diag2", [128, 128], BF16)
        v2l4 = carveP("v2l4", [128, 4 * 1024], BF16)
        gfm = at("gfm", [16, 32 * 128], BF16, 139904)
        selec = carveP("selec", [16, 16 * 128], BF16)
        gtok = carveP("gtok", [128, 32 * 16], BF16)
        wrt = carveP("wrt", [128, 8 * 16], BF16)
        thb = carveP("thb", [16, 128], BF16)
        gb_col = carveP("gb_col", [16, 1], F32)
        rcols = carveP("rcols", [128, 8], F32)
        cga = carveP("cga", [128, 16], BF16)
        ct2 = carveP("ct2", [128, 16], F32)
        clg = carveP("clg", [128, 16], F32)
        gbc = carveP("gbc", [128, 2 * 256], BF16)
        hsil = carveP("hsil", [128, 2 * 4 * 256], BF16)
        hcmb = carveP("hcmb", [128, 2 * 4 * 256], BF16)
        v3tok2 = at("v3tok2", [128, 2 * 1024], BF16, op_[0] - 8192)
        v3fm = at("v3fm", [128, 8 * 256], BF16, op_[0] - 4096)
        accbuf = [carveP(f"accbuf{i}", [128, 8 * 256], BF16) for i in range(2)]
        outst2 = [carveP(f"outst{i}", [128, 1024], F32) for i in range(2)]
        v2loc = [carveP(f"v2loc{i}", [128, 2 * 1024], BF16) for i in range(2)]
        fmu = carveP("fmu", [128, 32], F32)
        fsq = carveP("fsq", [128, 32], F32)
        fs2 = carveP("fs2", [128, 32], F32)
        assert RB + op_[0] <= 222592, f"regP overflow {RB + op_[0]}"

        V2H = nc.dram_tensor("v2_scratch", [128, 32 * 1024], BF16)
        V3H = nc.dram_tensor("v3_scratch", [16, 128, 2048], BF16)
        ACCH = nc.dram_tensor("acc_scratch", [16, 128, 2048], BF16)
        sc_in = nc.dram_tensor("sc_in", [1, NTOK], F32)
        sc_out = nc.dram_tensor("sc_out", [NCORES, NTOK], F32)
        pool_in = nc.dram_tensor("pool_in", [64, 64], F32)
        pool_out = nc.dram_tensor("pool_out", [64, 64], F32)

        a_end = list(fin_last) + [t for t in prev_upd if t is not None] + \
            list(prev_ssd_shared) + list(prev_xs_group.get(0, ())) + list(prev_xs_group.get(1, ())) + \
            (prev_zs_group.get(0) or []) + (prev_zs_group.get(1) or []) + \
            [chains[k] for k in chains if chains[k] is not None] + \
            list(prev_fin_ap) + list(prev_dt_readers)

        d_sc = dbg_tensor("sc", [32, 256])
        d_pool = dbg_tensor("pool", [64, 64])
        d_x1 = dbg_tensor("x1", [8 * 128, 256])
        d_gw = dbg_tensor("gw", [128, 512])
        d_acc0 = dbg_tensor("acc0", [128, 2048])

        a2w = []
        for k in range(8):
            for (dst, srcw, width) in ((wsc1, "scorer_w1", 256), (wsumm, "summ_w", 64),
                                       (wq, "q_w", 64)):
                a2w.append(sc.dma("G", lambda h, k=k, dst=dst, srcw=srcw, width=width:
                                  h.dma_start(out=dst[:, k * width:(k + 1) * width],
                                              in_=W[srcw][k * 128:(k + 1) * 128, :]),
                                  deps=a_end, name=f"a2w_{srcw}{k}"))
            a2w.append(sc.dma("G", lambda h, k=k: h.dma_start(
                out=wg1[:, k * 1024:(k + 1) * 1024],
                in_=W["gate_w"][k * 128:(k + 1) * 128, :]), deps=a_end, name=f"wg1{k}"))
            a2w.append(sc.dma("G", lambda h, k=k: h.dma_start(
                out=wg2[:, k * 1024:(k + 1) * 1024],
                in_=W["gate_w"][1024 + k * 128:1024 + (k + 1) * 128, :]),
                deps=a_end, name=f"wg2{k}"))
        a2w.append(sc.dma("G", lambda h: h.dma_start(out=wkv[:, 0:64], in_=W["k_w"]),
                          deps=a_end, name="wkw"))
        a2w.append(sc.dma("G", lambda h: h.dma_start(out=wkv[:, 64:1088], in_=W["v_w"]),
                          deps=a_end, name="wvw"))
        wsc2b = at("wsc2b", [128, 2], BF16, 65024)
        a2w.append(sc.dma("G", lambda h: h.dma_start(
            out=wsc2b[:, :], in_=W["scorer_w2"].rearrange("(k p) one -> p (k one)", k=2),
            allow_slow_non_contiguous=True),
            deps=a_end, name="wsc2"))

        sc_dma_toks = []
        scst_toks = []
        summ_toks = []
        ld_y_prev = None
        for s in range(16):
            gt0 = s * GRP
            t_y = sc.dma("SYNC", lambda h, gt0=gt0: h.dma_start(
                out=ybuf[:, :], in_=YSC.ap()[:, :, gt0:gt0 + GRP].rearrange("k p t -> p k t")),
                deps=a_end + ysc_dmas[s:s + 1] + ([ld_y_prev] if ld_y_prev else []), name=f"a2y{s}")
            h1toks = []
            for kt in range(2):
                si = kt

                def mk_h1(h, kt=kt, si=si):
                    last = None
                    for k in range(8):
                        last = h.matmul(p_inproj[si][:, :],
                                        wsc1[:, k * 256 + kt * 128:k * 256 + kt * 128 + 128],
                                        ybuf[:, k * GRP:(k + 1) * GRP],
                                        start=(k == 0), stop=(k == 7))
                    return last
                tok_mm = sc.t(mk_h1, deps=chn(f"ip{si}", [t_y] + a2w), name=f"h1m{s}_{kt}")
                chains[f"ip{si}"] = tok_mm
                tok_ev = sc.s(lambda h, kt=kt, si=si: h.activation(
                    out=h1b[:, kt * GRP:(kt + 1) * GRP], in_=p_inproj[si][:, :], func=AF.Relu),
                    deps=chn(f"ip{si}", [tok_mm]), name=f"h1e{s}_{kt}")
                chains[f"ip{si}"] = tok_ev
                h1toks.append(tok_ev)

            def mk_sc2(h):
                h.matmul(p_sb[0:1, :], wsc2b[:, 0:1], h1b[:, 0:GRP], start=True, stop=False)
                return h.matmul(p_sb[0:1, :], wsc2b[:, 1:2], h1b[:, GRP:2 * GRP],
                                start=False, stop=True)
            tok_scm = sc.t(mk_sc2, deps=chn("b6", h1toks + [a2w[-1]]), name=f"scm{s}")
            chains["b6"] = tok_scm
            tok_th = sc.s(lambda h: h.activation(
                out=scrow[:, :], in_=p_sb[0:1, :], func=AF.Tanh, scale=0.5),
                deps=chn("b6", [tok_scm] + scst_toks[-1:] + sc_dma_toks[-1:]),
                name=f"sct{s}")
            chains["b6"] = tok_th
            tok_sc = sc.v(lambda h: h.tensor_scalar(
                out=scrow[:, :], in0=scrow[:, :], scalar1=0.5, scalar2=0.5,
                op0=ALU.mult, op1=ALU.add), deps=[tok_th], name=f"scs{s}")
            scst_toks.append(sc.dma("SYNC", lambda h, s=s: h.dma_start(
                out=scst[s:s + 1, :], in_=scrow[:, :]), deps=[tok_sc], name=f"sc2{s}"))
            sc_dma_toks.append(sc.dma("SYNC", lambda h, s=s: h.dma_start(
                out=sc_in.ap()[0:1, s * GRP:(s + 1) * GRP], in_=scrow[:, :]),
                deps=[tok_sc], name=f"scd{s}"))
            for t2 in range(2):
                def mk_sm(h, t2=t2):
                    last = None
                    for k in range(8):
                        last = h.matmul(p_b4[:, 0:64],
                                        ybuf[:, k * GRP + t2 * 128:k * GRP + t2 * 128 + 128],
                                        wsumm[:, k * 64:(k + 1) * 64],
                                        start=(k == 0), stop=(k == 7))
                    return last
                tok_sm = sc.t(mk_sm, deps=chn("b4", [t_y] + a2w), name=f"smm{s}_{t2}")
                chains["b4"] = tok_sm
                tok_se = sc.s(lambda h, s=s, t2=t2: h.copy(
                    out=summ_sb[:, (2 * s + t2) * 64:(2 * s + t2 + 1) * 64], in_=p_b4[:, 0:64]),
                    deps=chn("b4", [tok_sm]), name=f"sme{s}_{t2}")
                chains["b4"] = tok_se
                summ_toks.append(tok_se)
            ld_y_prev = tok_sm

        if d_sc is not None:
            sc.dma("SYNC", lambda h: h.dma_start(out=d_sc, in_=scst[:, :]),
                   deps=scst_toks, name="dbg_sc")

        tok_ag = sc.cc(lambda h: h.collective_compute(
            "AllGather", ALU.bypass, replica_groups=[list(range(NCORES))],
            ins=[sc_in.ap().opt()], outs=[sc_out.ap().opt()]),
            deps=sc_dma_toks, name="ag")
        tok_sg = sc.dma("SYNC", lambda h: h.dma_start(
            out=sglob[:, :], in_=sc_out.ap().rearrange("r (p f) -> (r p) f", p=16)),
            deps=[tok_ag], name="sgl")

        scl_toks = []
        for half in range(2):
            tok_tm = sc.t(lambda h, half=half: h.transpose(
                p_b7[:, 0:16], scst[0:16, half * 128:(half + 1) * 128], eye_f[0:16, 0:16]),
                deps=chn("b7", scst_toks), name=f"sclt{half}")
            chains["b7"] = tok_tm
            tok_te = sc.s(lambda h, half=half: h.copy(
                out=scloc[:, half * 16:(half + 1) * 16], in_=p_b7[:, 0:16]),
                deps=chn("b7", [tok_tm]), name=f"scle{half}")
            chains["b7"] = tok_te
            scl_toks.append(tok_te)

        tok_vm = sc.v(lambda h: h.tensor_scalar(out=svm[:, :], in0=sglob[:, :],
                                                scalar1=0.5, scalar2=None, op0=ALU.is_gt),
                      deps=[tok_sg], name="vm")
        tok_w2 = sc.v(lambda h: h.tensor_mul(out=cascA[:, 0:256], in0=sglob[:, :],
                                             in1=svm[:, :]), deps=[tok_sg, tok_vm],
                      name="sw2")
        tok_of = sc.v(lambda h: h.memset(onescol_f[:, :], 1.0), deps=a_end, name="onef")
        tok_cnt1 = sc.v(lambda h: h.reduce_sum(out=sccols[:, 0:1], in_=svm[:, :], axis=AX.X),
                        deps=[tok_vm, tok_w2], name="cnt1")
        tok_sum1 = sc.v(lambda h: h.reduce_sum(out=sccols[:, 1:2], in_=sglob[:, :], axis=AX.X),
                        deps=[tok_sg, tok_cnt1], name="sum1")
        tok_cnt2 = sc.t(lambda h: h.matmul(p_b7[0:1, 0:2], onescol_f[:, :], sccols[:, 0:2],
                                           start=True, stop=True),
                        deps=chn("b7", [tok_cnt1, tok_sum1, tok_of]), name="cnt2")
        chains["b7"] = tok_cnt2
        tok_cs2 = sc.s(lambda h: h.copy(out=sccols[0:1, 2:4], in_=p_b7[0:1, 0:2]),
                       deps=chn("b7", [tok_cnt2]), name="cnts")
        chains["b7"] = tok_cs2
        tok_c1 = sc.v(lambda h: h.tensor_scalar(out=sccols[0:1, 4:5], in0=sccols[0:1, 3:4],
                                                scalar1=1.0 / 32768.0, scalar2=0.3,
                                                op0=ALU.mult, op1=ALU.is_gt),
                      deps=[tok_cs2], name="c1")
        tok_c3 = sc.v(lambda h: h.tensor_scalar(out=sccols[0:1, 5:6], in0=sccols[0:1, 2:3],
                                                scalar1=0.0, scalar2=None, op0=ALU.is_gt),
                      deps=[tok_cs2, tok_c1], name="c3")
        tok_c4 = sc.v(lambda h: h.tensor_mul(out=sccols[0:1, 4:5], in0=sccols[0:1, 4:5],
                                             in1=sccols[0:1, 5:6]), deps=[tok_c3], name="c4")
        tok_c5 = sc.v(lambda h: h.tensor_scalar(out=sccols[0:1, 6:7], in0=sccols[0:1, 4:5],
                                                scalar1=0.5, scalar2=None, op0=ALU.mult),
                      deps=[tok_c4], name="c5")
        tok_cb = sc.t(lambda h: h.matmul(p_b7[:, 0:1], ones1_f[:, :], sccols[0:1, 6:7],
                                         start=True, stop=True),
                      deps=chn("b7", [tok_c5]), name="cb")
        chains["b7"] = tok_cb
        tok_cbe = sc.s(lambda h: h.copy(out=cond05[:, :], in_=p_b7[:, 0:1]),
                       deps=chn("b7", [tok_cb]), name="cbe")
        chains["b7"] = tok_cbe

        def cascade(src_toks, rows, width, tag):
            toks = list(src_toks)
            cur, oth = cascA, cascB
            for r in range(7):
                t1 = sc.v(lambda h, r=r, cur=cur, rows=rows, width=width: h.max(
                    out=cands[0:rows, r * 8:(r + 1) * 8], in_=cur[0:rows, 0:width]),
                    deps=toks[-2:], name=f"{tag}mx{r}")
                toks.append(t1)
                if r == 6:
                    break
                t2 = sc.v(lambda h, r=r, cur=cur, rows=rows, width=width: h.tensor_scalar(
                    out=cmask[0:rows, 0:width], in0=cur[0:rows, 0:width],
                    scalar1=cands[0:rows, r * 8 + 7:r * 8 + 8], scalar2=None, op0=ALU.is_lt),
                    deps=[t1], name=f"{tag}lt{r}")
                t3 = sc.v(lambda h, cur=cur, oth=oth, rows=rows, width=width: h.tensor_mul(
                    out=oth[0:rows, 0:width], in0=cur[0:rows, 0:width],
                    in1=cmask[0:rows, 0:width]), deps=[t2], name=f"{tag}sel{r}")
                toks.append(t3)
                cur, oth = oth, cur
            return toks

        l1 = cascade([tok_w2, tok_cnt1], 128, 256, "L1")
        tok_f2 = sc.dma("SYNC", lambda h: h.dma_start(out=cascA[0:8, 0:896], in_=cands[:, :]),
                        deps=l1, name="fl2")
        l2 = cascade([tok_f2, tok_f2], 8, 896, "L2")
        tok_f3 = sc.dma("SYNC", lambda h: h.dma_start(out=cascA[0:1, 0:448], in_=cands[0:8, :]),
                        deps=l2, name="fl3")
        l3 = cascade([tok_f3, tok_f3], 1, 448, "L3")
        tok_vv0 = sc.v(lambda h: h.tensor_copy(out=vvals[:, :], in_=cands[0:1, 0:56]),
                       deps=l3, name="vvc0")
        tok_vb = sc.t(lambda h: h.matmul(p_b7[:, 0:50], ones1_f[:, :], vvals[0:1, 0:50],
                                         start=True, stop=True),
                      deps=chn("b7", [tok_vv0]), name="vbm")
        chains["b7"] = tok_vb
        tok_vbe = sc.s(lambda h: h.copy(out=vb50[:, :], in_=p_b7[:, 0:50]),
                       deps=chn("b7", [tok_vb]), name="vbe")
        chains["b7"] = tok_vbe
        tok_mr1 = sc.v(lambda h: h.tensor_scalar(out=vvals[0:1, 0:50], in0=vvals[0:1, 0:50],
                                                 scalar1=0.5, scalar2=None, op0=ALU.is_gt),
                       deps=[tok_vbe], name="mr1")
        tok_mr2 = sc.v(lambda h: h.tensor_scalar(out=vvals[0:1, 0:50], in0=vvals[0:1, 0:50],
                                                 scalar1=-1.0, scalar2=1e9,
                                                 op0=ALU.add, op1=ALU.mult),
                       deps=[tok_mr1], name="mr2")
        tok_mb = sc.t(lambda h: h.matmul(p_b7[:, 0:50], ones1_f[:, :], vvals[0:1, 0:50],
                                         start=True, stop=True),
                      deps=chn("b7", [tok_mr2]), name="mbm")
        chains["b7"] = tok_mb
        tok_mbe = sc.s(lambda h: h.copy(out=maskb[:, 0:50], in_=p_b7[:, 0:50]),
                       deps=chn("b7", [tok_mb]), name="mbe")
        chains["b7"] = tok_mbe

        pp_prev = None
        prev_sx = None
        for j in range(32):
            col = (j % 2) * 16 + (j // 2)
            tok_sx = sc.v(lambda h, col=col: h.tensor_scalar(
                out=Sx[:, 0:50], in0=vb50[:, :], scalar1=scloc[:, col:col + 1], scalar2=None,
                op0=ALU.is_equal),
                deps=[tok_vbe] + scl_toks + ([pp_prev, prev_sx] if pp_prev else []),
                name=f"sx{j}")
            tok_pm = sc.t(lambda h, j=j: h.matmul(
                p_b4[0:50, 64:128], Sx[:, 0:50], summ_sb[:, j * 64:(j + 1) * 64],
                start=(j == 0), stop=(j == 31)),
                deps=chn("b4", [tok_sx] + (summ_toks if j == 0 else [])), name=f"pm{j}")
            chains["b4"] = tok_pm
            pp_prev = tok_pm
            prev_sx = tok_sx
        tok_pz0 = sc.v(lambda h: h.memset(pool_f[:, :], 0.0), deps=a_end, name="plz")
        tok_pe = sc.s(lambda h: h.copy(out=pool_f[0:50, 0:64], in_=p_b4[0:50, 64:128]),
                      deps=chn("b4", [pp_prev, tok_pz0]), name="pe")
        chains["b4"] = tok_pe
        tok_pd = sc.dma("SYNC", lambda h: h.dma_start(out=pool_in.ap(), in_=pool_f[:, :]),
                        deps=[tok_pe], name="pd")
        tok_ar = sc.cc(lambda h: h.collective_compute(
            "AllReduce", ALU.add, replica_groups=[list(range(NCORES))],
            ins=[pool_in.ap().opt()], outs=[pool_out.ap().opt()]),
            deps=[tok_pd], name="ar")
        tok_pl = sc.dma("SYNC", lambda h: h.dma_start(out=pool_f[:, :], in_=pool_out.ap()),
                        deps=[tok_ar, tok_pe], name="pl")
        if d_pool is not None:
            sc.dma("SYNC", lambda h: h.dma_start(out=d_pool, in_=pool_f[:, :]),
                   deps=[tok_pl], name="dbg_pool")

        tok_pb = sc.v(lambda h: h.tensor_copy(out=poolb[:, :], in_=pool_f[:, :]),
                      deps=[tok_pl], name="pb")
        tok_ptm = sc.t(lambda h: h.transpose(p_trb[0:64, 0:64], poolb[0:64, :],
                                             eye_bf[0:64, 0:64]),
                       deps=chn("b5", [tok_pb]), name="ptm")
        chains["b5"] = tok_ptm
        tok_pte = sc.s(lambda h: h.copy(out=poolT_bf[:, :], in_=p_trb[0:64, 0:64]),
                       deps=chn("b5", [tok_ptm]), name="pte")
        chains["b5"] = tok_pte
        tok_kk = sc.t(lambda h: h.matmul(p_b4[0:64, 0:64], wkv[:, 0:64], poolT_bf[:, :],
                                         start=True, stop=True),
                      deps=chn("b4", [tok_pte] + a2w), name="kkm")
        chains["b4"] = tok_kk
        tok_kke = sc.s(lambda h: h.copy(out=kk_bf[:, :], in_=p_b4[0:64, 0:64]),
                       deps=chn("b4", [tok_kk]), name="kke")
        chains["b4"] = tok_kke
        vv_toks = []
        for half in range(2):
            tok_vvm = sc.t(lambda h, half=half: h.matmul(
                p_T[0][0:64, 0:512], poolT_bf[:, :],
                wkv[:, 64 + half * 512:64 + (half + 1) * 512], start=True, stop=True),
                deps=chn("T0", [tok_pte] + a2w), name=f"vvm{half}")
            chains["T0"] = tok_vvm
            tok_ve = sc.s(lambda h, half=half: h.copy(
                out=vv_bf[:, half * 512:(half + 1) * 512], in_=p_T[0][0:64, 0:512]),
                deps=chn("T0", [tok_vvm]), name=f"vve{half}")
            chains["T0"] = tok_ve
            vv_toks.append(tok_ve)
        tok_vvc = sc.v(lambda h: h.tensor_scalar(out=vvc_bf[:, :], in0=vv_bf[:, :],
                                                 scalar1=cond05[0:64, 0:1], scalar2=None,
                                                 op0=ALU.mult),
                       deps=vv_toks + [tok_cbe], name="vvc")
        vvT_toks = []
        for k in range(8):
            tok_tm = sc.t(lambda h, k=k: h.transpose(
                p_trb[:, 0:64], vv_bf[0:64, k * 128:(k + 1) * 128], eye_bf[0:64, 0:64]),
                deps=chn("b5", vv_toks), name=f"vvtm{k}")
            chains["b5"] = tok_tm
            tok_te = sc.s(lambda h, k=k: h.copy(out=vvT[:, k * 64:(k + 1) * 64],
                                                in_=p_trb[:, 0:64]),
                          deps=chn("b5", [tok_tm]), name=f"vvte{k}")
            chains["b5"] = tok_te
            vvT_toks.append(tok_te)
        vg_toks = []
        for half in range(2):
            def mk_vg(h, half=half):
                last = None
                for k in range(8):
                    last = h.matmul(p_T[0][0:64, 0:512], vvT[:, k * 64:(k + 1) * 64],
                                    wg2[:, k * 1024 + half * 512:k * 1024 + half * 512 + 512],
                                    start=(k == 0), stop=(k == 7))
                return last
            tok_vg = sc.t(mk_vg, deps=chn("T0", vvT_toks + a2w), name=f"vgm{half}")
            chains["T0"] = tok_vg
            tok_vge = sc.s(lambda h, half=half: h.copy(
                out=VG_bf[:, half * 512:(half + 1) * 512], in_=p_T[0][0:64, 0:512]),
                deps=chn("T0", [tok_vg]), name=f"vge{half}")
            chains["T0"] = tok_vge
            vg_toks.append(tok_vge)

        pool_ready = [tok_kke, tok_vvc, tok_mbe, tok_cbe] + vv_toks + vg_toks
        b_sq_toks = []
        v2h_outs = {}
        prev_y_readers = []
        prev_x_readers = []
        prev_v2_state = {}
        prev_pr = []
        wg_last_reader = [None]
        for s in range(16):
            gt0 = s * GRP
            bloc = s // 8
            t0 = (s % 8) * GRP
            sp = s % 2
            t_y = sc.dma("SYNC", lambda h, gt0=gt0: h.dma_start(
                out=ybuf[:, :], in_=YSC.ap()[:, :, gt0:gt0 + GRP].rearrange("k p t -> p k t")),
                deps=pool_ready + prev_y_readers + [ld_y_prev], name=f"by{s}")
            t_x = sc.dma("SYNC", lambda h, bloc=bloc, t0=t0: h.dma_start(
                out=xbuf[:, :].rearrange("p (tt d) -> p tt d", tt=2),
                in_=XI[bloc, t0:t0 + GRP, :].rearrange("(tt p) d -> p tt d", tt=2)),
                deps=(prev_x_readers or pool_ready), name=f"bx{s}")

            def mk_q(h):
                last = None
                for k in range(8):
                    last = h.matmul(p_b4[0:64, 0:256], wq[:, k * 64:(k + 1) * 64],
                                    ybuf[:, k * GRP:(k + 1) * GRP],
                                    start=(k == 0), stop=(k == 7))
                return last
            tok_q = sc.t(mk_q, deps=chn("b4", [t_y]), name=f"bq{s}")
            chains["b4"] = tok_q
            tok_qe = sc.s(lambda h: h.copy(out=q_bf[:, :], in_=p_b4[0:64, 0:256]),
                          deps=chn("b4", [tok_q]), name=f"bqe{s}")
            chains["b4"] = tok_qe
            ptT_toks = []
            for tt in range(2):
                tok_lg = sc.t(lambda h, tt=tt: h.matmul(
                    p_b7[:, 0:50], q_bf[:, tt * 128:(tt + 1) * 128], kk_bf[:, 0:50],
                    start=True, stop=True), deps=chn("b7", [tok_qe, tok_kke]),
                    name=f"blg{s}_{tt}")
                chains["b7"] = tok_lg
                tok_l1 = sc.v(lambda h: h.scalar_tensor_tensor(
                    out=lgb[:, 0:50], in0=p_b7[:, 0:50], scalar=0.25, in1=maskb[:, 0:50],
                    op0=ALU.mult, op1=ALU.add), deps=chn("b7", [tok_lg, tok_mbe]),
                    name=f"bl1{s}_{tt}")
                chains["b7"] = tok_l1
                tok_mx = sc.v(lambda h: h.reduce_max(out=sccols[:, 7:8], in_=lgb[:, 0:50],
                                                     axis=AX.X),
                              deps=[tok_l1] + prev_pr[-1:], name=f"bmx{s}_{tt}")
                tok_nm = sc.v(lambda h: h.tensor_scalar(out=sccols[:, 7:8], in0=sccols[:, 7:8],
                                                        scalar1=-1.0, scalar2=None,
                                                        op0=ALU.mult),
                              deps=[tok_mx], name=f"bnm{s}_{tt}")
                tok_ex = sc.s(lambda h: h.activation(out=lgb[:, 0:50], in_=lgb[:, 0:50],
                                                     func=AF.Exp, bias=sccols[:, 7:8]),
                              deps=[tok_nm], name=f"bex{s}_{tt}")
                tok_sm2 = sc.v(lambda h: h.reduce_sum(out=sccols[:, 7:8], in_=lgb[:, 0:50],
                                                      axis=AX.X), deps=[tok_ex],
                               name=f"bsm{s}_{tt}")
                tok_rc = sc.v(lambda h: h.reciprocal(out=sccols[:, 7:8], in_=sccols[:, 7:8]),
                              deps=[tok_sm2], name=f"brc{s}_{tt}")
                tok_pr = sc.v(lambda h: h.tensor_scalar(out=prow[:, 0:50], in0=lgb[:, 0:50],
                                                        scalar1=sccols[:, 7:8], scalar2=None,
                                                        op0=ALU.mult),
                              deps=[tok_rc], name=f"bpr{s}_{tt}")
                prev_pr.append(tok_pr)
                tok_ptm2 = sc.t(lambda h: h.transpose(p_trb[0:50, :], prow[:, 0:50],
                                                      eye_bf[:, :]),
                                deps=chn("b5", [tok_pr]), name=f"bptm{s}_{tt}")
                chains["b5"] = tok_ptm2
                tok_pte2 = sc.s(lambda h, tt=tt: h.copy(
                    out=pT_bf[0:50, tt * 128:(tt + 1) * 128], in_=p_trb[0:50, 0:128]),
                    deps=chn("b5", [tok_ptm2]), name=f"bpte{s}_{tt}")
                chains["b5"] = tok_pte2
                ptT_toks.append(tok_pte2)
            y_readers = [tok_q]
            x_readers = []
            v2_writes = []
            for m in range(8):
                tok_rm = sc.t(lambda h, m=m: h.matmul(
                    p_inproj[0][:, :], vvc_bf[0:50, m * 128:(m + 1) * 128], pT_bf[0:50, :],
                    start=True, stop=True),
                    deps=chn("ip0", ptT_toks + [tok_vvc]), name=f"brt{s}_{m}")
                chains["ip0"] = tok_rm

                def mk_g(h, m=m):
                    for k in range(8):
                        h.matmul(p_inproj[1][:, :],
                                 wg1[:, k * 1024 + m * 128:k * 1024 + m * 128 + 128],
                                 ybuf[:, k * GRP:(k + 1) * GRP], start=(k == 0), stop=False)
                    return h.matmul(p_inproj[1][:, :], VG_bf[0:50, m * 128:(m + 1) * 128],
                                    pT_bf[0:50, :], start=False, stop=True)
                tok_gm = sc.t(mk_g, deps=chn("ip1", ptT_toks + [t_y] + vg_toks),
                              name=f"bg{s}_{m}")
                chains["ip1"] = tok_gm
                wg_last_reader[0] = tok_gm
                y_readers.append(tok_gm)
                tok_gt = sc.s(lambda h: h.activation(out=gate_bf[:, :], in_=p_inproj[1][:, :],
                                                     func=AF.Tanh, scale=0.5),
                              deps=chn("ip1", [tok_gm]), name=f"bgt{s}_{m}")
                chains["ip1"] = tok_gt
                tok_t2 = sc.v(lambda h: h.scalar_tensor_tensor(
                    out=t2b[:, :], in0=gate_bf[:, :], scalar=1.0, in1=p_inproj[0][:, :],
                    op0=ALU.add, op1=ALU.mult), deps=chn("ip0", [tok_gt]), name=f"bt2{s}_{m}")
                chains["ip0"] = tok_t2
                tok_a3 = sc.v(lambda h, m=m: h.tensor_tensor(
                    out=a3b[:, m * GRP:(m + 1) * GRP], in0=t2b[:, :],
                    in1=ybuf[:, m * GRP:(m + 1) * GRP], op=ALU.add),
                    deps=[tok_t2], name=f"ba3{s}_{m}")
                y_readers.append(tok_a3)
                for tt in range(2):
                    tok_tm = sc.t(lambda h, m=m, tt=tt: h.transpose(
                        p_trb[:, :], a3b[:, m * GRP + tt * 128:m * GRP + tt * 128 + 128],
                        eye_bf[:, :]), deps=chn("b5", [tok_a3]), name=f"btm{s}_{m}_{tt}")
                    chains["b5"] = tok_tm
                    tok_v2 = sc.v(lambda h, m=m, tt=tt, sp=sp: h.scalar_tensor_tensor(
                        out=v2loc[sp][:, tt * 1024 + m * 128:tt * 1024 + (m + 1) * 128],
                        in0=xbuf[:, tt * 1024 + m * 128:tt * 1024 + (m + 1) * 128],
                        scalar=2.0, in1=p_trb[:, :], op0=ALU.mult, op1=ALU.add),
                        deps=chn("b5", [tok_tm, t_x] + prev_v2_state.get(sp, [])),
                        name=f"bv2{s}_{m}_{tt}")
                    chains["b5"] = tok_v2
                    x_readers.append(tok_v2)
                    v2_writes.append(tok_v2)
            sq_s = []
            for tt in range(2):
                jj = 2 * s + tt
                tok_s1 = sc.s(lambda h, jj=jj, sp=sp, tt=tt: h.activation(
                    out=dump_b[:, :], in_=v2loc[sp][:, tt * 1024:(tt + 1) * 1024],
                    func=AF.Copy, scale=1.0 / 1024.0, accum_out=mucol[:, jj:jj + 1]),
                    deps=v2_writes + b_sq_toks[-1:], name=f"bsu{s}_{tt}")
                tok_s2 = sc.s(lambda h, jj=jj, sp=sp, tt=tt: h.activation(
                    out=dump_b[:, :], in_=v2loc[sp][:, tt * 1024:(tt + 1) * 1024],
                    func=AF.Square, scale=1.0 / 32.0, accum_out=sqcol[:, jj:jj + 1]),
                    deps=[tok_s1], name=f"bsq{s}_{tt}")
                b_sq_toks.append(tok_s2)
                sq_s.append(tok_s2)
            v2h_outs[s] = sc.dma("SYNC", lambda h, s=s, sp=sp: h.dma_start(
                out=V2H.ap()[:, s * 2048:(s + 1) * 2048], in_=v2loc[sp][:, :]),
                deps=v2_writes + sq_s, name=f"bvh{s}")
            prev_v2_state[sp] = sq_s + [v2h_outs[s]]
            prev_y_readers = y_readers
            prev_x_readers = x_readers

        all_v2h = list(v2h_outs.values())
        tok_m2 = sc.v(lambda h: h.tensor_mul(out=sscol[:, :], in0=mucol[:, :], in1=mucol[:, :]),
                      deps=b_sq_toks + all_v2h, name="bm2")
        tok_m4 = sc.v(lambda h: h.tensor_sub(out=sqcol[:, :], in0=sqcol[:, :], in1=sscol[:, :]),
                      deps=[tok_m2], name="bm4")
        tok_m5 = sc.v(lambda h: h.tensor_scalar(out=sqcol[:, :], in0=sqcol[:, :],
                                                scalar1=EPS, scalar2=None, op0=ALU.add),
                      deps=[tok_m4], name="bm5")
        tok_m6 = sc.s(lambda h: h.activation(out=sqcol[:, :], in_=sqcol[:, :], func=AF.Ln),
                      deps=[tok_m5], name="bm6")
        tok_m7 = sc.s(lambda h: h.activation(out=sscol[:, :], in_=sqcol[:, :], func=AF.Exp,
                                             scale=-0.5), deps=[tok_m6], name="bm7")

        x1_toks = []
        v2in = {}
        slot_readers = {i: [] for i in range(4)}
        prev_dg = []
        for j in range(32):
            if j == 0:
                for jj0 in range(3):
                    v2in[jj0] = sc.dma("SYNC", lambda h, jj0=jj0: h.dma_start(
                        out=v2l4[:, (jj0 % 4) * 1024:(jj0 % 4 + 1) * 1024],
                        in_=V2H.ap()[:, jj0 * 1024:(jj0 + 1) * 1024]),
                        deps=all_v2h + b_sq_toks, name=f"b3in{jj0}")
            if j + 3 < 32:
                jn = j + 3
                v2in[jn] = sc.dma("SYNC", lambda h, jn=jn: h.dma_start(
                    out=v2l4[:, (jn % 4) * 1024:(jn % 4 + 1) * 1024],
                    in_=V2H.ap()[:, jn * 1024:(jn + 1) * 1024]),
                    deps=slot_readers[jn % 4] + [v2h_outs[jn // 2]], name=f"b3in{jn}")
            sl = j % 4
            tok_sub = sc.v(lambda h, j=j, sl=sl: h.tensor_scalar(
                out=v2l4[:, sl * 1024:(sl + 1) * 1024], in0=v2l4[:, sl * 1024:(sl + 1) * 1024],
                scalar1=mucol[:, j:j + 1], scalar2=None, op0=ALU.subtract),
                deps=[v2in[j], tok_m7], name=f"bns{j}")
            tok_dg = sc.v(lambda h, j=j: h.tensor_scalar(
                out=diag2[:, :], in0=eye_bf[:, :], scalar1=sscol[:, j:j + 1], scalar2=None,
                op0=ALU.mult), deps=[tok_m7] + prev_dg, name=f"bdg{j}")
            prev_dg = []
            for k in range(8):
                tok_mm = sc.t(lambda h, j=j, k=k, sl=sl: h.matmul(
                    p_b4[:, 0:128], v2l4[:, sl * 1024 + k * 128:sl * 1024 + (k + 1) * 128],
                    diag2[:, :], start=True, stop=True),
                    deps=chn("b4", [tok_sub, tok_dg]), name=f"bxm{j}_{k}")
                chains["b4"] = tok_mm
                prev_dg.append(tok_mm)
                tok_ev = sc.s(lambda h, j=j, k=k: h.copy(
                    out=x1fm[:, k * 4096 + j * 128:k * 4096 + (j + 1) * 128],
                    in_=p_b4[:, 0:128]),
                    deps=chn("b4", [tok_mm]), name=f"bxe{j}_{k}")
                chains["b4"] = tok_ev
                x1_toks.append(tok_ev)
            slot_readers[sl] = prev_dg + [tok_sub]
        if d_x1 is not None:
            def mk_dbg_x1(h, sem):
                for k in range(8):
                    h.dma_start(out=d_x1[k * 128:(k + 1) * 128, :],
                                in_=x1fm[:, k * 4096:k * 4096 + 256]).then_inc(sem, 16)
            sc.dma_multi("G", mk_dbg_x1, 8, deps=x1_toks[-1:], name="dbg_x1")

        tok_sel = sc.dma("SYNC", lambda h: h.dma_start(out=selec[:, :], in_=C["sele_bf"]),
                         deps=a_end, name="c_sele")
        tok_wr = sc.dma("G", lambda h: h.dma_start(
            out=wrt[:, :].rearrange("p (k ex) -> p k ex", k=8),
            in_=W["ffn_gate_w"].rearrange("(k p) ex -> p k ex", k=8)),
            deps=a_end, name="c_wrt")
        tok_gb = sc.dma("SYNC", lambda h: h.dma_start(
            out=gb_col[:, :], in_=W["ffn_gate_b"].rearrange("(p one) -> p one", one=1)),
            deps=a_end, name="c_gb")
        tok_gbh = sc.v(lambda h: h.tensor_scalar(out=gb_col[:, :], in0=gb_col[:, :],
                                                 scalar1=0.5, scalar2=None, op0=ALU.mult),
                       deps=[tok_gb], name="c_gbh")

        gfm_toks = []
        prev_th = []
        for j in range(32):
            def mk_rt(h, j=j):
                last = None
                for k in range(8):
                    last = h.matmul(p_b4[0:16, 0:128], wrt[:, k * 16:(k + 1) * 16],
                                    x1fm[:, k * 4096 + j * 128:k * 4096 + (j + 1) * 128],
                                    start=(k == 0), stop=(k == 7))
                return last
            tok_rm2 = sc.t(mk_rt, deps=chn("b4", x1_toks[-1:] + [tok_wr]), name=f"crt{j}")
            chains["b4"] = tok_rm2
            tok_th2 = sc.s(lambda h: h.activation(out=thb[:, :], in_=p_b4[0:16, 0:128],
                                                  func=AF.Tanh, scale=0.5, bias=gb_col[:, :]),
                           deps=chn("b4", [tok_rm2, tok_gbh] + prev_th), name=f"cth{j}")
            chains["b4"] = tok_th2
            tok_tt2 = sc.t(lambda h: h.transpose(p_trb[:, 0:16], thb[:, :],
                                                 eye_bf[0:16, 0:16]),
                           deps=chn("b5", [tok_th2]), name=f"ctt{j}")
            chains["b5"] = tok_tt2
            prev_th = [tok_tt2]
            tok_te2 = sc.s(lambda h: h.copy(out=cga[:, :], in_=p_trb[:, 0:16]),
                           deps=chn("b5", [tok_tt2]), name=f"cte{j}")
            chains["b5"] = tok_te2
            tk = sc.v(lambda h: h.reduce_max(out=rcols[:, 0:1], in_=cga[:, :], axis=AX.X),
                      deps=[tok_te2], name=f"cm1{j}")
            tk = sc.v(lambda h: h.tensor_scalar(out=ct2[:, :], in0=cga[:, :],
                                                scalar1=rcols[:, 0:1], scalar2=None,
                                                op0=ALU.is_ge), deps=[tk], name=f"ce1{j}")
            tk = sc.v(lambda h: h.scalar_tensor_tensor(out=clg[:, :], in0=ct2[:, :],
                                                       scalar=-10.0, in1=cga[:, :],
                                                       op0=ALU.mult, op1=ALU.add),
                      deps=[tk], name=f"cz2{j}")
            tk = sc.v(lambda h: h.reduce_max(out=rcols[:, 1:2], in_=clg[:, :], axis=AX.X),
                      deps=[tk], name=f"cm2{j}")
            tk = sc.v(lambda h: h.tensor_scalar(out=clg[:, :], in0=clg[:, :],
                                                scalar1=rcols[:, 1:2], scalar2=None,
                                                op0=ALU.is_ge), deps=[tk], name=f"ce2{j}")
            tk = sc.v(lambda h: h.tensor_add(out=rcols[:, 2:3], in0=rcols[:, 0:1],
                                             in1=rcols[:, 1:2]), deps=[tk], name=f"csu{j}")
            tk = sc.v(lambda h: h.tensor_scalar(out=rcols[:, 2:3], in0=rcols[:, 2:3],
                                                scalar1=2.0, scalar2=None, op0=ALU.add),
                      deps=[tk], name=f"csu2{j}")
            tk = sc.v(lambda h: h.reciprocal(out=rcols[:, 2:3], in_=rcols[:, 2:3]),
                      deps=[tk], name=f"crc{j}")
            tk = sc.v(lambda h: h.tensor_scalar(out=rcols[:, 3:4], in0=rcols[:, 0:1],
                                                scalar1=1.0, scalar2=rcols[:, 2:3],
                                                op0=ALU.add, op1=ALU.mult),
                      deps=[tk], name=f"cv1{j}")
            tk = sc.v(lambda h: h.tensor_scalar(out=rcols[:, 4:5], in0=rcols[:, 1:2],
                                                scalar1=1.0, scalar2=rcols[:, 2:3],
                                                op0=ALU.add, op1=ALU.mult),
                      deps=[tk], name=f"cv2{j}")
            tk = sc.v(lambda h: h.tensor_scalar(out=ct2[:, :], in0=ct2[:, :],
                                                scalar1=rcols[:, 3:4], scalar2=None,
                                                op0=ALU.mult), deps=[tk], name=f"cg1{j}")
            tk = sc.v(lambda h, j=j: h.scalar_tensor_tensor(
                out=gtok[:, j * 16:(j + 1) * 16], in0=clg[:, :], scalar=rcols[:, 4:5],
                in1=ct2[:, :], op0=ALU.mult, op1=ALU.add), deps=[tk], name=f"cg2{j}")
            tok_gtm = sc.t(lambda h, j=j: h.transpose(
                p_trb[0:16, :], gtok[:, j * 16:(j + 1) * 16], eye_bf[:, :]),
                deps=chn("b5", [tk]), name=f"cgt{j}")
            chains["b5"] = tok_gtm
            tok_gte = sc.s(lambda h, j=j: h.copy(out=gfm[:, j * 128:(j + 1) * 128],
                                                 in_=p_trb[0:16, 0:128]),
                           deps=chn("b5", [tok_gtm]), name=f"cge{j}")
            chains["b5"] = tok_gte
            gfm_toks.append(tok_gte)
        if d_gw is not None:
            sc.dma("G", lambda h: h.dma_start(out=d_gw, in_=gtok[:, :]),
                   deps=gfm_toks, name="dbg_gw")

        def load_expert(ei, slot, deps):
            names = (("experts_wg", "experts_wu", "experts_wd") if ei < 16 else
                     ("shared_wg", "shared_wu", "shared_wd"))
            toks = []
            for wnm, key, ks in ((names[0], "wg", 8), (names[1], "wu", 8), (names[2], "wd", 4)):
                def mk(h, ei=ei, slot=slot, wnm=wnm, key=key, ks=ks):
                    srcap = W[wnm][ei] if ei < 16 else W[wnm]
                    return h.dma_start(
                        out=wslot[slot][key][:, :].rearrange("p (k i) -> p k i", k=ks),
                        in_=srcap.rearrange("(k p) i -> p k i", k=ks))
                toks.append(sc.dma("G", mk, deps=deps, name=f"cw_{key}{ei}"))
            return toks

        dacc = [p_ip_t[:, i * 256:(i + 1) * 256] for i in range(4)] + \
               [p_T_t[:, i * 256:(i + 1) * 256] for i in range(4)]
        _nexp = int(os.environ.get("MK_NEXP", "17"))
        stream = (list(range(_nexp - 1)) + [16]) if _nexp < 17 else list(range(16)) + [16]
        pairs = [stream[i:i + 2] for i in range(0, len(stream), 2)]
        wtoks = {}
        slot_uses = {0: [], 1: [], 2: []}
        wtoks[0] = load_expert(0, 0, [wg_last_reader[0]])
        wtoks[1] = load_expert(1, 1, [wg_last_reader[0]])
        acc_out = {}
        acc_in = {}
        last_gbc_mm = [None]
        last_h = [None]
        dn_last = [None]
        for pi2, elist in enumerate(pairs):
            i = 2 * pi2 + 2
            if i < len(stream):
                slot = i % 3
                wtoks[i] = load_expert(stream[i], slot,
                                       slot_uses[slot] or [wg_last_reader[0]])
                slot_uses[slot] = []
            for ch in range(16):
                ct0 = ch * 256
                chp = ch % 2
                if pi2 > 0:
                    if ch == 0:
                        for cpre in range(2):
                            acc_in[cpre] = sc.dma("SYNC", lambda h, cpre=cpre: h.dma_start(
                                out=accbuf[cpre % 2][:, :], in_=ACCH.ap()[cpre]),
                                deps=[acc_out[cpre], acc_out[14 + cpre]],
                                name=f"fai{pi2}_{cpre}")
                ev_deps_extra = [acc_in[ch]] if pi2 > 0 else []
                dn_toks = []
                h_all = []
                for idx, ei in enumerate(elist):
                    si2 = 2 * pi2 + idx
                    slot = si2 % 3
                    ep = idx
                    if ei < 16:
                        tok_gb2 = sc.t(lambda h, ei=ei, ch=ch: h.matmul(
                            p_b6[:, 256:512], selec[:, ei * 128:(ei + 1) * 128],
                            gfm[0:16, ch * 256:(ch + 1) * 256], start=True, stop=True),
                            deps=chn("b6", ([tok_sel] + gfm_toks if pi2 == 0 and ch == 0
                                            else [])), name=f"fgb{ei}_{ch}")
                        chains["b6"] = tok_gb2
                        last_gbc_mm[0] = tok_gb2
                        tok_gbe = sc.s(lambda h, ep=ep: h.copy(
                            out=gbc[:, ep * 256:(ep + 1) * 256], in_=p_b6[:, 256:512]),
                            deps=chn("b6", [tok_gb2]), name=f"fge{ei}_{ch}")
                        chains["b6"] = tok_gbe
                    else:
                        tok_gbe = None
                    h_toks = []
                    for it in range(4):
                        def mk_hg(h, slot=slot, it=it, ct0=ct0):
                            last = None
                            for k in range(8):
                                last = h.matmul(
                                    p_b7[:, 0:256],
                                    wslot[slot]["wg"][:, k * 512 + it * 128:k * 512 + it * 128 + 128],
                                    x1fm[:, k * 4096 + ct0:k * 4096 + ct0 + 256],
                                    start=(k == 0), stop=(k == 7))
                            return last
                        tok_hg = sc.t(mk_hg, deps=chn("b7", wtoks[si2] + x1_toks[-1:]),
                                      name=f"fhg{ei}_{ch}_{it}")
                        chains["b7"] = tok_hg
                        slot_uses[slot].append(tok_hg)
                        tok_sl = sc.s(lambda h, ep=ep, it=it: h.activation(
                            out=hsil[:, (ep * 4 + it) * 256:(ep * 4 + it + 1) * 256],
                            in_=p_b7[:, 0:256], func=AF.Silu),
                            deps=chn("b7", [tok_hg]), name=f"fsl{ei}_{ch}_{it}")
                        chains["b7"] = tok_sl

                        def mk_hu(h, slot=slot, it=it, ct0=ct0):
                            last = None
                            for k in range(8):
                                last = h.matmul(
                                    p_b4[:, 0:256],
                                    wslot[slot]["wu"][:, k * 512 + it * 128:k * 512 + it * 128 + 128],
                                    x1fm[:, k * 4096 + ct0:k * 4096 + ct0 + 256],
                                    start=(k == 0), stop=(k == 7))
                            return last
                        tok_hu = sc.t(mk_hu, deps=chn("b4", wtoks[si2]),
                                      name=f"fhu{ei}_{ch}_{it}")
                        chains["b4"] = tok_hu
                        slot_uses[slot].append(tok_hu)
                        tok_h = sc.v(lambda h, ep=ep, it=it: h.tensor_tensor(
                            out=hcmb[:, (ep * 4 + it) * 256:(ep * 4 + it + 1) * 256],
                            in0=hsil[:, (ep * 4 + it) * 256:(ep * 4 + it + 1) * 256],
                            in1=p_b4[:, 0:256], op=ALU.mult),
                            deps=chn("b4", [tok_sl, tok_hu]), name=f"fh{ei}_{ch}_{it}")
                        chains["b4"] = tok_h
                        if ei < 16:
                            tok_h = sc.v(lambda h, ep=ep, it=it: h.tensor_tensor(
                                out=hcmb[:, (ep * 4 + it) * 256:(ep * 4 + it + 1) * 256],
                                in0=hcmb[:, (ep * 4 + it) * 256:(ep * 4 + it + 1) * 256],
                                in1=gbc[:, ep * 256:(ep + 1) * 256], op=ALU.mult),
                                deps=[tok_h, tok_gbe], name=f"fh2{ei}_{ch}_{it}")
                        h_toks.append(tok_h)
                        last_h[0] = tok_h

                    h_all.append((2 * pi2 + idx, idx, h_toks))
                def mk_dn(h, h_all=tuple(h_all)):
                    last = None
                    for m in range(8):
                        for (si3, idx3, _ht) in h_all:
                            slot3 = si3 % 3
                            for k in range(4):
                                last = h.matmul(
                                    dacc[m],
                                    wslot[slot3]["wd"][:, k * 1024 + m * 128:k * 1024 + m * 128 + 128],
                                    hcmb[:, (idx3 * 4 + k) * 256:(idx3 * 4 + k + 1) * 256],
                                    start=(idx3 == 0 and k == 0),
                                    stop=(idx3 == len(h_all) - 1 and k == 3))
                    return last
                dn_deps = [t for (_si, _idx, ht) in h_all for t in ht]
                for (si3, _idx, _ht) in h_all:
                    dn_deps += wtoks[si3]
                dn_deps += chn("ip0", []) + chn("ip1", []) + chn("T0", []) + chn("T1", [])
                tok_dn = sc.t(mk_dn, deps=dn_deps, name=f"fdn{pi2}_{ch}")
                for (si3, _idx, _ht) in h_all:
                    slot_uses[si3 % 3].append(tok_dn)
                dn_toks.append(tok_dn)
                dn_last[0] = tok_dn
                ev_toks = []
                for m in range(8):
                    if pi2 == 0:
                        tok_ev = sc.v(lambda h, m=m, chp=chp: h.tensor_copy(
                            out=accbuf[chp][:, m * 256:(m + 1) * 256], in_=dacc[m]),
                            deps=dn_toks[-1:] + ev_deps_extra +
                                 ([acc_out[ch - 2]] if ch >= 2 else []),
                            name=f"fac{pi2}_{ch}_{m}")
                    else:
                        tok_ev = sc.v(lambda h, m=m, chp=chp: h.tensor_tensor(
                            out=accbuf[chp][:, m * 256:(m + 1) * 256],
                            in0=accbuf[chp][:, m * 256:(m + 1) * 256],
                            in1=dacc[m], op=ALU.add),
                            deps=dn_toks[-1:] + ev_deps_extra, name=f"fac{pi2}_{ch}_{m}")
                    if m < 2:
                        chains["ip0"] = tok_ev
                    elif m < 4:
                        chains["ip1"] = tok_ev
                    elif m < 6:
                        chains["T0"] = tok_ev
                    else:
                        chains["T1"] = tok_ev
                    ev_toks.append(tok_ev)
                acc_out[ch] = sc.dma("SYNC", lambda h, ch=ch, chp=chp: h.dma_start(
                    out=ACCH.ap()[ch], in_=accbuf[chp][:, :]), deps=ev_toks,
                    name=f"fao{pi2}_{ch}")
                if pi2 > 0 and ch + 2 < 16:
                    acc_in[ch + 2] = sc.dma("SYNC", lambda h, ch=ch, chp=chp: h.dma_start(
                        out=accbuf[chp][:, :], in_=ACCH.ap()[ch + 2]),
                        deps=[acc_out[ch], acc_out[ch + 2]], name=f"fai{pi2}_{ch + 2}")
            i = 2 * pi2 + 3
            if i < len(stream):
                slot = i % 3
                wtoks[i] = load_expert(stream[i], slot,
                                       slot_uses[slot] or [wg_last_reader[0]])
                slot_uses[slot] = []
        if d_acc0 is not None:
            sc.dma("G", lambda h: h.dma_start(out=d_acc0, in_=ACCH.ap()[0]),
                   deps=[acc_out[0]], name="dbg_acc0")

        c3_v3h = {}
        c3_in = {}
        c3_slot_readers = {0: [acc_out[14]], 1: [acc_out[15]]}
        stat_prev = []
        for ch in range(16):
            chp = ch % 2
            ct0 = ch * 256
            c3_in[ch] = sc.dma("SYNC", lambda h, ch=ch, chp=chp: h.dma_start(
                out=accbuf[chp][:, :], in_=ACCH.ap()[ch]),
                deps=[acc_out[ch]] + c3_slot_readers[chp], name=f"g3i{ch}")
            v3_toks = []
            for m in range(8):
                tok_v3 = sc.v(lambda h, m=m, ct0=ct0, chp=chp: h.scalar_tensor_tensor(
                    out=v3fm[:, m * 256:(m + 1) * 256],
                    in0=x1fm[:, m * 4096 + ct0:m * 4096 + ct0 + 256], scalar=2.0,
                    in1=accbuf[chp][:, m * 256:(m + 1) * 256], op0=ALU.mult, op1=ALU.add),
                    deps=[c3_in[ch], dn_last[0]] + stat_prev, name=f"g3v{ch}_{m}")
                v3_toks.append(tok_v3)
            c3_slot_readers[chp] = list(v3_toks)
            tr_toks = []
            for tt in range(2):
                for m in range(8):
                    tok_tm = sc.t(lambda h, m=m, tt=tt: h.transpose(
                        p_trb[:, :], v3fm[:, m * 256 + tt * 128:m * 256 + tt * 128 + 128],
                        eye_bf[:, :]), deps=chn("b5", [v3_toks[m]]), name=f"g3t{ch}_{tt}_{m}")
                    chains["b5"] = tok_tm
                    tok_te = sc.s(lambda h, m=m, tt=tt: h.copy(
                        out=v3tok2[:, tt * 1024 + m * 128:tt * 1024 + (m + 1) * 128],
                        in_=p_trb[:, :]),
                        deps=chn("b5", [tok_tm, last_h[0]] + stat_prev), name=f"g3e{ch}_{tt}_{m}")
                    chains["b5"] = tok_te
                    tr_toks.append(tok_te)
            st_toks = []
            for tt in range(2):
                jj = 2 * ch + tt
                tok_s1 = sc.s(lambda h, jj=jj, tt=tt: h.activation(
                    out=outst2[0][:, :], in_=v3tok2[:, tt * 1024:(tt + 1) * 1024],
                    func=AF.Copy, scale=-1.0 / 1024.0, accum_out=fmu[:, jj:jj + 1]),
                    deps=tr_toks[tt * 8:tt * 8 + 8] + st_toks[-1:], name=f"g3s{ch}_{tt}")
                tok_s2 = sc.s(lambda h, jj=jj, tt=tt: h.activation(
                    out=outst2[0][:, :], in_=v3tok2[:, tt * 1024:(tt + 1) * 1024],
                    func=AF.Square, scale=1.0 / 32.0, accum_out=fsq[:, jj:jj + 1]),
                    deps=[tok_s1], name=f"g3q{ch}_{tt}")
                st_toks += [tok_s1, tok_s2]
            c3_v3h[ch] = sc.dma("SYNC", lambda h, ch=ch: h.dma_start(
                out=V3H.ap()[ch], in_=v3tok2[:, :]), deps=tr_toks + st_toks,
                name=f"g3o{ch}")
            stat_prev = st_toks + [c3_v3h[ch]]

        tok_f2b = sc.v(lambda h: h.tensor_mul(out=fs2[:, :], in0=fmu[:, :], in1=fmu[:, :]),
                       deps=stat_prev, name="g4a")
        tok_f4 = sc.v(lambda h: h.tensor_sub(out=fsq[:, :], in0=fsq[:, :], in1=fs2[:, :]),
                      deps=[tok_f2b] + stat_prev, name="g4b")
        tok_f5 = sc.v(lambda h: h.tensor_scalar(out=fsq[:, :], in0=fsq[:, :],
                                                scalar1=EPS, scalar2=None, op0=ALU.add),
                      deps=[tok_f4], name="g4c")
        tok_f6 = sc.s(lambda h: h.activation(out=fsq[:, :], in_=fsq[:, :], func=AF.Ln),
                      deps=[tok_f5], name="g4d")
        tok_f7 = sc.s(lambda h: h.activation(out=fsq[:, :], in_=fsq[:, :], func=AF.Exp,
                                             scale=-0.5), deps=[tok_f6], name="g4e")

        v3slots = [v3tok2, at("v3tok3", [128, 2 * 1024], BF16,
                              nc.lookup_mloc(selec).addr - RB)]
        c5_in = {}
        c5_readers = {0: [], 1: []}
        out_dmas = []
        for ch in range(16):
            cs = ch % 2
            if ch == 0:
                for cpre in range(2):
                    c5_in[cpre] = sc.dma("SYNC", lambda h, cpre=cpre: h.dma_start(
                        out=v3slots[cpre % 2][:, :], in_=V3H.ap()[cpre]),
                        deps=[c3_v3h[cpre], tok_f7, last_gbc_mm[0]] + stat_prev,
                        name=f"g5i{cpre}")
            rds = []
            for tt in range(2):
                jj = 2 * ch + tt
                op2 = jj % 2
                tok_n1 = sc.s(lambda h, jj=jj, tt=tt, cs=cs, op2=op2: h.activation(
                    out=outst2[op2][:, :], in_=v3slots[cs][:, tt * 1024:(tt + 1) * 1024],
                    func=AF.Identity, bias=fmu[:, jj:jj + 1]),
                    deps=[c5_in[ch], tok_f7] + out_dmas[-2:-1], name=f"g5n{ch}_{tt}")
                tok_n2 = sc.v(lambda h, jj=jj, op2=op2: h.tensor_scalar(
                    out=outst2[op2][:, :], in0=outst2[op2][:, :],
                    scalar1=fsq[:, jj:jj + 1], scalar2=None, op0=ALU.mult),
                    deps=[tok_n1], name=f"g5m{ch}_{tt}")
                rds.append(tok_n1)
                gtt = ch * 256 + tt * 128
                bloc2 = gtt // T
                trow = gtt % T
                out_dmas.append(sc.dma("SYNC", lambda h, bloc2=bloc2, trow=trow, op2=op2:
                                       h.dma_start(out=OUT[bloc2, trow:trow + 128, :],
                                                   in_=outst2[op2][:, :]),
                                       deps=[tok_n2], name=f"g5o{ch}_{tt}"))
            c5_readers[cs] = rds
            if ch + 2 < 16:
                c5_in[ch + 2] = sc.dma("SYNC", lambda h, ch=ch, cs=cs: h.dma_start(
                    out=v3slots[cs][:, :], in_=V3H.ap()[ch + 2]),
                    deps=[c3_v3h[ch + 2]] + rds, name=f"g5i{ch + 2}")

        block = e(nc.Block())

        @block.tensor
        def _(h):
            emit_engine_stream(sc, "T", h, sems)

        @block.vector
        def _(h):
            emit_engine_stream(sc, "V", h, sems)

        @block.scalar
        def _(h):
            emit_engine_stream(sc, "S", h, sems)

        @block.gpsimd
        def _(h):
            emit_engine_stream(sc, "G", h, sems)

        @block.sync
        def _(h):
            emit_engine_stream(sc, "SYNC", h, sems)
            for i in range(8):
                if sc.cnt[f"DS{i}"]:
                    h.wait_ge(sems[f"DS{i}"], sc.cnt[f"DS{i}"])
                if sc.cnt[f"DG{i}"]:
                    h.wait_ge(sems[f"DG{i}"], sc.cnt[f"DG{i}"])

    return nc, dbg_outs


def _in_maps(inputs):
    x = np.ascontiguousarray(inputs["L0_attn_x"])
    maps = []
    ren = {
        "norm_w": "L0_attn_norm_w", "in_proj_w": "L0_attn_in_proj_w", "conv_w": "L0_attn_conv_w",
        "conv_b": "L0_attn_conv_b", "dt_bias": "L0_attn_dt_bias", "A_log": "L0_attn_A_log",
        "Dp": "L0_attn_D", "ssm_norm_w": "L0_attn_ssm_norm_w", "out_proj_w": "L0_attn_out_proj_w",
        "scorer_w1": "L0_attn_scorer_w1", "scorer_w2": "L0_attn_scorer_w2",
        "summ_w": "L0_attn_summ_w", "q_w": "L0_attn_q_w", "k_w": "L0_attn_k_w",
        "v_w": "L0_attn_v_w", "gate_w": "L0_attn_gate_w",
        "ffn_gate_w": "L0_ffn_gate_w", "ffn_gate_b": "L0_ffn_gate_b",
        "shared_wg": "L0_ffn_shared_wg", "shared_wu": "L0_ffn_shared_wu",
        "shared_wd": "L0_ffn_shared_wd", "experts_wg": "L0_ffn_experts_wg",
        "experts_wu": "L0_ffn_experts_wu", "experts_wd": "L0_ffn_experts_wd",
    }
    nw = np.ascontiguousarray(inputs["L0_attn_norm_w"].reshape(8, 128).T)
    sw = np.ascontiguousarray(inputs["L0_attn_ssm_norm_w"].reshape(16, 128).T)
    cw = np.ascontiguousarray(inputs["L0_attn_conv_w"].reshape(18, 128, 4)
                              .transpose(1, 0, 2).reshape(128, 72))
    cb = np.ascontiguousarray(inputs["L0_attn_conv_b"].reshape(18, 128).T)
    dp = np.ascontiguousarray(np.repeat(inputs["L0_attn_D"].reshape(16, 2), 64, axis=1)
                              .reshape(16, 128).T)
    for r in range(NCORES):
        m = {"x": x[r * B_LOC:(r + 1) * B_LOC],
             "normw_t": nw, "ssmw_t": sw, "convw_t": cw, "convb_t": cb, "dp_rep": dp}
        for k, v in ren.items():
            m[k] = np.ascontiguousarray(inputs[v])
        for nm, arr in CONSTS.items():
            m["c_" + nm] = arr
        maps.append(m)
    return maps


def kernel(**inputs):
    dbg = tuple(x for x in DBG.split(",") if x)
    nc, dbg_outs = build_program(dbg)
    res = run_bass_kernel_spmd(nc, _in_maps(inputs), core_ids=list(range(NCORES)))
    out = np.concatenate([r["out"] for r in res.results], axis=0)
    if dbg:
        kernel.dbg = [{k: r["dbg_" + k] for k in dbg} for r in res.results]
    return out


# version 04
# speedup vs baseline: 1.0021x; 1.0021x over previous
import os
import numpy as np
import ml_dtypes
from contextlib import ExitStack

import concourse.bass as bass
import concourse.mybir as mybir
from concourse.bass_utils import run_bass_kernel_spmd

F32 = mybir.dt.float32
BF16 = mybir.dt.bfloat16
AF = mybir.ActivationFunctionType
ALU = mybir.AluOpType
AX = mybir.AxisListType

NCORES = 8
B = 16
T = 2048
D = 1024
B_LOC = B // NCORES
NTOK = B_LOC * T
D_STATE = 128
HEADDIM = 64
NHEADS = 32
D_INNER = 2048
CONV_DIM = 2304
D_IN_PROJ = 4384
D_CONV = 4
LCH = 128
GRP = 256
NGRP = NTOK // GRP
NGRP_EFF = int(os.environ.get("MK_NGRP", "0")) or NTOK // GRP
GPB = NGRP // B_LOC
CHG = GRP // LCH
POOL = 50
SUMD = 64
SCORER_HID = 256
EPS = 1e-5
NEXP = 16
DFF = 512
LN05 = float(np.log(0.5))

DBG = os.environ.get("MK_DBG", "")


def _consts():
    c = {}
    c["eye_bf"] = np.eye(128, dtype=ml_dtypes.bfloat16)
    c["eye_f32"] = np.eye(128, dtype=np.float32)
    j = np.arange(128)[:, None]
    i = np.arange(128)[None, :]
    c["maskc_bf"] = np.where(j > i, -1e30, 0.0).astype(ml_dtypes.bfloat16)
    c["ones2_bf"] = np.ones((2, 128), dtype=ml_dtypes.bfloat16)
    c["ones1_f"] = np.ones((1, 128), dtype=np.float32)
    c["onescol_bf"] = np.ones((128, 1), dtype=ml_dtypes.bfloat16)
    m64 = np.zeros((64, 32 * 128), dtype=ml_dtypes.bfloat16)
    for h in range(32):
        m64[h, h * 128:(h + 1) * 128] = 1.0
        m64[32 + h, h * 128:(h + 1) * 128] = 1.0
    c["mask64_bf"] = m64
    sele = np.zeros((16, 16 * 128), dtype=ml_dtypes.bfloat16)
    for ex in range(16):
        sele[ex, ex * 128:(ex + 1) * 128] = 1.0
    c["sele_bf"] = sele
    return c


CONSTS = _consts()


NDMASEM = 8

class Sched:
    def __init__(self):
        self.ops = []
        self.cnt = {k: 0 for k in ("T", "V", "S", "G", "CC")}
        for i in range(NDMASEM):
            self.cnt[f"DS{i}"] = 0
            self.cnt[f"DG{i}"] = 0
        self.rr = {"DS": 0, "DG": 0}

    def _dma_sem(self, base):
        i = self.rr[base]
        self.rr[base] = (i + 1) % NDMASEM
        return f"{base}{i}"

    def _add(self, eng, emit, deps, inc_sem, inc_amt, name):
        self.cnt[inc_sem] += inc_amt
        tok = (inc_sem, self.cnt[inc_sem])
        deps = tuple(d for d in deps if d is not None)
        self.ops.append((eng, emit, deps, inc_sem, inc_amt, name))
        return tok

    def t(self, emit, deps=(), name=""):
        return self._add("T", emit, deps, "T", 1, name)

    def v(self, emit, deps=(), name=""):
        return self._add("V", emit, deps, "V", 1, name)

    def s(self, emit, deps=(), name=""):
        return self._add("S", emit, deps, "S", 1, name)

    def g(self, emit, deps=(), name=""):
        return self._add("G", emit, deps, "G", 1, name)

    def dma(self, queue, emit, deps=(), name=""):
        sem = self._dma_sem("DS" if queue == "SYNC" else "DG")
        eng = "SYNC" if queue == "SYNC" else "G"
        deps = list(deps)
        if self.cnt[sem]:
            deps.append((sem, self.cnt[sem]))
        return self._add(eng, emit, deps, sem, 16, name)

    def dma_multi(self, queue, emit, n, deps=(), name=""):
        sem = self._dma_sem("DS" if queue == "SYNC" else "DG")
        eng = "SYNC" if queue == "SYNC" else "G"
        deps = list(deps)
        if self.cnt[sem]:
            deps.append((sem, self.cnt[sem]))
        self.cnt[sem] += 16 * n
        tok = (sem, self.cnt[sem])
        deps = tuple(d for d in deps if d is not None)
        self.ops.append((eng, ("MULTI", emit, sem), deps, sem, 0, name))
        return tok

    def cc(self, emit, deps=(), name=""):
        return self._add("G", emit, deps, "CC", 1, name)


_OWN = {"T": "T", "V": "V", "S": "S", "G": "G"}


def emit_engine_stream(sched, eng_name, handle, sems):
    hw = {k: 0 for k in sched.cnt}
    own_sem = _OWN.get(eng_name)
    for (eng, emit, deps, inc_sem, inc_amt, name) in sched.ops:
        if eng != eng_name:
            continue
        for (dsem, dval) in deps:
            if eng_name == "T" and dsem == "T":
                continue
            if hw[dsem] >= dval:
                continue
            handle.wait_ge(sems[dsem], dval)
            hw[dsem] = dval
        if isinstance(emit, tuple) and emit[0] == "MULTI":
            emit[1](handle, sems[emit[2]])
        else:
            last = emit(handle)
            if last is None:
                raise RuntimeError(f"op '{name}' on {eng} emitted nothing")
            last.then_inc(sems[inc_sem], inc_amt)


def build_program(dbg=()):
    nc = bass.Bass("TRN2", target_bir_lowering=False, debug=False)
    sc = Sched()

    XI = nc.dram_tensor("x", [B_LOC, T, D], F32, kind="ExternalInput").ap()
    W = {}
    for nm, shp in [
        ("norm_w", [D]), ("in_proj_w", [D, D_IN_PROJ]), ("conv_w", [CONV_DIM, D_CONV]),
        ("conv_b", [CONV_DIM]), ("dt_bias", [NHEADS]), ("A_log", [NHEADS]), ("Dp", [NHEADS]),
        ("ssm_norm_w", [D_INNER]), ("out_proj_w", [D_INNER, D]),
        ("scorer_w1", [D, SCORER_HID]), ("scorer_w2", [SCORER_HID, 1]),
        ("summ_w", [D, SUMD]), ("q_w", [D, SUMD]), ("k_w", [SUMD, SUMD]), ("v_w", [SUMD, D]),
        ("gate_w", [2 * D, D]),
        ("ffn_gate_w", [D, NEXP]), ("ffn_gate_b", [NEXP]),
        ("shared_wg", [D, DFF]), ("shared_wu", [D, DFF]), ("shared_wd", [DFF, D]),
        ("experts_wg", [NEXP, D, DFF]), ("experts_wu", [NEXP, D, DFF]), ("experts_wd", [NEXP, DFF, D]),
        ("normw_t", [128, 8]), ("ssmw_t", [128, 16]), ("convw_t", [128, 72]),
        ("convb_t", [128, 18]), ("dp_rep", [128, 16]),
    ]:
        W[nm] = nc.dram_tensor(nm, shp, F32, kind="ExternalInput").ap()
    C = {}
    for nm, arr in CONSTS.items():
        dt = BF16 if arr.dtype == ml_dtypes.bfloat16 else F32
        C[nm] = nc.dram_tensor("c_" + nm, list(arr.shape), dt, kind="ExternalInput").ap()

    OUT = nc.dram_tensor("out", [B_LOC, T, D], F32, kind="ExternalOutput").ap()
    YSC = nc.dram_tensor("y_scratch", [8, 128, NTOK], BF16)
    dbg_outs = {}

    def dbg_tensor(name, shp, dt=F32):
        if name in dbg:
            dbg_outs[name] = nc.dram_tensor("dbg_" + name, shp, dt, kind="ExternalOutput").ap()
            return dbg_outs[name]
        return None

    with ExitStack() as ctx:
        e = ctx.enter_context
        sb = lambda name, shp, dt=F32: e(nc.sbuf_tensor(name, shp, dt))
        ps = lambda name, shp: e(nc.psum_tensor(name, shp, F32))
        semnames = ["T", "V", "S", "G", "CC"] + [f"DS{i}" for i in range(8)] + [f"DG{i}" for i in range(8)]
        sems = {k: e(nc.semaphore("sem_" + k)) for k in semnames}

        eye_bf = sb("eye_bf", [128, 128], BF16)
        eye_f = sb("eye_f", [128, 128], F32)
        maskc_bf = sb("maskc_bf", [128, 128], BF16)
        ones2_bf = sb("ones2_bf", [2, 128], BF16)
        ones1_f = sb("ones1_f", [1, 128], F32)
        onescol_bf = sb("onescol_bf", [128, 1], BF16)
        mask64 = sb("mask64", [64, 32 * 128], BF16)

        wip = sb("wip", [128, 8 * D_IN_PROJ], BF16)
        wop = sb("wop", [128, 16 * D], BF16)
        convw = sb("convw", [128, 72], F32)
        convb = sb("convb", [128, 18], F32)
        dtb_col = sb("dtb_col", [32, 1], F32)
        negA_col = sb("negA_col", [32, 1], F32)
        d128 = sb("d128", [128, 16], F32)
        ln05_col = sb("ln05_col", [128, 1], F32)

        tok_const = []

        def ld(queue, dst, src, deps=(), name="ld"):
            return sc.dma(queue, lambda h, dst=dst, src=src: h.dma_start(out=dst, in_=src),
                          deps=deps, name=name)

        tok_const.append(ld("SYNC", eye_bf[:, :], C["eye_bf"], name="c_eye"))
        tok_const.append(ld("SYNC", eye_f[:, :], C["eye_f32"], name="c_eyef"))
        tok_const.append(ld("SYNC", maskc_bf[:, :], C["maskc_bf"], name="c_mask"))
        tok_const.append(ld("SYNC", ones2_bf[:, :], C["ones2_bf"], name="c_ones2"))
        tok_const.append(ld("SYNC", ones1_f[:, :], C["ones1_f"], name="c_ones1"))
        tok_const.append(ld("SYNC", onescol_bf[:, :], C["onescol_bf"], name="c_onesc"))
        tok_const.append(ld("SYNC", mask64[:, :], C["mask64_bf"], name="c_m64"))

        nrm_stage = sb("nrm_stage", [128, 8], F32)
        t_nw = ld("SYNC", nrm_stage[:, :], W["normw_t"], name="normw")
        toks_w = []
        for k in range(8):
            t1 = sc.dma("G", lambda h, k=k: h.dma_start(
                out=wip[:, k * D_IN_PROJ:(k + 1) * D_IN_PROJ],
                in_=W["in_proj_w"][k * 128:(k + 1) * 128, :]), name=f"wip{k}")
            toks_w.append(sc.v(lambda h, k=k: h.tensor_scalar(
                out=wip[:, k * D_IN_PROJ:(k + 1) * D_IN_PROJ],
                in0=wip[:, k * D_IN_PROJ:(k + 1) * D_IN_PROJ],
                scalar1=nrm_stage[:, k:k + 1], scalar2=None, op0=ALU.mult),
                deps=[t_nw, t1], name=f"wipsc{k}"))
        ssm_stage = sb("ssm_stage", [128, 16], F32)
        t_sw = ld("SYNC", ssm_stage[:, :], W["ssmw_t"], name="ssmw")
        for k in range(16):
            t1 = sc.dma("G", lambda h, k=k: h.dma_start(
                out=wop[:, k * D:(k + 1) * D],
                in_=W["out_proj_w"][k * 128:(k + 1) * 128, :]), name=f"wop{k}")
            toks_w.append(sc.v(lambda h, k=k: h.tensor_scalar(
                out=wop[:, k * D:(k + 1) * D], in0=wop[:, k * D:(k + 1) * D],
                scalar1=ssm_stage[:, k:k + 1], scalar2=None, op0=ALU.mult),
                deps=[t_sw, t1], name=f"wopsc{k}"))

        toks_w.append(ld("SYNC", convw[:, :], W["convw_t"], name="convw"))
        toks_w.append(ld("SYNC", convb[:, :], W["convb_t"], name="convb"))

        smallv = sb("smallv", [32, 4], F32)
        t_sv = [ld("SYNC", smallv[:, 0:1], W["dt_bias"].rearrange("(p one) -> p one", one=1), name="dtb"),
                ld("SYNC", smallv[:, 1:2], W["A_log"].rearrange("(p one) -> p one", one=1), name="alog")]
        toks_w.append(sc.v(lambda h: h.tensor_copy(out=dtb_col[:, :], in_=smallv[:, 0:1]), deps=t_sv, name="dtbcp"))
        t_na = sc.s(lambda h: h.activation(out=negA_col[:, :], in_=smallv[:, 1:2], func=AF.Exp), deps=t_sv, name="negAe")
        toks_w.append(sc.v(lambda h: h.tensor_scalar(out=negA_col[:, :], in0=negA_col[:, :],
                                                     scalar1=-1.0, scalar2=None, op0=ALU.mult),
                           deps=[t_na], name="negAn"))
        t_d1 = ld("SYNC", d128[:, :], W["dp_rep"], name="d128")
        toks_w.append(sc.v(lambda h: h.tensor_scalar(out=d128[:, :], in0=d128[:, :],
                                                     scalar1=0.5, scalar2=None, op0=ALU.mult),
                           deps=[t_d1], name="d128h"))
        toks_w.append(sc.v(lambda h: h.memset(ln05_col[:, :], LN05), name="ln05"))
        weights_ready = toks_w + tok_const

        xtok = [sb(f"xtok{i}", [128, D], BF16) for i in range(2)]
        ssq = sb("ssq", [128, 2], F32)
        rs_col = sb("rs_col", [128, 2], F32)
        diag_s = sb("diag_s", [128, 128], BF16)
        sq_dump = sb("sq_dump", [128, D], BF16)
        xnT = sb("xnT", [128, 8 * GRP], BF16)
        zs2 = [sb(f"zs2{i}", [128, 16 * GRP], BF16) for i in range(2)]
        zt_tmp = sb("zt_tmp", [128, GRP], BF16)
        xbcu = [sb(f"xbcu{i}", [128, 18 * (GRP + 3)], BF16) for i in range(2)]
        hsave = [sb(f"hsave{i}", [128, 18 * 3], BF16) for i in range(2)]
        convacc = [sb(f"cva{i}", [128, GRP], F32) for i in range(2)]
        convt_tmp = sb("convt_tmp", [128, GRP], BF16)
        dtg = [sb(f"dtg{i}", [32, GRP], F32) for i in range(2)]
        dt_abs = sb("dt_abs", [32, GRP], F32)

        dAc = sb("dAc", [32, LCH], F32)
        csc = sb("csc", [32, LCH], F32)
        cs32f = sb("cs32f", [32, LCH], F32)
        cshl = sb("cshl", [64, LCH], BF16)
        cshlneg = sb("cshlneg", [64, LCH], BF16)
        csi2 = [sb(f"csi2{i}", [2, 4 * LCH], BF16) for i in range(2)]
        dtT = sb("dtT", [128, 32], F32)
        decc = sb("decc", [32, LCH], F32)
        decT = sb("decT", [128, 32], F32)
        lamrow = sb("lamrow", [1, 32], F32)
        lamb = sb("lamb", [128, 32], F32)
        xdT = sb("xdT", [128, D_INNER], BF16)
        xdec = [sb(f"xdec{i}", [128, HEADDIM], BF16) for i in range(2)]
        BuT = sb("BuT", [128, 128], BF16)
        GT = sb("GT", [128, 128], BF16)
        exp05 = sb("exp05", [128, 4 * LCH], BF16)
        L_sb = sb("L_sb", [128, 4 * LCH], BF16)
        m_tok = sb("m_tok", [128, 4 * LCH], BF16)
        m_aug = sb("m_aug", [128, 4 * LCH], BF16)
        prevst = sb("prevst", [128, D_INNER], BF16)
        t1_sb = sb("t1_sb", [128, LCH], BF16)
        y2sq = sb("y2sq", [128, GRP], BF16)
        y2 = [sb(f"y2{i}", [128, 16 * GRP], BF16) for i in range(2)]
        srow = sb("srow", [1, GRP], F32)
        srow_bf = sb("srow_bf", [1, GRP], BF16)
        sb_bf = sb("sb_bf", [128, GRP], BF16)
        yfm = sb("yfm", [128, 8 * GRP], BF16)


        p_ip_t = ps("p_ip", [128, 1024])
        p_inproj = [p_ip_t[:, 0:GRP], p_ip_t[:, 512:512 + GRP]]
        p_T_t = ps("p_Tt", [128, 8 * LCH])
        p_T = [p_T_t[:, 0:4 * LCH], p_T_t[:, 4 * LCH:8 * LCH]]
        p_b4 = ps("p_b4", [128, 512])
        p_b7 = ps("p_b7", [128, 512])
        p_y = [p_b4[:, 0:128], p_b7[:, 0:128]]
        p_st = [p_b4[:, 128:192], p_b7[:, 128:192]]
        p_trb_t = e(nc.psum_tensor("p_trb", [128, 256], BF16))
        p_trb = p_trb_t[:, 0:128]
        p_b6 = ps("p_b6", [128, 512])
        p_ftr = [p_b6[:, 0:32], p_b6[:, 32:64]]
        p_lam = p_b6[:, 64:96]
        p_gt = p_b6[:, 128:256]
        p_sb = p_b6[:, 256:512]

        d_xnT = dbg_tensor("xnT", [8 * 128, GRP])
        d_dt = dbg_tensor("dt", [32, GRP])
        d_zs2 = dbg_tensor("zs2", [16 * 128, GRP])
        d_xs = dbg_tensor("xs", [18 * 128, GRP])
        d_y2 = dbg_tensor("y2", [16 * 128, GRP])
        d_L = dbg_tensor("Lmat", [128, 8 * LCH])
        d_e05 = dbg_tensor("e05", [128, 8 * LCH])
        d_cs = dbg_tensor("cs", [32, LCH])
        d_Tm = dbg_tensor("Tm", [128, 8 * LCH])
        d_csi = dbg_tensor("csi", [2, 8 * LCH])
        Tm_cap = None
        d_y = dbg_tensor("y", [8 * 128, GRP])

        chains = {k: None for k in ("ip0", "ip1", "T0", "T1", "b4", "b7", "b5", "b6")}

        def chn(key, deps):
            return list(deps) + ([chains[key]] if chains[key] is not None else [])

        xdec_slot_last = [None] * 2
        csi2_slot_last = [[], []]
        lam_last = []
        cva_slot_last = [None] * 2
        prev_zt = None
        prev_ct = None
        prev_xnT_readers = []
        prev_xtok_readers = [[], []]
        prev_xtok_writer = [None, None]
        prev_xs_group = {}
        prev_zs_group = {}
        prev_dt_readers = []
        prev_ssd_shared = []
        prev_upd = [None] * NHEADS
        prev_aug = [None] * NHEADS
        last_maug = None
        last_mtok = None
        y2_reuse = {}
        yfm_reuse = []
        prev_fin_ap = []
        y2_group = {}
        ga_done = []
        ysc_dmas = []

        def finish_group(gp):
            par = gp % 2
            sumsq_tok, y2_toks = y2_group[gp]
            tok_ms = sc.v(lambda h: h.tensor_scalar(
                out=srow[:, :], in0=p_sb[0:1, :], scalar1=1.0 / D_INNER, scalar2=EPS,
                op0=ALU.mult, op1=ALU.add), deps=chn("b6", [sumsq_tok]), name=f"fms{gp}")
            chains["b6"] = tok_ms
            tok_l = sc.s(lambda h: h.activation(out=srow[:, :], in_=srow[:, :], func=AF.Ln),
                         deps=[tok_ms], name=f"fln{gp}")
            tok_r = sc.s(lambda h: h.activation(out=srow[:, :], in_=srow[:, :], func=AF.Exp,
                                                scale=-0.5), deps=[tok_l], name=f"frs{gp}")
            tok_bf = sc.v(lambda h: h.tensor_copy(out=srow_bf[:, :], in_=srow[:, :]),
                          deps=[tok_r], name=f"fbf{gp}")
            tok_bc = sc.t(lambda h: h.matmul(p_sb[:, :], ones2_bf[0:1, :], srow_bf[0:1, :],
                                             start=True, stop=True),
                          deps=chn("b6", [tok_bf]), name=f"fbc{gp}")
            chains["b6"] = tok_bc
            tok_sb = sc.s(lambda h: h.copy(out=sb_bf[:, :], in_=p_sb[:, :]),
                          deps=chn("b6", prev_fin_ap), name=f"fsb{gp}")
            chains["b6"] = tok_sb
            ap_toks = []
            for ct in range(16):
                ap_toks.append(sc.v(lambda h, ct=ct: h.tensor_tensor(
                    out=y2[par][:, ct * GRP:(ct + 1) * GRP], in0=y2[par][:, ct * GRP:(ct + 1) * GRP],
                    in1=sb_bf[:, :], op=ALU.mult), deps=[tok_sb] + (y2_toks if ct == 0 else []),
                    name=f"fap{gp}_{ct}"))
            op_ev = []
            for m in range(8):
                si = m % 2

                def mk_op(h, m=m, par=par, si=si):
                    last = None
                    for k in range(16):
                        last = h.matmul(p_inproj[si][:, :],
                                        wop[:, k * D + m * 128:k * D + (m + 1) * 128],
                                        y2[par][:, k * GRP:(k + 1) * GRP],
                                        start=(k == 0), stop=(k == 15))
                    return last
                tok_mm = sc.t(mk_op, deps=chn(f"ip{si}", ap_toks), name=f"fop{gp}_{m}")
                chains[f"ip{si}"] = tok_mm
                tok_ev = sc.s(lambda h, m=m, si=si: h.copy(
                    out=yfm[:, m * GRP:(m + 1) * GRP], in_=p_inproj[si][:, :]),
                    deps=chn(f"ip{si}", [tok_mm] + list(yfm_reuse)), name=f"fev{gp}_{m}")
                chains[f"ip{si}"] = tok_ev
                op_ev.append(tok_ev)
            gt0 = gp * GRP
            tok_dma = sc.dma("SYNC", lambda h, gt0=gt0: h.dma_start(
                out=YSC.ap()[:, :, gt0:gt0 + GRP].rearrange("k p t -> p k t"),
                in_=yfm[:, :]), deps=op_ev, name=f"fyd{gp}")
            if d_y is not None and gp == 0:
                sc.dma("G", lambda h: h.dma_start(
                    out=d_y.rearrange("(k p) t -> p k t", k=8), in_=yfm[:, :]),
                    deps=op_ev, name="dbg_y")
            y2_reuse[gp % 2] = [op_ev[-1]]
            yfm_reuse[:] = [tok_dma]
            ysc_dmas.append(tok_dma)
            prev_fin_ap[:] = ap_toks
            return [tok_dma] + op_ev

        def emit_front(g):
            nonlocal prev_xnT_readers, prev_zt, prev_ct
            gpar = g % 2
            bloc = g // GPB
            t0 = (g % GPB) * GRP
            first_in_batch = (t0 == 0)

            tok_xnT = []
            for tt in range(2):
                t_x = sc.dma("G", lambda h, bloc=bloc, t0=t0, tt=tt: h.dma_start(
                    out=xtok[tt][:, :],
                    in_=XI[bloc, t0 + tt * 128:t0 + (tt + 1) * 128, :]),
                    deps=(weights_ready if g == 0 else prev_xtok_readers[tt] + [prev_xtok_writer[tt]]),
                    name=f"x{g}_{tt}")
                prev_xtok_writer[tt] = t_x
                tok_sq = sc.s(lambda h, tt=tt: h.activation(
                    out=sq_dump[:, :], in_=xtok[tt][:, :],
                    func=AF.Square, accum_out=ssq[:, tt:tt + 1]),
                    deps=[t_x], name=f"ssq{g}_{tt}")
                tok_m = sc.v(lambda h, tt=tt: h.tensor_scalar(
                    out=rs_col[:, tt:tt + 1], in0=ssq[:, tt:tt + 1],
                    scalar1=1.0 / D, scalar2=EPS, op0=ALU.mult, op1=ALU.add),
                    deps=[tok_sq], name=f"m{g}_{tt}")
                tok_ln = sc.s(lambda h, tt=tt: h.activation(
                    out=rs_col[:, tt:tt + 1], in_=rs_col[:, tt:tt + 1], func=AF.Ln),
                    deps=[tok_m], name=f"ln{g}_{tt}")
                tok_rs = sc.s(lambda h, tt=tt: h.activation(
                    out=rs_col[:, tt:tt + 1], in_=rs_col[:, tt:tt + 1], func=AF.Exp, scale=-0.5),
                    deps=[tok_ln], name=f"rs{g}_{tt}")
                tok_diag = sc.v(lambda h, tt=tt: h.tensor_scalar(
                    out=diag_s[:, :], in0=eye_bf[:, :], scalar1=rs_col[:, tt:tt + 1],
                    scalar2=None, op0=ALU.mult),
                    deps=[tok_rs] + ([t for t in tok_xnT[-8:]] if tt == 1 else prev_xnT_readers),
                    name=f"diag{g}_{tt}")
                readers = []
                for k in range(8):
                    si = k % 2
                    tok_mm = sc.t(lambda h, tt=tt, k=k, si=si: h.matmul(
                        p_inproj[si][:, 0:128], xtok[tt][:, k * 128:(k + 1) * 128],
                        diag_s[:, :], start=True, stop=True),
                        deps=chn(f"ip{si}", [tok_diag]), name=f"xnTmm{g}_{tt}_{k}")
                    chains[f"ip{si}"] = tok_mm
                    tok_ev = sc.s(lambda h, tt=tt, k=k, si=si: h.copy(
                        out=xnT[:, k * GRP + tt * 128:k * GRP + tt * 128 + 128],
                        in_=p_inproj[si][:, 0:128]), deps=[tok_mm], name=f"xnTev{g}_{tt}_{k}")
                    chains[f"ip{si}"] = tok_ev
                    readers.append(tok_mm)
                    tok_xnT.append(tok_ev)
                prev_xtok_readers[tt] = readers
            prev_xnT_readers = []
            if d_xnT is not None and g == 0:
                sc.dma("G", lambda h: h.dma_start(
                    out=d_xnT.rearrange("(k p) t -> p k t", k=8), in_=xnT[:, :]),
                    deps=tok_xnT, name="dbg_xnT")

            mlist = [34] + list(range(34))
            evac_toks = {}
            for mi, m in enumerate(mlist):
                si = mi % 2
                mw = 32 if m == 34 else 128

                def mk_ip(h, m=m, si=si, mw=mw):
                    last = None
                    for k in range(8):
                        last = h.matmul(p_inproj[si][0:mw, :],
                                        wip[:, k * D_IN_PROJ + m * 128:k * D_IN_PROJ + m * 128 + mw],
                                        xnT[:, k * GRP:(k + 1) * GRP],
                                        start=(k == 0), stop=(k == 7))
                    return last
                tok_mm = sc.t(mk_ip, deps=chn(f"ip{si}", tok_xnT), name=f"ip{g}_{m}")
                chains[f"ip{si}"] = tok_mm
                prev_xnT_readers.append(tok_mm)
                if m == 34:
                    tok_a1 = sc.v(lambda h, si=si, gpar=gpar: h.tensor_scalar(
                        out=dtg[gpar][:, :], in0=p_inproj[si][0:32, :], scalar1=dtb_col[:, :],
                        scalar2=None, op0=ALU.add), deps=chn(f"ip{si}", [tok_mm] + prev_dt_readers),
                        name=f"dta1_{g}")
                    chains[f"ip{si}"] = tok_a1
                    tok_aa = sc.s(lambda h, gpar=gpar: h.activation(
                        out=dt_abs[:, :], in_=dtg[gpar][:, :], func=AF.Abs),
                        deps=[tok_a1], name=f"dtab_{g}")
                    tok_e = sc.s(lambda h: h.activation(out=dt_abs[:, :], in_=dt_abs[:, :],
                                                        func=AF.Exp, scale=-1.0),
                                 deps=[tok_aa], name=f"dte_{g}")
                    tok_l = sc.s(lambda h: h.activation(out=dt_abs[:, :], in_=dt_abs[:, :],
                                                        func=AF.Ln, bias=1.0),
                                 deps=[tok_e], name=f"dtl_{g}")
                    tok_r = sc.s(lambda h, gpar=gpar: h.activation(
                        out=dtg[gpar][:, :], in_=dtg[gpar][:, :], func=AF.Relu),
                        deps=[tok_l], name=f"dtr_{g}")
                    tok_dt = sc.v(lambda h, gpar=gpar: h.tensor_add(
                        out=dtg[gpar][:, :], in0=dtg[gpar][:, :], in1=dt_abs[:, :]),
                                  deps=[tok_r], name=f"dt_{g}")
                    evac_toks["dt"] = tok_dt
                    if d_dt is not None and g == 0:
                        evac_toks["dbg_dt"] = sc.dma("SYNC", lambda h, gpar=gpar: h.dma_start(
                            out=d_dt, in_=dtg[gpar][:, :]), deps=[tok_dt], name="dbg_dt")
                    if g >= 2:
                        finish_group(g - 2)
                elif m < 16:
                    tok_t = sc.s(lambda h, si=si: h.activation(
                        out=zt_tmp[:, :], in_=p_inproj[si][:, :], func=AF.Tanh, scale=0.5),
                        deps=chn(f"ip{si}", [tok_mm, prev_zt]), name=f"zt{g}_{m}")
                    tok_z = sc.v(lambda h, m=m, gpar=gpar, si=si: h.scalar_tensor_tensor(
                        out=zs2[gpar][:, m * GRP:(m + 1) * GRP], in0=zt_tmp[:, :], scalar=1.0,
                        in1=p_inproj[si][:, :], op0=ALU.add, op1=ALU.mult),
                        deps=[tok_t] + (prev_zs_group.get(gpar, []) if m == 0 else []),
                        name=f"z{g}_{m}")
                    chains[f"ip{si}"] = tok_z
                    prev_zt = tok_z
                    evac_toks[m] = tok_z
                else:
                    ct = m - 16
                    tok_ev = sc.s(lambda h, ct=ct, gpar=gpar, si=si: h.copy(
                        out=xbcu[gpar][:, ct * (GRP + 3) + 3:ct * (GRP + 3) + 3 + GRP],
                        in_=p_inproj[si][:, :]),
                        deps=chn(f"ip{si}", [tok_mm] + (list(prev_xs_group.get(gpar, ())) if ct == 0 else [])),
                        name=f"xbc{g}_{ct}")
                    chains[f"ip{si}"] = tok_ev
                    evac_toks[m] = tok_ev

            xbc_toks = [evac_toks[m] for m in range(16, 34)]

            if first_in_batch:
                def mk_halo(h, gpar=gpar):
                    last = None
                    for ct in range(18):
                        last = h.memset(xbcu[gpar][:, ct * (GRP + 3):ct * (GRP + 3) + 3], 0.0)
                    return last
                tok_halo = sc.v(mk_halo, deps=list(prev_xs_group.get(gpar, ())), name=f"halo0_{g}")
            else:
                pbi = 1 - gpar

                def mk_halo(h, gpar=gpar, pbi=pbi):
                    last = None
                    for ct in range(18):
                        last = h.tensor_copy(
                            out=xbcu[gpar][:, ct * (GRP + 3):ct * (GRP + 3) + 3],
                            in_=hsave[pbi][:, ct * 3:ct * 3 + 3])
                    return last
                tok_halo = sc.v(mk_halo, deps=list(prev_xs_group.get(gpar, ())) + xbc_toks,
                                name=f"halo_{g}")

            def mk_hsave(h, gpar=gpar):
                last = None
                for ct in range(18):
                    last = h.tensor_copy(
                        out=hsave[gpar][:, ct * 3:ct * 3 + 3],
                        in_=xbcu[gpar][:, ct * (GRP + 3) + GRP:ct * (GRP + 3) + GRP + 3])
                return last
            tok_hsave = sc.v(mk_hsave, deps=xbc_toks, name=f"hsave_{g}")

            tok_xs_all = []
            for ct in range(18):
                si = ct % 2
                base = ct * (GRP + 3)
                tok_ck = sc.v(lambda h, ct=ct, gpar=gpar, si=si, base=base: h.tensor_scalar(
                    out=convacc[si][:, :], in0=xbcu[gpar][:, base + 0:base + 0 + GRP],
                    scalar1=convw[:, ct * 4:ct * 4 + 1], scalar2=convb[:, ct:ct + 1],
                    op0=ALU.mult, op1=ALU.add),
                    deps=[evac_toks[16 + ct], tok_halo, tok_hsave, cva_slot_last[si]],
                    name=f"cv0_{g}_{ct}")
                for k in range(1, 4):
                    tok_ck = sc.v(lambda h, ct=ct, gpar=gpar, si=si, base=base, k=k: h.scalar_tensor_tensor(
                        out=convacc[si][:, :], in0=xbcu[gpar][:, base + k:base + k + GRP],
                        scalar=convw[:, ct * 4 + k:ct * 4 + k + 1], in1=convacc[si][:, :],
                        op0=ALU.mult, op1=ALU.add), deps=[tok_ck], name=f"cv{k}_{g}_{ct}")
                tok_ct2 = sc.s(lambda h, si=si: h.activation(
                    out=convt_tmp[:, :], in_=convacc[si][:, :], func=AF.Tanh, scale=0.5),
                    deps=[tok_ck, prev_ct], name=f"cvt{g}_{ct}")
                tok_xs = sc.v(lambda h, ct=ct, gpar=gpar, si=si, base=base: h.scalar_tensor_tensor(
                    out=xbcu[gpar][:, base + 3:base + 3 + GRP],
                    in0=convt_tmp[:, :], scalar=1.0, in1=convacc[si][:, :],
                    op0=ALU.add, op1=ALU.mult),
                    deps=[tok_ct2], name=f"cxs{g}_{ct}")
                cva_slot_last[si] = tok_xs
                prev_ct = tok_xs
                tok_xs_all.append(tok_xs)

            xs_group_readers = []
            zs_group_readers = []
            if d_zs2 is not None and g == 0:
                zs_group_readers.append(sc.dma("G", lambda h: h.dma_start(
                    out=d_zs2.rearrange("(ct p) t -> p ct t", ct=16), in_=zs2[0][:, :]),
                    deps=[evac_toks[m] for m in range(16)], name="dbg_zs2"))
            if d_xs is not None and g == 0:
                def mk_dbg_xs(h, sem):
                    for ct in range(18):
                        h.dma_start(out=d_xs[ct * 128:(ct + 1) * 128, :],
                                    in_=xbcu[0][:, ct * (GRP + 3) + 3:ct * (GRP + 3) + 3 + GRP]
                                    ).then_inc(sem, 16)
                xs_group_readers.append(sc.dma_multi("G", mk_dbg_xs, 18, deps=tok_xs_all, name="dbg_xs"))

            return (evac_toks, tok_xs_all, xs_group_readers, zs_group_readers)

        def emit_ssd(g, fs):
            nonlocal prev_upd, prev_aug, prev_ssd_shared, lam_last, \
                last_maug, last_mtok, prev_dt_readers
            (evac_toks, tok_xs_all, xs_group_readers, zs_group_readers) = fs
            gpar = g % 2
            first_in_batch = ((g % GPB) == 0)
            dt_readers = []
            y2_toks = []
            sumsq_tok = None
            if first_in_batch:
                tok_pz = sc.v(lambda h: h.memset(prevst[:, :], 0.0),
                              deps=[t for t in prev_aug if t is not None] +
                                   [t for t in prev_upd if t is not None], name=f"pz{g}")
                prev_upd = [tok_pz] * NHEADS
                prev_aug = [None] * NHEADS

            for c in range(CHG):
                cbase = c * LCH
                shared_deps = list(prev_ssd_shared)
                tok_da = sc.v(lambda h, cbase=cbase, gpar=gpar: h.tensor_scalar(
                    out=dAc[:, :], in0=dtg[gpar][:, cbase:cbase + LCH], scalar1=negA_col[:, :],
                    scalar2=None, op0=ALU.mult), deps=[evac_toks["dt"]] + shared_deps,
                    name=f"da{g}_{c}")
                dt_readers.append(tok_da)
                tok_cs = sc.v(lambda h: h.tensor_tensor_scan(
                    out=csc[:, :], data0=dAc[:, :], data1=dAc[:, :], initial=0.0,
                    op0=ALU.add, op1=ALU.bypass), deps=[tok_da], name=f"cs{g}_{c}")
                if d_cs is not None and g == 0 and c == 0:
                    sc.dma("SYNC", lambda h: h.dma_start(out=d_cs, in_=csc[:, :]),
                           deps=[tok_cs], name="dbg_cs")
                tok_h1 = sc.v(lambda h: h.tensor_copy(out=cshl[0:32, :], in_=csc[:, :]),
                              deps=[tok_cs], name=f"hl1{g}_{c}")
                tok_h2 = sc.v(lambda h: h.tensor_copy(out=cs32f[:, :], in_=cshl[0:32, :]),
                              deps=[tok_h1], name=f"hl2{g}_{c}")
                tok_h3 = sc.v(lambda h: h.tensor_sub(out=cshl[32:64, :], in0=csc[:, :],
                                                     in1=cs32f[:, :]), deps=[tok_h2], name=f"hl3{g}_{c}")
                tok_n1 = sc.v(lambda h: h.tensor_scalar(
                    out=cshlneg[:, :], in0=cshl[:, :], scalar1=-1.0, scalar2=None, op0=ALU.mult),
                    deps=[tok_h3], name=f"neg{g}_{c}")

                tok_dtt_m = sc.t(lambda h, cbase=cbase, gpar=gpar: h.transpose(
                    p_ftr[0][:, :], dtg[gpar][0:32, cbase:cbase + LCH], eye_f[0:32, 0:32]),
                    deps=chn("b6", [evac_toks["dt"]]), name=f"dttm{g}_{c}")
                chains["b6"] = tok_dtt_m
                dt_readers.append(tok_dtt_m)
                tok_dtt = sc.s(lambda h: h.activation(out=dtT[:, :], in_=p_ftr[0][:, :],
                                                      func=AF.Copy, scale=0.5),
                               deps=chn("b6", [tok_dtt_m] + shared_deps), name=f"dtt{g}_{c}")
                chains["b6"] = tok_dtt
                tok_d1 = sc.v(lambda h: h.tensor_scalar(
                    out=decc[:, :], in0=csc[:, :], scalar1=csc[:, LCH - 1:LCH], scalar2=None,
                    op0=ALU.subtract), deps=[tok_cs], name=f"dc1{g}_{c}")
                tok_d2 = sc.s(lambda h: h.activation(out=decc[:, :], in_=decc[:, :],
                                                     func=AF.Exp, scale=-1.0, bias=ln05_col[0:32, :]),
                              deps=[tok_d1], name=f"dc2{g}_{c}")
                tok_d3 = sc.t(lambda h: h.transpose(p_ftr[1][:, :], decc[:, :], eye_f[0:32, 0:32]),
                              deps=chn("b6", [tok_d2]), name=f"dc3{g}_{c}")
                chains["b6"] = tok_d3
                tok_d4 = sc.s(lambda h: h.copy(out=decT[:, :], in_=p_ftr[1][:, :]),
                              deps=chn("b6", [tok_d3] + shared_deps), name=f"dc4{g}_{c}")
                chains["b6"] = tok_d4
                tok_lr = sc.dma("SYNC", lambda h: h.dma_start(
                    out=lamrow[:, :], in_=csc[:, LCH - 1:LCH]),
                    deps=[tok_cs] + lam_last, name=f"lr{g}_{c}")
                tok_le = sc.s(lambda h: h.activation(out=lamrow[:, :], in_=lamrow[:, :],
                                                     func=AF.Exp), deps=[tok_lr], name=f"le{g}_{c}")
                lam_last = [tok_lr, tok_le, tok_lb_m] if False else [tok_lr, tok_le]
                tok_lb_m = sc.t(lambda h: h.matmul(p_lam[:, :], ones1_f[:, :], lamrow[:, :],
                                                   start=True, stop=True),
                                deps=chn("b6", [tok_le]), name=f"lbm{g}_{c}")
                chains["b6"] = tok_lb_m
                tok_lb = sc.s(lambda h: h.copy(out=lamb[:, :], in_=p_lam[:, :]),
                              deps=chn("b6", [tok_lb_m] + shared_deps), name=f"lb{g}_{c}")
                chains["b6"] = tok_lb

                xdt_toks = []
                for ct in range(16):
                    xin = ct * (GRP + 3) + 3 + cbase
                    tok_tm = sc.t(lambda h, gpar=gpar, xin=xin: h.transpose(
                        p_trb[:, :], xbcu[gpar][:, xin:xin + LCH], eye_bf[:, :]),
                        deps=chn("b5", [tok_xs_all[ct]]), name=f"xdm{g}_{c}_{ct}")
                    chains["b5"] = tok_tm
                    xs_group_readers.append(tok_tm)

                    def mk_xde(h, ct=ct):
                        h.activation(out=xdT[:, ct * 128:ct * 128 + 64], in_=p_trb[:, 0:64],
                                     func=AF.Copy, scale=dtT[:, 2 * ct:2 * ct + 1])
                        return h.activation(out=xdT[:, ct * 128 + 64:ct * 128 + 128],
                                            in_=p_trb[:, 64:128],
                                            func=AF.Copy, scale=dtT[:, 2 * ct + 1:2 * ct + 2])
                    tok_te = sc.s(mk_xde, deps=chn("b5", [tok_tm, tok_dtt] + shared_deps),
                                  name=f"xde{g}_{c}_{ct}")
                    chains["b5"] = tok_te
                    xdt_toks.append(tok_te)
                bu_off = 16 * (GRP + 3) + 3 + cbase
                cu_off = 17 * (GRP + 3) + 3 + cbase
                tok_bm = sc.t(lambda h, gpar=gpar, bu_off=bu_off: h.transpose(
                    p_trb[:, :], xbcu[gpar][:, bu_off:bu_off + LCH], eye_bf[:, :]),
                    deps=chn("b5", [tok_xs_all[16]]), name=f"bum{g}_{c}")
                chains["b5"] = tok_bm
                xs_group_readers.append(tok_bm)
                tok_bu = sc.s(lambda h: h.copy(out=BuT[:, :], in_=p_trb[:, :]),
                              deps=chn("b5", [tok_bm] + shared_deps), name=f"bue{g}_{c}")
                chains["b5"] = tok_bu
                tok_gm = sc.t(lambda h, gpar=gpar, bu_off=bu_off, cu_off=cu_off: h.matmul(
                    p_gt[:, :], xbcu[gpar][:, bu_off:bu_off + LCH],
                    xbcu[gpar][:, cu_off:cu_off + LCH], start=True, stop=True),
                    deps=chn("b6", [tok_xs_all[16], tok_xs_all[17]]), name=f"gtm{g}_{c}")
                chains["b6"] = tok_gm
                xs_group_readers.append(tok_gm)
                tok_gt = sc.s(lambda h: h.activation(out=GT[:, :], in_=p_gt[:, :],
                                                     func=AF.Copy, scale=0.25),
                              deps=chn("b6", [tok_gm] + shared_deps), name=f"gte{g}_{c}")
                chains["b6"] = tok_gt

                new_shared = [tok_dtt, tok_d4, tok_lb]
                y_pair_mm = {}
                for bb in range(8):
                    csl = bb % 2
                    sl = bb % 2
                    Tk = f"T{sl}"
                    dep_m_reuse = [last_maug, last_mtok]

                    def mk_csi(h, sem, bb=bb, csl=csl):
                        h.dma_start(out=csi2[csl][0:1, :],
                                    in_=cshl[bb * 4:bb * 4 + 4, :]).then_inc(sem, 16)
                        h.dma_start(out=csi2[csl][1:2, :],
                                    in_=cshl[32 + bb * 4:32 + bb * 4 + 4, :]).then_inc(sem, 16)
                    tok_csi = sc.dma_multi("SYNC", mk_csi, 2,
                                           deps=[tok_h3] + csi2_slot_last[csl],
                                           name=f"csi{g}_{c}_{bb}")
                    csi2_slot_last[csl] = [tok_csi]

                    tok_rm = sc.t(lambda h, sl=sl, csl=csl: h.matmul(
                        p_T[sl][:, :], ones2_bf[:, :], csi2[csl][:, :], start=True, stop=True),
                        deps=chn(Tk, [tok_csi]), name=f"rm_{g}_{c}_{bb}")
                    chains[Tk] = tok_rm
                    tok_e05 = sc.s(lambda h, sl=sl: h.activation(
                        out=exp05[:, :], in_=p_T[sl][:, :], func=AF.Exp, bias=ln05_col[:, :]),
                        deps=chn(Tk, [tok_rm, last_maug]), name=f"e05_{g}_{c}_{bb}")
                    chains[Tk] = tok_e05
                    tok_maug = sc.v(lambda h, gpar=gpar, cu_off=cu_off: h.tensor_tensor(
                        out=m_aug[:, :], in0=exp05[:, :],
                        in1=xbcu[gpar][:, cu_off:cu_off + LCH]
                        .rearrange("p (o f) -> p o f", o=1).to_broadcast([128, 4, LCH]),
                        op=ALU.mult), deps=[tok_e05] + dep_m_reuse, name=f"maug_{g}_{c}_{bb}")
                    xs_group_readers.append(tok_maug)
                    last_maug = tok_maug

                    def mk_T(h, sl=sl, csl=csl, bb=bb):
                        h.matmul(p_T[sl][:, :], ones2_bf[:, :], csi2[csl][:, :],
                                 start=True, stop=False)
                        h.matmul(p_T[sl][:, :], cshlneg[:, :],
                                 mask64[:, bb * 512:(bb + 1) * 512], start=False, stop=False)
                        last = None
                        for v in range(4):
                            last = h.matmul(p_T[sl][:, v * 128:(v + 1) * 128], eye_bf[:, :],
                                            maskc_bf[:, :], start=False, stop=(v == 3))
                        return last
                    tok_mmT = sc.t(mk_T, deps=chn(Tk, [tok_e05, tok_n1, tok_csi]),
                                   name=f"mmT_{g}_{c}_{bb}")
                    chains[Tk] = tok_mmT
                    csi2_slot_last[csl] = csi2_slot_last[csl] + [tok_mmT]
                    tok_L = sc.s(lambda h, sl=sl: h.activation(
                        out=L_sb[:, :], in_=p_T[sl][:, :], func=AF.Exp),
                        deps=chn(Tk, [tok_mmT, last_mtok]), name=f"L_{g}_{c}_{bb}")
                    chains[Tk] = tok_L
                    tok_mtok = sc.v(lambda h: h.tensor_tensor(
                        out=m_tok[:, :], in0=L_sb[:, :],
                        in1=GT[:, :].rearrange("p (o f) -> p o f", o=1).to_broadcast([128, 4, LCH]),
                        op=ALU.mult), deps=[tok_L, tok_gt] + dep_m_reuse, name=f"mtok_{g}_{c}_{bb}")
                    new_shared.append(tok_mtok)
                    last_mtok = tok_mtok

                    for v in range(4):
                        hh = bb * 4 + v
                        pi = (hh // 2) % 2
                        bk = "b4" if pi == 0 else "b7"
                        off = 64 * (hh % 2)
                        tp = (0, 64) if off else None

                        def mk_yd(h, hh=hh, v=v, pi=pi, off=off, tp=tp):
                            h.matmul(p_y[pi][off:off + 64, :], xdT[:, hh * 64:(hh + 1) * 64],
                                     m_tok[:, v * 128:(v + 1) * 128], start=True, stop=False,
                                     tile_position=tp)
                            return h.matmul(p_y[pi][off:off + 64, :], prevst[:, hh * 64:(hh + 1) * 64],
                                            m_aug[:, v * 128:(v + 1) * 128], start=False, stop=True,
                                            tile_position=tp)
                        tok_yd = sc.t(mk_yd, deps=chn(bk, [tok_mtok, tok_maug, xdt_toks[hh // 2],
                                                           prev_upd[hh]]),
                                      name=f"yd{g}_{c}_{hh}")
                        chains[bk] = tok_yd
                        prev_aug[hh] = tok_yd
                        new_shared.append(tok_yd)
                        s2 = pi
                        tok_xd = sc.v(lambda h, hh=hh, s2=s2: h.tensor_scalar(
                            out=xdec[s2][:, :], in0=xdT[:, hh * 64:(hh + 1) * 64],
                            scalar1=decT[:, hh:hh + 1], scalar2=None, op0=ALU.mult),
                            deps=[xdt_toks[hh // 2], tok_d4, xdec_slot_last[s2]],
                            name=f"xdc{g}_{c}_{hh}")
                        new_shared.append(tok_xd)
                        tok_st = sc.t(lambda h, s2=s2: h.matmul(
                            p_st[s2][:, :], BuT[:, :], xdec[s2][:, :], start=True, stop=True),
                            deps=chn(bk, [tok_xd, tok_bu]), name=f"st{g}_{c}_{hh}")
                        chains[bk] = tok_st
                        xdec_slot_last[s2] = tok_st
                        new_shared.append(tok_st)
                        tok_up = sc.v(lambda h, hh=hh, s2=s2: h.scalar_tensor_tensor(
                            out=prevst[:, hh * 64:(hh + 1) * 64], in0=prevst[:, hh * 64:(hh + 1) * 64],
                            scalar=lamb[:, hh:hh + 1], in1=p_st[s2][:, :],
                            op0=ALU.mult, op1=ALU.add),
                            deps=chn(bk, [tok_st, tok_yd, tok_lb]), name=f"up{g}_{c}_{hh}")
                        chains[bk] = tok_up
                        prev_upd[hh] = tok_up
                        new_shared.append(tok_up)

                        if off == 64:
                            ct2 = hh // 2
                            xin = ct2 * (GRP + 3) + 3 + cbase
                            tok_t1 = sc.v(lambda h, ct2=ct2, gpar=gpar, xin=xin, pi=pi: h.scalar_tensor_tensor(
                                out=t1_sb[:, :], in0=xbcu[gpar][:, xin:xin + LCH],
                                scalar=d128[:, ct2:ct2 + 1], in1=p_y[pi][:, :],
                                op0=ALU.mult, op1=ALU.add),
                                deps=chn(bk, [tok_yd, y_pair_mm.get(hh - 1)]), name=f"t1_{g}_{c}_{ct2}")
                            chains[bk] = tok_t1
                            xs_group_readers.append(tok_t1)
                            tok_y2 = sc.v(lambda h, ct2=ct2, gpar=gpar, cbase=cbase: h.tensor_tensor(
                                out=y2[gpar][:, ct2 * GRP + cbase:ct2 * GRP + cbase + LCH],
                                in0=t1_sb[:, :],
                                in1=zs2[gpar][:, ct2 * GRP + cbase:ct2 * GRP + cbase + LCH],
                                op=ALU.mult), deps=[tok_t1] + y2_reuse.get(gpar, []),
                                name=f"y2_{g}_{c}_{ct2}")
                            zs_group_readers.append(tok_y2)
                            y2_toks.append(tok_y2)
                        else:
                            y_pair_mm[hh] = tok_yd
                prev_ssd_shared = new_shared

            for ct in range(16):
                tok_sq = sc.v(lambda h, ct=ct, gpar=gpar: h.tensor_tensor(
                    out=y2sq[:, :], in0=y2[gpar][:, ct * GRP:(ct + 1) * GRP],
                    in1=y2[gpar][:, ct * GRP:(ct + 1) * GRP], op=ALU.mult),
                    deps=([sumsq_tok] if ct > 0 else y2_toks), name=f"ysq{g}_{ct}")
                sumsq_tok = sc.t(lambda h, ct=ct: h.matmul(
                    p_sb[0:1, :], onescol_bf[:, :], y2sq[:, :],
                    start=(ct == 0), stop=(ct == 15)),
                    deps=chn("b6", [tok_sq]), name=f"ymm{g}_{ct}")
                chains["b6"] = sumsq_tok
            y2_group[g] = (sumsq_tok, y2_toks)

            if d_y2 is not None and g == 0:
                y2_toks.append(sc.dma("G", lambda h: h.dma_start(
                    out=d_y2.rearrange("(ct p) t -> p ct t", ct=16), in_=y2[0][:, :]),
                    deps=list(y2_toks), name="dbg_y2"))

            prev_xs_group[gpar] = tuple(xs_group_readers)
            prev_zs_group[gpar] = zs_group_readers
            prev_dt_readers = dt_readers + ([evac_toks["dbg_dt"]] if "dbg_dt" in evac_toks else [])
            ga_done.append(tok_xs_all[-1])

        fstash = {0: emit_front(0)}
        for g in range(NGRP_EFF):
            if g + 1 < NGRP_EFF:
                fstash[g + 1] = emit_front(g + 1)
            emit_ssd(g, fstash.pop(g))
        if NGRP_EFF >= 2:
            finish_group(NGRP_EFF - 2)
        fin_last = finish_group(NGRP_EFF - 1)

        RB = nc.lookup_mloc(wip).addr

        def at(name, shp, dt, off):
            return nc.alloc_sbuf_tensor_at(name, shp, dt, offset=RB + off)

        def mk_carver(base):
            off = [base]
            def carve(name, shp, dt):
                t = at(name, shp, dt, off[0])
                esz = 2 if dt == BF16 else 4
                off[0] += (shp[1] * esz + 31) // 32 * 32
                return t
            return carve, off

        carveA, oa = mk_carver(0)
        wsc1 = carveA("wsc1", [128, 8 * 256], BF16)
        wsumm = carveA("wsumm", [128, 8 * 64], BF16)
        wkv = carveA("wkv", [64, 64 + 1024], BF16)
        wq = carveA("wq", [128, 8 * 64], BF16)
        scst = carveA("scst", [32, 256], F32)
        scrow = carveA("scrow", [1, 256], F32)
        summ_sb = carveA("summ_sb", [128, 32 * 64], BF16)
        ybuf = carveA("ybuf", [128, 8 * 256], BF16)
        xbuf = carveA("xbuf", [128, 2 * 1024], F32)
        h1b = carveA("h1b", [128, 2 * 256], BF16)
        a3b = carveA("a3b", [128, 8 * 256], BF16)
        q_bf = carveA("q_bf", [64, 256], BF16)
        pT_bf = carveA("pT_bf", [64, 256], BF16)
        prow = carveA("prow", [128, 64], BF16)
        lgb = carveA("lgb", [128, 64], F32)
        gate_bf = carveA("gate_bf", [128, 256], BF16)
        t2b = carveA("t2b", [128, 256], F32)
        dump_b = carveA("dump_b", [128, 1024], BF16)
        sglob = carveA("sglob", [128, 256], F32)
        svm = carveA("svm", [128, 256], F32)
        cascA = carveA("cascA", [128, 896], F32)
        cascB = carveA("cascB", [128, 896], F32)
        cmask = carveA("cmask", [128, 896], F32)
        cands = carveA("cands", [128, 56], F32)
        vvals = carveA("vvals", [1, 56], F32)
        vb50 = carveA("vb50", [128, 50], F32)
        scloc = carveA("scloc", [128, 32], F32)
        sccols = carveA("sccols", [128, 12], F32)
        onescol_f = carveA("onescol_f", [128, 1], F32)
        cond05 = carveA("cond05", [128, 1], F32)
        pool_f = carveA("pool_f", [64, 64], F32)
        poolb = carveA("poolb", [64, 64], BF16)
        poolT_bf = carveA("poolT_bf", [64, 64], BF16)
        kk_bf = carveA("kk_bf", [64, 64], BF16)
        vv_bf = carveA("vv_bf", [64, 1024], BF16)
        vvc_bf = carveA("vvc_bf", [64, 1024], BF16)
        VG_bf = carveA("VG_bf", [64, 1024], BF16)
        vvT = carveA("vvT", [128, 8 * 64], BF16)
        maskb = carveA("maskb", [128, 64], F32)
        Sx = carveA("Sx", [128, 50], BF16)
        assert oa[0] <= 65536, f"regA overflow {oa[0]}"

        x1fm = at("x1fm", [128, 8 * 4096], BF16, 0)
        wg1 = at("wg1", [128, 8 * 1024], BF16, 65536)
        wg2 = at("wg2", [128, 8 * 1024], BF16, 81920)
        wslot = []
        for i in range(3):
            wb = 65536 + i * 24576
            wslot.append({"wg": at(f"cwg{i}", [128, 8 * 512], BF16, wb),
                          "wu": at(f"cwu{i}", [128, 8 * 512], BF16, wb + 8192),
                          "wd": at(f"cwd{i}", [128, 4 * 1024], BF16, wb + 16384)})

        carveP, op_ = mk_carver(139264)
        mucol = carveP("mucol", [128, 32], F32)
        sqcol = carveP("sqcol", [128, 32], F32)
        sscol = carveP("sscol", [128, 32], F32)
        diag2 = carveP("diag2", [128, 128], BF16)
        v2l4 = carveP("v2l4", [128, 4 * 1024], BF16)
        gfm = at("gfm", [16, 32 * 128], BF16, 139904)
        selec = carveP("selec", [16, 16 * 128], BF16)
        gtok = carveP("gtok", [128, 32 * 16], BF16)
        wrt = carveP("wrt", [128, 8 * 16], BF16)
        thb = carveP("thb", [16, 128], BF16)
        gb_col = carveP("gb_col", [16, 1], F32)
        rcols = carveP("rcols", [128, 8], F32)
        cga = carveP("cga", [128, 16], BF16)
        ct2 = carveP("ct2", [128, 16], F32)
        clg = carveP("clg", [128, 16], F32)
        gbc = carveP("gbc", [128, 2 * 256], BF16)
        hsil = carveP("hsil", [128, 2 * 4 * 256], BF16)
        hcmb = carveP("hcmb", [128, 2 * 4 * 256], BF16)
        v3tok2 = at("v3tok2", [128, 2 * 1024], BF16, op_[0] - 8192)
        v3fm = at("v3fm", [128, 8 * 256], BF16, op_[0] - 4096)
        accbuf = [carveP(f"accbuf{i}", [128, 8 * 256], BF16) for i in range(2)]
        outst2 = [carveP(f"outst{i}", [128, 1024], F32) for i in range(2)]
        v2loc = [carveP(f"v2loc{i}", [128, 2 * 1024], BF16) for i in range(2)]
        fmu = carveP("fmu", [128, 32], F32)
        fsq = carveP("fsq", [128, 32], F32)
        fs2 = carveP("fs2", [128, 32], F32)
        assert RB + op_[0] <= 222592, f"regP overflow {RB + op_[0]}"

        V2H = nc.dram_tensor("v2_scratch", [128, 32 * 1024], BF16)
        V3H = nc.dram_tensor("v3_scratch", [16, 128, 2048], BF16)
        ACCH = nc.dram_tensor("acc_scratch", [16, 128, 2048], BF16)
        sc_in = nc.dram_tensor("sc_in", [1, NTOK], F32)
        sc_out = nc.dram_tensor("sc_out", [NCORES, NTOK], F32)
        pool_in = nc.dram_tensor("pool_in", [64, 64], F32)
        pool_out = nc.dram_tensor("pool_out", [64, 64], F32)

        a_end = list(fin_last) + [t for t in prev_upd if t is not None] + \
            list(prev_ssd_shared) + list(prev_xs_group.get(0, ())) + list(prev_xs_group.get(1, ())) + \
            (prev_zs_group.get(0) or []) + (prev_zs_group.get(1) or []) + \
            [chains[k] for k in chains if chains[k] is not None] + \
            list(prev_fin_ap) + list(prev_dt_readers)

        d_sc = dbg_tensor("sc", [32, 256])
        d_pool = dbg_tensor("pool", [64, 64])
        d_x1 = dbg_tensor("x1", [8 * 128, 256])
        d_gw = dbg_tensor("gw", [128, 512])
        d_acc0 = dbg_tensor("acc0", [128, 2048])

        a2w = []
        for k in range(8):
            for (dst, srcw, width) in ((wsc1, "scorer_w1", 256), (wsumm, "summ_w", 64),
                                       (wq, "q_w", 64)):
                a2w.append(sc.dma("G", lambda h, k=k, dst=dst, srcw=srcw, width=width:
                                  h.dma_start(out=dst[:, k * width:(k + 1) * width],
                                              in_=W[srcw][k * 128:(k + 1) * 128, :]),
                                  deps=a_end, name=f"a2w_{srcw}{k}"))
            a2w.append(sc.dma("G", lambda h, k=k: h.dma_start(
                out=wg1[:, k * 1024:(k + 1) * 1024],
                in_=W["gate_w"][k * 128:(k + 1) * 128, :]), deps=a_end, name=f"wg1{k}"))
            a2w.append(sc.dma("G", lambda h, k=k: h.dma_start(
                out=wg2[:, k * 1024:(k + 1) * 1024],
                in_=W["gate_w"][1024 + k * 128:1024 + (k + 1) * 128, :]),
                deps=a_end, name=f"wg2{k}"))
        a2w.append(sc.dma("G", lambda h: h.dma_start(out=wkv[:, 0:64], in_=W["k_w"]),
                          deps=a_end, name="wkw"))
        a2w.append(sc.dma("G", lambda h: h.dma_start(out=wkv[:, 64:1088], in_=W["v_w"]),
                          deps=a_end, name="wvw"))
        wsc2b = at("wsc2b", [128, 2], BF16, 65024)
        a2w.append(sc.dma("G", lambda h: h.dma_start(
            out=wsc2b[:, :], in_=W["scorer_w2"].rearrange("(k p) one -> p (k one)", k=2),
            allow_slow_non_contiguous=True),
            deps=a_end, name="wsc2"))

        sc_dma_toks = []
        scst_toks = []
        summ_toks = []
        ld_y_prev = None
        for s in range(16):
            gt0 = s * GRP
            t_y = sc.dma("SYNC", lambda h, gt0=gt0: h.dma_start(
                out=ybuf[:, :], in_=YSC.ap()[:, :, gt0:gt0 + GRP].rearrange("k p t -> p k t")),
                deps=a_end + ysc_dmas[s:s + 1] + ([ld_y_prev] if ld_y_prev else []), name=f"a2y{s}")
            h1toks = []
            for kt in range(2):
                si = kt

                def mk_h1(h, kt=kt, si=si):
                    last = None
                    for k in range(8):
                        last = h.matmul(p_inproj[si][:, :],
                                        wsc1[:, k * 256 + kt * 128:k * 256 + kt * 128 + 128],
                                        ybuf[:, k * GRP:(k + 1) * GRP],
                                        start=(k == 0), stop=(k == 7))
                    return last
                tok_mm = sc.t(mk_h1, deps=chn(f"ip{si}", [t_y] + a2w), name=f"h1m{s}_{kt}")
                chains[f"ip{si}"] = tok_mm
                tok_ev = sc.s(lambda h, kt=kt, si=si: h.activation(
                    out=h1b[:, kt * GRP:(kt + 1) * GRP], in_=p_inproj[si][:, :], func=AF.Relu),
                    deps=chn(f"ip{si}", [tok_mm]), name=f"h1e{s}_{kt}")
                chains[f"ip{si}"] = tok_ev
                h1toks.append(tok_ev)

            def mk_sc2(h):
                h.matmul(p_sb[0:1, :], wsc2b[:, 0:1], h1b[:, 0:GRP], start=True, stop=False)
                return h.matmul(p_sb[0:1, :], wsc2b[:, 1:2], h1b[:, GRP:2 * GRP],
                                start=False, stop=True)
            tok_scm = sc.t(mk_sc2, deps=chn("b6", h1toks + [a2w[-1]]), name=f"scm{s}")
            chains["b6"] = tok_scm
            tok_th = sc.s(lambda h: h.activation(
                out=scrow[:, :], in_=p_sb[0:1, :], func=AF.Tanh, scale=0.5),
                deps=chn("b6", [tok_scm] + scst_toks[-1:] + sc_dma_toks[-1:]),
                name=f"sct{s}")
            chains["b6"] = tok_th
            tok_sc = sc.v(lambda h: h.tensor_scalar(
                out=scrow[:, :], in0=scrow[:, :], scalar1=0.5, scalar2=0.5,
                op0=ALU.mult, op1=ALU.add), deps=[tok_th], name=f"scs{s}")
            scst_toks.append(sc.dma("SYNC", lambda h, s=s: h.dma_start(
                out=scst[s:s + 1, :], in_=scrow[:, :]), deps=[tok_sc], name=f"sc2{s}"))
            sc_dma_toks.append(sc.dma("SYNC", lambda h, s=s: h.dma_start(
                out=sc_in.ap()[0:1, s * GRP:(s + 1) * GRP], in_=scrow[:, :]),
                deps=[tok_sc], name=f"scd{s}"))
            for t2 in range(2):
                def mk_sm(h, t2=t2):
                    last = None
                    for k in range(8):
                        last = h.matmul(p_b4[:, 0:64],
                                        ybuf[:, k * GRP + t2 * 128:k * GRP + t2 * 128 + 128],
                                        wsumm[:, k * 64:(k + 1) * 64],
                                        start=(k == 0), stop=(k == 7))
                    return last
                tok_sm = sc.t(mk_sm, deps=chn("b4", [t_y] + a2w), name=f"smm{s}_{t2}")
                chains["b4"] = tok_sm
                tok_se = sc.s(lambda h, s=s, t2=t2: h.copy(
                    out=summ_sb[:, (2 * s + t2) * 64:(2 * s + t2 + 1) * 64], in_=p_b4[:, 0:64]),
                    deps=chn("b4", [tok_sm]), name=f"sme{s}_{t2}")
                chains["b4"] = tok_se
                summ_toks.append(tok_se)
            ld_y_prev = tok_sm

        if d_sc is not None:
            sc.dma("SYNC", lambda h: h.dma_start(out=d_sc, in_=scst[:, :]),
                   deps=scst_toks, name="dbg_sc")

        tok_ag = sc.cc(lambda h: h.collective_compute(
            "AllGather", ALU.bypass, replica_groups=[list(range(NCORES))],
            ins=[sc_in.ap().opt()], outs=[sc_out.ap().opt()]),
            deps=sc_dma_toks, name="ag")
        tok_sg = sc.dma("SYNC", lambda h: h.dma_start(
            out=sglob[:, :], in_=sc_out.ap().rearrange("r (p f) -> (r p) f", p=16)),
            deps=[tok_ag], name="sgl")

        scl_toks = []
        for half in range(2):
            tok_tm = sc.t(lambda h, half=half: h.transpose(
                p_b7[:, 0:16], scst[0:16, half * 128:(half + 1) * 128], eye_f[0:16, 0:16]),
                deps=chn("b7", scst_toks), name=f"sclt{half}")
            chains["b7"] = tok_tm
            tok_te = sc.s(lambda h, half=half: h.copy(
                out=scloc[:, half * 16:(half + 1) * 16], in_=p_b7[:, 0:16]),
                deps=chn("b7", [tok_tm]), name=f"scle{half}")
            chains["b7"] = tok_te
            scl_toks.append(tok_te)

        tok_vm = sc.v(lambda h: h.tensor_scalar(out=svm[:, :], in0=sglob[:, :],
                                                scalar1=0.5, scalar2=None, op0=ALU.is_gt),
                      deps=[tok_sg], name="vm")
        tok_w2 = sc.v(lambda h: h.tensor_mul(out=cascA[:, 0:256], in0=sglob[:, :],
                                             in1=svm[:, :]), deps=[tok_sg, tok_vm],
                      name="sw2")
        tok_of = sc.v(lambda h: h.memset(onescol_f[:, :], 1.0), deps=a_end, name="onef")
        tok_cnt1 = sc.v(lambda h: h.reduce_sum(out=sccols[:, 0:1], in_=svm[:, :], axis=AX.X),
                        deps=[tok_vm, tok_w2], name="cnt1")
        tok_sum1 = sc.v(lambda h: h.reduce_sum(out=sccols[:, 1:2], in_=sglob[:, :], axis=AX.X),
                        deps=[tok_sg, tok_cnt1], name="sum1")
        tok_cnt2 = sc.t(lambda h: h.matmul(p_b7[0:1, 0:2], onescol_f[:, :], sccols[:, 0:2],
                                           start=True, stop=True),
                        deps=chn("b7", [tok_cnt1, tok_sum1, tok_of]), name="cnt2")
        chains["b7"] = tok_cnt2
        tok_cs2 = sc.s(lambda h: h.copy(out=sccols[0:1, 2:4], in_=p_b7[0:1, 0:2]),
                       deps=chn("b7", [tok_cnt2]), name="cnts")
        chains["b7"] = tok_cs2
        tok_c1 = sc.v(lambda h: h.tensor_scalar(out=sccols[0:1, 4:5], in0=sccols[0:1, 3:4],
                                                scalar1=1.0 / 32768.0, scalar2=0.3,
                                                op0=ALU.mult, op1=ALU.is_gt),
                      deps=[tok_cs2], name="c1")
        tok_c3 = sc.v(lambda h: h.tensor_scalar(out=sccols[0:1, 5:6], in0=sccols[0:1, 2:3],
                                                scalar1=0.0, scalar2=None, op0=ALU.is_gt),
                      deps=[tok_cs2, tok_c1], name="c3")
        tok_c4 = sc.v(lambda h: h.tensor_mul(out=sccols[0:1, 4:5], in0=sccols[0:1, 4:5],
                                             in1=sccols[0:1, 5:6]), deps=[tok_c3], name="c4")
        tok_c5 = sc.v(lambda h: h.tensor_scalar(out=sccols[0:1, 6:7], in0=sccols[0:1, 4:5],
                                                scalar1=0.5, scalar2=None, op0=ALU.mult),
                      deps=[tok_c4], name="c5")
        tok_cb = sc.t(lambda h: h.matmul(p_b7[:, 0:1], ones1_f[:, :], sccols[0:1, 6:7],
                                         start=True, stop=True),
                      deps=chn("b7", [tok_c5]), name="cb")
        chains["b7"] = tok_cb
        tok_cbe = sc.s(lambda h: h.copy(out=cond05[:, :], in_=p_b7[:, 0:1]),
                       deps=chn("b7", [tok_cb]), name="cbe")
        chains["b7"] = tok_cbe

        def cascade(src_toks, rows, width, tag):
            toks = list(src_toks)
            cur, oth = cascA, cascB
            for r in range(7):
                t1 = sc.v(lambda h, r=r, cur=cur, rows=rows, width=width: h.max(
                    out=cands[0:rows, r * 8:(r + 1) * 8], in_=cur[0:rows, 0:width]),
                    deps=toks[-2:], name=f"{tag}mx{r}")
                toks.append(t1)
                if r == 6:
                    break
                t2 = sc.v(lambda h, r=r, cur=cur, rows=rows, width=width: h.tensor_scalar(
                    out=cmask[0:rows, 0:width], in0=cur[0:rows, 0:width],
                    scalar1=cands[0:rows, r * 8 + 7:r * 8 + 8], scalar2=None, op0=ALU.is_lt),
                    deps=[t1], name=f"{tag}lt{r}")
                t3 = sc.v(lambda h, cur=cur, oth=oth, rows=rows, width=width: h.tensor_mul(
                    out=oth[0:rows, 0:width], in0=cur[0:rows, 0:width],
                    in1=cmask[0:rows, 0:width]), deps=[t2], name=f"{tag}sel{r}")
                toks.append(t3)
                cur, oth = oth, cur
            return toks

        l1 = cascade([tok_w2, tok_cnt1], 128, 256, "L1")
        tok_f2 = sc.dma("SYNC", lambda h: h.dma_start(out=cascA[0:8, 0:896], in_=cands[:, :]),
                        deps=l1, name="fl2")
        l2 = cascade([tok_f2, tok_f2], 8, 896, "L2")
        tok_f3 = sc.dma("SYNC", lambda h: h.dma_start(out=cascA[0:1, 0:448], in_=cands[0:8, :]),
                        deps=l2, name="fl3")
        l3 = cascade([tok_f3, tok_f3], 1, 448, "L3")
        tok_vv0 = sc.v(lambda h: h.tensor_copy(out=vvals[:, :], in_=cands[0:1, 0:56]),
                       deps=l3, name="vvc0")
        tok_vb = sc.t(lambda h: h.matmul(p_b7[:, 0:50], ones1_f[:, :], vvals[0:1, 0:50],
                                         start=True, stop=True),
                      deps=chn("b7", [tok_vv0]), name="vbm")
        chains["b7"] = tok_vb
        tok_vbe = sc.s(lambda h: h.copy(out=vb50[:, :], in_=p_b7[:, 0:50]),
                       deps=chn("b7", [tok_vb]), name="vbe")
        chains["b7"] = tok_vbe
        tok_mr1 = sc.v(lambda h: h.tensor_scalar(out=vvals[0:1, 0:50], in0=vvals[0:1, 0:50],
                                                 scalar1=0.5, scalar2=None, op0=ALU.is_gt),
                       deps=[tok_vbe], name="mr1")
        tok_mr2 = sc.v(lambda h: h.tensor_scalar(out=vvals[0:1, 0:50], in0=vvals[0:1, 0:50],
                                                 scalar1=-1.0, scalar2=1e9,
                                                 op0=ALU.add, op1=ALU.mult),
                       deps=[tok_mr1], name="mr2")
        tok_mb = sc.t(lambda h: h.matmul(p_b7[:, 0:50], ones1_f[:, :], vvals[0:1, 0:50],
                                         start=True, stop=True),
                      deps=chn("b7", [tok_mr2]), name="mbm")
        chains["b7"] = tok_mb
        tok_mbe = sc.s(lambda h: h.copy(out=maskb[:, 0:50], in_=p_b7[:, 0:50]),
                       deps=chn("b7", [tok_mb]), name="mbe")
        chains["b7"] = tok_mbe

        pp_prev = None
        prev_sx = None
        for j in range(32):
            col = (j % 2) * 16 + (j // 2)
            tok_sx = sc.v(lambda h, col=col: h.tensor_scalar(
                out=Sx[:, 0:50], in0=vb50[:, :], scalar1=scloc[:, col:col + 1], scalar2=None,
                op0=ALU.is_equal),
                deps=[tok_vbe] + scl_toks + ([pp_prev, prev_sx] if pp_prev else []),
                name=f"sx{j}")
            tok_pm = sc.t(lambda h, j=j: h.matmul(
                p_b4[0:50, 64:128], Sx[:, 0:50], summ_sb[:, j * 64:(j + 1) * 64],
                start=(j == 0), stop=(j == 31)),
                deps=chn("b4", [tok_sx] + (summ_toks if j == 0 else [])), name=f"pm{j}")
            chains["b4"] = tok_pm
            pp_prev = tok_pm
            prev_sx = tok_sx
        tok_pz0 = sc.v(lambda h: h.memset(pool_f[:, :], 0.0), deps=a_end, name="plz")
        tok_pe = sc.s(lambda h: h.copy(out=pool_f[0:50, 0:64], in_=p_b4[0:50, 64:128]),
                      deps=chn("b4", [pp_prev, tok_pz0]), name="pe")
        chains["b4"] = tok_pe
        tok_pd = sc.dma("SYNC", lambda h: h.dma_start(out=pool_in.ap(), in_=pool_f[:, :]),
                        deps=[tok_pe], name="pd")
        tok_ar = sc.cc(lambda h: h.collective_compute(
            "AllReduce", ALU.add, replica_groups=[list(range(NCORES))],
            ins=[pool_in.ap().opt()], outs=[pool_out.ap().opt()]),
            deps=[tok_pd], name="ar")
        tok_pl = sc.dma("SYNC", lambda h: h.dma_start(out=pool_f[:, :], in_=pool_out.ap()),
                        deps=[tok_ar, tok_pe], name="pl")
        if d_pool is not None:
            sc.dma("SYNC", lambda h: h.dma_start(out=d_pool, in_=pool_f[:, :]),
                   deps=[tok_pl], name="dbg_pool")

        tok_pb = sc.v(lambda h: h.tensor_copy(out=poolb[:, :], in_=pool_f[:, :]),
                      deps=[tok_pl], name="pb")
        tok_ptm = sc.t(lambda h: h.transpose(p_trb[0:64, 0:64], poolb[0:64, :],
                                             eye_bf[0:64, 0:64]),
                       deps=chn("b5", [tok_pb]), name="ptm")
        chains["b5"] = tok_ptm
        tok_pte = sc.s(lambda h: h.copy(out=poolT_bf[:, :], in_=p_trb[0:64, 0:64]),
                       deps=chn("b5", [tok_ptm]), name="pte")
        chains["b5"] = tok_pte
        tok_kk = sc.t(lambda h: h.matmul(p_b4[0:64, 0:64], wkv[:, 0:64], poolT_bf[:, :],
                                         start=True, stop=True),
                      deps=chn("b4", [tok_pte] + a2w), name="kkm")
        chains["b4"] = tok_kk
        tok_kke = sc.s(lambda h: h.copy(out=kk_bf[:, :], in_=p_b4[0:64, 0:64]),
                       deps=chn("b4", [tok_kk]), name="kke")
        chains["b4"] = tok_kke
        vv_toks = []
        for half in range(2):
            tok_vvm = sc.t(lambda h, half=half: h.matmul(
                p_T[0][0:64, 0:512], poolT_bf[:, :],
                wkv[:, 64 + half * 512:64 + (half + 1) * 512], start=True, stop=True),
                deps=chn("T0", [tok_pte] + a2w), name=f"vvm{half}")
            chains["T0"] = tok_vvm
            tok_ve = sc.s(lambda h, half=half: h.copy(
                out=vv_bf[:, half * 512:(half + 1) * 512], in_=p_T[0][0:64, 0:512]),
                deps=chn("T0", [tok_vvm]), name=f"vve{half}")
            chains["T0"] = tok_ve
            vv_toks.append(tok_ve)
        tok_vvc = sc.v(lambda h: h.tensor_scalar(out=vvc_bf[:, :], in0=vv_bf[:, :],
                                                 scalar1=cond05[0:64, 0:1], scalar2=None,
                                                 op0=ALU.mult),
                       deps=vv_toks + [tok_cbe], name="vvc")
        vvT_toks = []
        for k in range(8):
            tok_tm = sc.t(lambda h, k=k: h.transpose(
                p_trb[:, 0:64], vv_bf[0:64, k * 128:(k + 1) * 128], eye_bf[0:64, 0:64]),
                deps=chn("b5", vv_toks), name=f"vvtm{k}")
            chains["b5"] = tok_tm
            tok_te = sc.s(lambda h, k=k: h.copy(out=vvT[:, k * 64:(k + 1) * 64],
                                                in_=p_trb[:, 0:64]),
                          deps=chn("b5", [tok_tm]), name=f"vvte{k}")
            chains["b5"] = tok_te
            vvT_toks.append(tok_te)
        vg_toks = []
        for half in range(2):
            def mk_vg(h, half=half):
                last = None
                for k in range(8):
                    last = h.matmul(p_T[0][0:64, 0:512], vvT[:, k * 64:(k + 1) * 64],
                                    wg2[:, k * 1024 + half * 512:k * 1024 + half * 512 + 512],
                                    start=(k == 0), stop=(k == 7))
                return last
            tok_vg = sc.t(mk_vg, deps=chn("T0", vvT_toks + a2w), name=f"vgm{half}")
            chains["T0"] = tok_vg
            tok_vge = sc.s(lambda h, half=half: h.copy(
                out=VG_bf[:, half * 512:(half + 1) * 512], in_=p_T[0][0:64, 0:512]),
                deps=chn("T0", [tok_vg]), name=f"vge{half}")
            chains["T0"] = tok_vge
            vg_toks.append(tok_vge)

        pool_ready = [tok_kke, tok_vvc, tok_mbe, tok_cbe] + vv_toks + vg_toks
        b_sq_toks = []
        v2h_outs = {}
        prev_y_readers = []
        prev_x_readers = []
        prev_v2_state = {}
        prev_pr = []
        wg_last_reader = [None]
        for s in range(16):
            gt0 = s * GRP
            bloc = s // 8
            t0 = (s % 8) * GRP
            sp = s % 2
            t_y = sc.dma("SYNC", lambda h, gt0=gt0: h.dma_start(
                out=ybuf[:, :], in_=YSC.ap()[:, :, gt0:gt0 + GRP].rearrange("k p t -> p k t")),
                deps=pool_ready + prev_y_readers + [ld_y_prev], name=f"by{s}")
            t_x = sc.dma("SYNC", lambda h, bloc=bloc, t0=t0: h.dma_start(
                out=xbuf[:, :].rearrange("p (tt d) -> p tt d", tt=2),
                in_=XI[bloc, t0:t0 + GRP, :].rearrange("(tt p) d -> p tt d", tt=2)),
                deps=(prev_x_readers or pool_ready), name=f"bx{s}")

            def mk_q(h):
                last = None
                for k in range(8):
                    last = h.matmul(p_b4[0:64, 0:256], wq[:, k * 64:(k + 1) * 64],
                                    ybuf[:, k * GRP:(k + 1) * GRP],
                                    start=(k == 0), stop=(k == 7))
                return last
            tok_q = sc.t(mk_q, deps=chn("b4", [t_y]), name=f"bq{s}")
            chains["b4"] = tok_q
            tok_qe = sc.s(lambda h: h.copy(out=q_bf[:, :], in_=p_b4[0:64, 0:256]),
                          deps=chn("b4", [tok_q]), name=f"bqe{s}")
            chains["b4"] = tok_qe
            ptT_toks = []
            for tt in range(2):
                tok_lg = sc.t(lambda h, tt=tt: h.matmul(
                    p_b7[:, 0:50], q_bf[:, tt * 128:(tt + 1) * 128], kk_bf[:, 0:50],
                    start=True, stop=True), deps=chn("b7", [tok_qe, tok_kke]),
                    name=f"blg{s}_{tt}")
                chains["b7"] = tok_lg
                tok_l1 = sc.v(lambda h: h.scalar_tensor_tensor(
                    out=lgb[:, 0:50], in0=p_b7[:, 0:50], scalar=0.25, in1=maskb[:, 0:50],
                    op0=ALU.mult, op1=ALU.add), deps=chn("b7", [tok_lg, tok_mbe]),
                    name=f"bl1{s}_{tt}")
                chains["b7"] = tok_l1
                tok_mx = sc.v(lambda h: h.reduce_max(out=sccols[:, 7:8], in_=lgb[:, 0:50],
                                                     axis=AX.X),
                              deps=[tok_l1] + prev_pr[-1:], name=f"bmx{s}_{tt}")
                tok_nm = sc.v(lambda h: h.tensor_scalar(out=sccols[:, 7:8], in0=sccols[:, 7:8],
                                                        scalar1=-1.0, scalar2=None,
                                                        op0=ALU.mult),
                              deps=[tok_mx], name=f"bnm{s}_{tt}")
                tok_ex = sc.s(lambda h: h.activation(out=lgb[:, 0:50], in_=lgb[:, 0:50],
                                                     func=AF.Exp, bias=sccols[:, 7:8]),
                              deps=[tok_nm], name=f"bex{s}_{tt}")
                tok_sm2 = sc.v(lambda h: h.reduce_sum(out=sccols[:, 7:8], in_=lgb[:, 0:50],
                                                      axis=AX.X), deps=[tok_ex],
                               name=f"bsm{s}_{tt}")
                tok_rc = sc.v(lambda h: h.reciprocal(out=sccols[:, 7:8], in_=sccols[:, 7:8]),
                              deps=[tok_sm2], name=f"brc{s}_{tt}")
                tok_pr = sc.v(lambda h: h.tensor_scalar(out=prow[:, 0:50], in0=lgb[:, 0:50],
                                                        scalar1=sccols[:, 7:8], scalar2=None,
                                                        op0=ALU.mult),
                              deps=[tok_rc], name=f"bpr{s}_{tt}")
                prev_pr.append(tok_pr)
                tok_ptm2 = sc.t(lambda h: h.transpose(p_trb[0:50, :], prow[:, 0:50],
                                                      eye_bf[:, :]),
                                deps=chn("b5", [tok_pr]), name=f"bptm{s}_{tt}")
                chains["b5"] = tok_ptm2
                tok_pte2 = sc.s(lambda h, tt=tt: h.copy(
                    out=pT_bf[0:50, tt * 128:(tt + 1) * 128], in_=p_trb[0:50, 0:128]),
                    deps=chn("b5", [tok_ptm2]), name=f"bpte{s}_{tt}")
                chains["b5"] = tok_pte2
                ptT_toks.append(tok_pte2)
            y_readers = [tok_q]
            x_readers = []
            v2_writes = []
            for m in range(8):
                tok_rm = sc.t(lambda h, m=m: h.matmul(
                    p_inproj[0][:, :], vvc_bf[0:50, m * 128:(m + 1) * 128], pT_bf[0:50, :],
                    start=True, stop=True),
                    deps=chn("ip0", ptT_toks + [tok_vvc]), name=f"brt{s}_{m}")
                chains["ip0"] = tok_rm

                def mk_g(h, m=m):
                    for k in range(8):
                        h.matmul(p_inproj[1][:, :],
                                 wg1[:, k * 1024 + m * 128:k * 1024 + m * 128 + 128],
                                 ybuf[:, k * GRP:(k + 1) * GRP], start=(k == 0), stop=False)
                    return h.matmul(p_inproj[1][:, :], VG_bf[0:50, m * 128:(m + 1) * 128],
                                    pT_bf[0:50, :], start=False, stop=True)
                tok_gm = sc.t(mk_g, deps=chn("ip1", ptT_toks + [t_y] + vg_toks),
                              name=f"bg{s}_{m}")
                chains["ip1"] = tok_gm
                wg_last_reader[0] = tok_gm
                y_readers.append(tok_gm)
                tok_gt = sc.s(lambda h: h.activation(out=gate_bf[:, :], in_=p_inproj[1][:, :],
                                                     func=AF.Tanh, scale=0.5),
                              deps=chn("ip1", [tok_gm]), name=f"bgt{s}_{m}")
                chains["ip1"] = tok_gt
                tok_t2 = sc.v(lambda h: h.scalar_tensor_tensor(
                    out=t2b[:, :], in0=gate_bf[:, :], scalar=1.0, in1=p_inproj[0][:, :],
                    op0=ALU.add, op1=ALU.mult), deps=chn("ip0", [tok_gt]), name=f"bt2{s}_{m}")
                chains["ip0"] = tok_t2
                tok_a3 = sc.v(lambda h, m=m: h.tensor_tensor(
                    out=a3b[:, m * GRP:(m + 1) * GRP], in0=t2b[:, :],
                    in1=ybuf[:, m * GRP:(m + 1) * GRP], op=ALU.add),
                    deps=[tok_t2], name=f"ba3{s}_{m}")
                y_readers.append(tok_a3)
                for tt in range(2):
                    tok_tm = sc.t(lambda h, m=m, tt=tt: h.transpose(
                        p_trb[:, :], a3b[:, m * GRP + tt * 128:m * GRP + tt * 128 + 128],
                        eye_bf[:, :]), deps=chn("b5", [tok_a3]), name=f"btm{s}_{m}_{tt}")
                    chains["b5"] = tok_tm
                    tok_v2 = sc.v(lambda h, m=m, tt=tt, sp=sp: h.scalar_tensor_tensor(
                        out=v2loc[sp][:, tt * 1024 + m * 128:tt * 1024 + (m + 1) * 128],
                        in0=xbuf[:, tt * 1024 + m * 128:tt * 1024 + (m + 1) * 128],
                        scalar=2.0, in1=p_trb[:, :], op0=ALU.mult, op1=ALU.add),
                        deps=chn("b5", [tok_tm, t_x] + prev_v2_state.get(sp, [])),
                        name=f"bv2{s}_{m}_{tt}")
                    chains["b5"] = tok_v2
                    x_readers.append(tok_v2)
                    v2_writes.append(tok_v2)
            sq_s = []
            for tt in range(2):
                jj = 2 * s + tt
                tok_s1 = sc.s(lambda h, jj=jj, sp=sp, tt=tt: h.activation(
                    out=dump_b[:, :], in_=v2loc[sp][:, tt * 1024:(tt + 1) * 1024],
                    func=AF.Copy, scale=1.0 / 1024.0, accum_out=mucol[:, jj:jj + 1]),
                    deps=v2_writes + b_sq_toks[-1:], name=f"bsu{s}_{tt}")
                tok_s2 = sc.s(lambda h, jj=jj, sp=sp, tt=tt: h.activation(
                    out=dump_b[:, :], in_=v2loc[sp][:, tt * 1024:(tt + 1) * 1024],
                    func=AF.Square, scale=1.0 / 32.0, accum_out=sqcol[:, jj:jj + 1]),
                    deps=[tok_s1], name=f"bsq{s}_{tt}")
                b_sq_toks.append(tok_s2)
                sq_s.append(tok_s2)
            v2h_outs[s] = sc.dma("SYNC", lambda h, s=s, sp=sp: h.dma_start(
                out=V2H.ap()[:, s * 2048:(s + 1) * 2048], in_=v2loc[sp][:, :]),
                deps=v2_writes + sq_s, name=f"bvh{s}")
            prev_v2_state[sp] = sq_s + [v2h_outs[s]]
            prev_y_readers = y_readers
            prev_x_readers = x_readers

        all_v2h = list(v2h_outs.values())
        tok_m2 = sc.v(lambda h: h.tensor_mul(out=sscol[:, :], in0=mucol[:, :], in1=mucol[:, :]),
                      deps=b_sq_toks + all_v2h, name="bm2")
        tok_m4 = sc.v(lambda h: h.tensor_sub(out=sqcol[:, :], in0=sqcol[:, :], in1=sscol[:, :]),
                      deps=[tok_m2], name="bm4")
        tok_m5 = sc.v(lambda h: h.tensor_scalar(out=sqcol[:, :], in0=sqcol[:, :],
                                                scalar1=EPS, scalar2=None, op0=ALU.add),
                      deps=[tok_m4], name="bm5")
        tok_m6 = sc.s(lambda h: h.activation(out=sqcol[:, :], in_=sqcol[:, :], func=AF.Ln),
                      deps=[tok_m5], name="bm6")
        tok_m7 = sc.s(lambda h: h.activation(out=sscol[:, :], in_=sqcol[:, :], func=AF.Exp,
                                             scale=-0.5), deps=[tok_m6], name="bm7")

        x1_toks = []
        v2in = {}
        slot_readers = {i: [] for i in range(4)}
        prev_dg = []
        for j in range(32):
            if j == 0:
                for jj0 in range(3):
                    v2in[jj0] = sc.dma("SYNC", lambda h, jj0=jj0: h.dma_start(
                        out=v2l4[:, (jj0 % 4) * 1024:(jj0 % 4 + 1) * 1024],
                        in_=V2H.ap()[:, jj0 * 1024:(jj0 + 1) * 1024]),
                        deps=all_v2h + b_sq_toks, name=f"b3in{jj0}")
            if j + 3 < 32:
                jn = j + 3
                v2in[jn] = sc.dma("SYNC", lambda h, jn=jn: h.dma_start(
                    out=v2l4[:, (jn % 4) * 1024:(jn % 4 + 1) * 1024],
                    in_=V2H.ap()[:, jn * 1024:(jn + 1) * 1024]),
                    deps=slot_readers[jn % 4] + [v2h_outs[jn // 2]], name=f"b3in{jn}")
            sl = j % 4
            tok_sub = sc.v(lambda h, j=j, sl=sl: h.tensor_scalar(
                out=v2l4[:, sl * 1024:(sl + 1) * 1024], in0=v2l4[:, sl * 1024:(sl + 1) * 1024],
                scalar1=mucol[:, j:j + 1], scalar2=None, op0=ALU.subtract),
                deps=[v2in[j], tok_m7], name=f"bns{j}")
            tok_dg = sc.v(lambda h, j=j: h.tensor_scalar(
                out=diag2[:, :], in0=eye_bf[:, :], scalar1=sscol[:, j:j + 1], scalar2=None,
                op0=ALU.mult), deps=[tok_m7] + prev_dg, name=f"bdg{j}")
            prev_dg = []
            for k in range(8):
                tok_mm = sc.t(lambda h, j=j, k=k, sl=sl: h.matmul(
                    p_b4[:, 0:128], v2l4[:, sl * 1024 + k * 128:sl * 1024 + (k + 1) * 128],
                    diag2[:, :], start=True, stop=True),
                    deps=chn("b4", [tok_sub, tok_dg]), name=f"bxm{j}_{k}")
                chains["b4"] = tok_mm
                prev_dg.append(tok_mm)
                tok_ev = sc.s(lambda h, j=j, k=k: h.copy(
                    out=x1fm[:, k * 4096 + j * 128:k * 4096 + (j + 1) * 128],
                    in_=p_b4[:, 0:128]),
                    deps=chn("b4", [tok_mm]), name=f"bxe{j}_{k}")
                chains["b4"] = tok_ev
                x1_toks.append(tok_ev)
            slot_readers[sl] = prev_dg + [tok_sub]
        if d_x1 is not None:
            def mk_dbg_x1(h, sem):
                for k in range(8):
                    h.dma_start(out=d_x1[k * 128:(k + 1) * 128, :],
                                in_=x1fm[:, k * 4096:k * 4096 + 256]).then_inc(sem, 16)
            sc.dma_multi("G", mk_dbg_x1, 8, deps=x1_toks[-1:], name="dbg_x1")

        tok_sel = sc.dma("SYNC", lambda h: h.dma_start(out=selec[:, :], in_=C["sele_bf"]),
                         deps=a_end, name="c_sele")
        tok_wr = sc.dma("G", lambda h: h.dma_start(
            out=wrt[:, :].rearrange("p (k ex) -> p k ex", k=8),
            in_=W["ffn_gate_w"].rearrange("(k p) ex -> p k ex", k=8)),
            deps=a_end, name="c_wrt")
        tok_gb = sc.dma("SYNC", lambda h: h.dma_start(
            out=gb_col[:, :], in_=W["ffn_gate_b"].rearrange("(p one) -> p one", one=1)),
            deps=a_end, name="c_gb")
        tok_gbh = sc.v(lambda h: h.tensor_scalar(out=gb_col[:, :], in0=gb_col[:, :],
                                                 scalar1=0.5, scalar2=None, op0=ALU.mult),
                       deps=[tok_gb], name="c_gbh")

        gfm_toks = []
        prev_th = []
        for j in range(32):
            def mk_rt(h, j=j):
                last = None
                for k in range(8):
                    last = h.matmul(p_b4[0:16, 0:128], wrt[:, k * 16:(k + 1) * 16],
                                    x1fm[:, k * 4096 + j * 128:k * 4096 + (j + 1) * 128],
                                    start=(k == 0), stop=(k == 7))
                return last
            tok_rm2 = sc.t(mk_rt, deps=chn("b4", x1_toks[-1:] + [tok_wr]), name=f"crt{j}")
            chains["b4"] = tok_rm2
            tok_th2 = sc.s(lambda h: h.activation(out=thb[:, :], in_=p_b4[0:16, 0:128],
                                                  func=AF.Tanh, scale=0.5, bias=gb_col[:, :]),
                           deps=chn("b4", [tok_rm2, tok_gbh] + prev_th), name=f"cth{j}")
            chains["b4"] = tok_th2
            tok_tt2 = sc.t(lambda h: h.transpose(p_trb[:, 0:16], thb[:, :],
                                                 eye_bf[0:16, 0:16]),
                           deps=chn("b5", [tok_th2]), name=f"ctt{j}")
            chains["b5"] = tok_tt2
            prev_th = [tok_tt2]
            tok_te2 = sc.s(lambda h: h.copy(out=cga[:, :], in_=p_trb[:, 0:16]),
                           deps=chn("b5", [tok_tt2]), name=f"cte{j}")
            chains["b5"] = tok_te2
            tk = sc.v(lambda h: h.reduce_max(out=rcols[:, 0:1], in_=cga[:, :], axis=AX.X),
                      deps=[tok_te2], name=f"cm1{j}")
            tk = sc.v(lambda h: h.tensor_scalar(out=ct2[:, :], in0=cga[:, :],
                                                scalar1=rcols[:, 0:1], scalar2=None,
                                                op0=ALU.is_ge), deps=[tk], name=f"ce1{j}")
            tk = sc.v(lambda h: h.scalar_tensor_tensor(out=clg[:, :], in0=ct2[:, :],
                                                       scalar=-10.0, in1=cga[:, :],
                                                       op0=ALU.mult, op1=ALU.add),
                      deps=[tk], name=f"cz2{j}")
            tk = sc.v(lambda h: h.reduce_max(out=rcols[:, 1:2], in_=clg[:, :], axis=AX.X),
                      deps=[tk], name=f"cm2{j}")
            tk = sc.v(lambda h: h.tensor_scalar(out=clg[:, :], in0=clg[:, :],
                                                scalar1=rcols[:, 1:2], scalar2=None,
                                                op0=ALU.is_ge), deps=[tk], name=f"ce2{j}")
            tk = sc.v(lambda h: h.tensor_add(out=rcols[:, 2:3], in0=rcols[:, 0:1],
                                             in1=rcols[:, 1:2]), deps=[tk], name=f"csu{j}")
            tk = sc.v(lambda h: h.tensor_scalar(out=rcols[:, 2:3], in0=rcols[:, 2:3],
                                                scalar1=2.0, scalar2=None, op0=ALU.add),
                      deps=[tk], name=f"csu2{j}")
            tk = sc.v(lambda h: h.reciprocal(out=rcols[:, 2:3], in_=rcols[:, 2:3]),
                      deps=[tk], name=f"crc{j}")
            tk = sc.v(lambda h: h.tensor_scalar(out=rcols[:, 3:4], in0=rcols[:, 0:1],
                                                scalar1=1.0, scalar2=rcols[:, 2:3],
                                                op0=ALU.add, op1=ALU.mult),
                      deps=[tk], name=f"cv1{j}")
            tk = sc.v(lambda h: h.tensor_scalar(out=rcols[:, 4:5], in0=rcols[:, 1:2],
                                                scalar1=1.0, scalar2=rcols[:, 2:3],
                                                op0=ALU.add, op1=ALU.mult),
                      deps=[tk], name=f"cv2{j}")
            tk = sc.v(lambda h: h.tensor_scalar(out=ct2[:, :], in0=ct2[:, :],
                                                scalar1=rcols[:, 3:4], scalar2=None,
                                                op0=ALU.mult), deps=[tk], name=f"cg1{j}")
            tk = sc.v(lambda h, j=j: h.scalar_tensor_tensor(
                out=gtok[:, j * 16:(j + 1) * 16], in0=clg[:, :], scalar=rcols[:, 4:5],
                in1=ct2[:, :], op0=ALU.mult, op1=ALU.add), deps=[tk], name=f"cg2{j}")
            tok_gtm = sc.t(lambda h, j=j: h.transpose(
                p_trb[0:16, :], gtok[:, j * 16:(j + 1) * 16], eye_bf[:, :]),
                deps=chn("b5", [tk]), name=f"cgt{j}")
            chains["b5"] = tok_gtm
            tok_gte = sc.s(lambda h, j=j: h.copy(out=gfm[:, j * 128:(j + 1) * 128],
                                                 in_=p_trb[0:16, 0:128]),
                           deps=chn("b5", [tok_gtm]), name=f"cge{j}")
            chains["b5"] = tok_gte
            gfm_toks.append(tok_gte)
        if d_gw is not None:
            sc.dma("G", lambda h: h.dma_start(out=d_gw, in_=gtok[:, :]),
                   deps=gfm_toks, name="dbg_gw")

        def load_expert(ei, slot, deps):
            names = (("experts_wg", "experts_wu", "experts_wd") if ei < 16 else
                     ("shared_wg", "shared_wu", "shared_wd"))
            toks = []
            for wnm, key, ks in ((names[0], "wg", 8), (names[1], "wu", 8), (names[2], "wd", 4)):
                def mk(h, ei=ei, slot=slot, wnm=wnm, key=key, ks=ks):
                    srcap = W[wnm][ei] if ei < 16 else W[wnm]
                    return h.dma_start(
                        out=wslot[slot][key][:, :].rearrange("p (k i) -> p k i", k=ks),
                        in_=srcap.rearrange("(k p) i -> p k i", k=ks))
                toks.append(sc.dma("G", mk, deps=deps, name=f"cw_{key}{ei}"))
            return toks

        dacc = [p_ip_t[:, i * 256:(i + 1) * 256] for i in range(4)] + \
               [p_T_t[:, i * 256:(i + 1) * 256] for i in range(4)]
        _nexp = int(os.environ.get("MK_NEXP", "17"))
        stream = (list(range(_nexp - 1)) + [16]) if _nexp < 17 else list(range(16)) + [16]
        pairs = [stream[i:i + 2] for i in range(0, len(stream), 2)]
        wtoks = {}
        slot_uses = {0: [], 1: [], 2: []}
        wtoks[0] = load_expert(0, 0, [wg_last_reader[0]])
        wtoks[1] = load_expert(1, 1, [wg_last_reader[0]])
        acc_out = {}
        acc_in = {}
        last_gbc_mm = [None]
        last_h = [None]
        dn_last = [None]
        for pi2, elist in enumerate(pairs):
            i = 2 * pi2 + 2
            if i < len(stream):
                slot = i % 3
                wtoks[i] = load_expert(stream[i], slot,
                                       slot_uses[slot] or [wg_last_reader[0]])
                slot_uses[slot] = []
            for ch in range(16):
                ct0 = ch * 256
                chp = ch % 2
                if pi2 > 0:
                    if ch == 0:
                        for cpre in range(2):
                            acc_in[cpre] = sc.dma("SYNC", lambda h, cpre=cpre: h.dma_start(
                                out=accbuf[cpre % 2][:, :], in_=ACCH.ap()[cpre]),
                                deps=[acc_out[cpre], acc_out[14 + cpre]],
                                name=f"fai{pi2}_{cpre}")
                ev_deps_extra = [acc_in[ch]] if pi2 > 0 else []
                dn_toks = []
                h_all = []
                for idx, ei in enumerate(elist):
                    si2 = 2 * pi2 + idx
                    slot = si2 % 3
                    ep = idx
                    if ei < 16:
                        tok_gb2 = sc.t(lambda h, ei=ei, ch=ch: h.matmul(
                            p_b6[:, 256:512], selec[:, ei * 128:(ei + 1) * 128],
                            gfm[0:16, ch * 256:(ch + 1) * 256], start=True, stop=True),
                            deps=chn("b6", ([tok_sel] + gfm_toks if pi2 == 0 and ch == 0
                                            else [])), name=f"fgb{ei}_{ch}")
                        chains["b6"] = tok_gb2
                        last_gbc_mm[0] = tok_gb2
                        tok_gbe = sc.s(lambda h, ep=ep: h.copy(
                            out=gbc[:, ep * 256:(ep + 1) * 256], in_=p_b6[:, 256:512]),
                            deps=chn("b6", [tok_gb2]), name=f"fge{ei}_{ch}")
                        chains["b6"] = tok_gbe
                    else:
                        tok_gbe = None
                    h_toks = []
                    for it in range(4):
                        def mk_hg(h, slot=slot, it=it, ct0=ct0):
                            last = None
                            for k in range(8):
                                last = h.matmul(
                                    p_b7[:, 0:256],
                                    wslot[slot]["wg"][:, k * 512 + it * 128:k * 512 + it * 128 + 128],
                                    x1fm[:, k * 4096 + ct0:k * 4096 + ct0 + 256],
                                    start=(k == 0), stop=(k == 7))
                            return last
                        tok_hg = sc.t(mk_hg, deps=chn("b7", wtoks[si2] + x1_toks[-1:]),
                                      name=f"fhg{ei}_{ch}_{it}")
                        chains["b7"] = tok_hg
                        slot_uses[slot].append(tok_hg)
                        tok_sl = sc.s(lambda h, ep=ep, it=it: h.activation(
                            out=hsil[:, (ep * 4 + it) * 256:(ep * 4 + it + 1) * 256],
                            in_=p_b7[:, 0:256], func=AF.Silu),
                            deps=chn("b7", [tok_hg]), name=f"fsl{ei}_{ch}_{it}")
                        chains["b7"] = tok_sl

                        def mk_hu(h, slot=slot, it=it, ct0=ct0):
                            last = None
                            for k in range(8):
                                last = h.matmul(
                                    p_b4[:, 0:256],
                                    wslot[slot]["wu"][:, k * 512 + it * 128:k * 512 + it * 128 + 128],
                                    x1fm[:, k * 4096 + ct0:k * 4096 + ct0 + 256],
                                    start=(k == 0), stop=(k == 7))
                            return last
                        tok_hu = sc.t(mk_hu, deps=chn("b4", wtoks[si2]),
                                      name=f"fhu{ei}_{ch}_{it}")
                        chains["b4"] = tok_hu
                        slot_uses[slot].append(tok_hu)
                        tok_h = sc.v(lambda h, ep=ep, it=it: h.tensor_tensor(
                            out=hcmb[:, (ep * 4 + it) * 256:(ep * 4 + it + 1) * 256],
                            in0=hsil[:, (ep * 4 + it) * 256:(ep * 4 + it + 1) * 256],
                            in1=p_b4[:, 0:256], op=ALU.mult),
                            deps=chn("b4", [tok_sl, tok_hu]), name=f"fh{ei}_{ch}_{it}")
                        chains["b4"] = tok_h
                        if ei < 16:
                            tok_h = sc.v(lambda h, ep=ep, it=it: h.tensor_tensor(
                                out=hcmb[:, (ep * 4 + it) * 256:(ep * 4 + it + 1) * 256],
                                in0=hcmb[:, (ep * 4 + it) * 256:(ep * 4 + it + 1) * 256],
                                in1=gbc[:, ep * 256:(ep + 1) * 256], op=ALU.mult),
                                deps=[tok_h, tok_gbe], name=f"fh2{ei}_{ch}_{it}")
                        h_toks.append(tok_h)
                        last_h[0] = tok_h

                    h_all.append((2 * pi2 + idx, idx, h_toks))
                def mk_dn(h, h_all=tuple(h_all)):
                    last = None
                    for m in range(8):
                        for (si3, idx3, _ht) in h_all:
                            slot3 = si3 % 3
                            for k in range(4):
                                last = h.matmul(
                                    dacc[m],
                                    wslot[slot3]["wd"][:, k * 1024 + m * 128:k * 1024 + m * 128 + 128],
                                    hcmb[:, (idx3 * 4 + k) * 256:(idx3 * 4 + k + 1) * 256],
                                    start=(idx3 == 0 and k == 0),
                                    stop=(idx3 == len(h_all) - 1 and k == 3))
                    return last
                dn_deps = [t for (_si, _idx, ht) in h_all for t in ht]
                for (si3, _idx, _ht) in h_all:
                    dn_deps += wtoks[si3]
                dn_deps += chn("ip0", []) + chn("ip1", []) + chn("T0", []) + chn("T1", [])
                tok_dn = sc.t(mk_dn, deps=dn_deps, name=f"fdn{pi2}_{ch}")
                for (si3, _idx, _ht) in h_all:
                    slot_uses[si3 % 3].append(tok_dn)
                dn_toks.append(tok_dn)
                dn_last[0] = tok_dn
                ev_toks = []
                for m in range(8):
                    if pi2 == 0:
                        tok_ev = sc.v(lambda h, m=m, chp=chp: h.tensor_copy(
                            out=accbuf[chp][:, m * 256:(m + 1) * 256], in_=dacc[m]),
                            deps=dn_toks[-1:] + ev_deps_extra +
                                 ([acc_out[ch - 2]] if ch >= 2 else []),
                            name=f"fac{pi2}_{ch}_{m}")
                    else:
                        tok_ev = sc.v(lambda h, m=m, chp=chp: h.tensor_tensor(
                            out=accbuf[chp][:, m * 256:(m + 1) * 256],
                            in0=accbuf[chp][:, m * 256:(m + 1) * 256],
                            in1=dacc[m], op=ALU.add),
                            deps=dn_toks[-1:] + ev_deps_extra, name=f"fac{pi2}_{ch}_{m}")
                    if m < 2:
                        chains["ip0"] = tok_ev
                    elif m < 4:
                        chains["ip1"] = tok_ev
                    elif m < 6:
                        chains["T0"] = tok_ev
                    else:
                        chains["T1"] = tok_ev
                    ev_toks.append(tok_ev)
                acc_out[ch] = sc.dma("SYNC", lambda h, ch=ch, chp=chp: h.dma_start(
                    out=ACCH.ap()[ch], in_=accbuf[chp][:, :]), deps=ev_toks,
                    name=f"fao{pi2}_{ch}")
                if pi2 > 0 and ch + 2 < 16:
                    acc_in[ch + 2] = sc.dma("SYNC", lambda h, ch=ch, chp=chp: h.dma_start(
                        out=accbuf[chp][:, :], in_=ACCH.ap()[ch + 2]),
                        deps=[acc_out[ch], acc_out[ch + 2]], name=f"fai{pi2}_{ch + 2}")
            i = 2 * pi2 + 3
            if i < len(stream):
                slot = i % 3
                wtoks[i] = load_expert(stream[i], slot,
                                       slot_uses[slot] or [wg_last_reader[0]])
                slot_uses[slot] = []
        if d_acc0 is not None:
            sc.dma("G", lambda h: h.dma_start(out=d_acc0, in_=ACCH.ap()[0]),
                   deps=[acc_out[0]], name="dbg_acc0")

        c3_v3h = {}
        c3_in = {}
        c3_slot_readers = {0: [acc_out[14]], 1: [acc_out[15]]}
        stat_prev = []
        for ch in range(16):
            chp = ch % 2
            ct0 = ch * 256
            c3_in[ch] = sc.dma("SYNC", lambda h, ch=ch, chp=chp: h.dma_start(
                out=accbuf[chp][:, :], in_=ACCH.ap()[ch]),
                deps=[acc_out[ch]] + c3_slot_readers[chp], name=f"g3i{ch}")
            v3_toks = []
            for m in range(8):
                tok_v3 = sc.v(lambda h, m=m, ct0=ct0, chp=chp: h.scalar_tensor_tensor(
                    out=v3fm[:, m * 256:(m + 1) * 256],
                    in0=x1fm[:, m * 4096 + ct0:m * 4096 + ct0 + 256], scalar=2.0,
                    in1=accbuf[chp][:, m * 256:(m + 1) * 256], op0=ALU.mult, op1=ALU.add),
                    deps=[c3_in[ch], dn_last[0]] + stat_prev, name=f"g3v{ch}_{m}")
                v3_toks.append(tok_v3)
            c3_slot_readers[chp] = list(v3_toks)
            tr_toks = []
            for tt in range(2):
                for m in range(8):
                    tok_tm = sc.t(lambda h, m=m, tt=tt: h.transpose(
                        p_trb[:, :], v3fm[:, m * 256 + tt * 128:m * 256 + tt * 128 + 128],
                        eye_bf[:, :]), deps=chn("b5", [v3_toks[m]]), name=f"g3t{ch}_{tt}_{m}")
                    chains["b5"] = tok_tm
                    tok_te = sc.s(lambda h, m=m, tt=tt: h.copy(
                        out=v3tok2[:, tt * 1024 + m * 128:tt * 1024 + (m + 1) * 128],
                        in_=p_trb[:, :]),
                        deps=chn("b5", [tok_tm, last_h[0]] + stat_prev), name=f"g3e{ch}_{tt}_{m}")
                    chains["b5"] = tok_te
                    tr_toks.append(tok_te)
            st_toks = []
            for tt in range(2):
                jj = 2 * ch + tt
                tok_s1 = sc.s(lambda h, jj=jj, tt=tt: h.activation(
                    out=outst2[0][:, :], in_=v3tok2[:, tt * 1024:(tt + 1) * 1024],
                    func=AF.Copy, scale=-1.0 / 1024.0, accum_out=fmu[:, jj:jj + 1]),
                    deps=tr_toks[tt * 8:tt * 8 + 8] + st_toks[-1:], name=f"g3s{ch}_{tt}")
                tok_s2 = sc.s(lambda h, jj=jj, tt=tt: h.activation(
                    out=outst2[0][:, :], in_=v3tok2[:, tt * 1024:(tt + 1) * 1024],
                    func=AF.Square, scale=1.0 / 32.0, accum_out=fsq[:, jj:jj + 1]),
                    deps=[tok_s1], name=f"g3q{ch}_{tt}")
                st_toks += [tok_s1, tok_s2]
            c3_v3h[ch] = sc.dma("SYNC", lambda h, ch=ch: h.dma_start(
                out=V3H.ap()[ch], in_=v3tok2[:, :]), deps=tr_toks + st_toks,
                name=f"g3o{ch}")
            stat_prev = st_toks + [c3_v3h[ch]]

        tok_f2b = sc.v(lambda h: h.tensor_mul(out=fs2[:, :], in0=fmu[:, :], in1=fmu[:, :]),
                       deps=stat_prev, name="g4a")
        tok_f4 = sc.v(lambda h: h.tensor_sub(out=fsq[:, :], in0=fsq[:, :], in1=fs2[:, :]),
                      deps=[tok_f2b] + stat_prev, name="g4b")
        tok_f5 = sc.v(lambda h: h.tensor_scalar(out=fsq[:, :], in0=fsq[:, :],
                                                scalar1=EPS, scalar2=None, op0=ALU.add),
                      deps=[tok_f4], name="g4c")
        tok_f6 = sc.s(lambda h: h.activation(out=fsq[:, :], in_=fsq[:, :], func=AF.Ln),
                      deps=[tok_f5], name="g4d")
        tok_f7 = sc.s(lambda h: h.activation(out=fsq[:, :], in_=fsq[:, :], func=AF.Exp,
                                             scale=-0.5), deps=[tok_f6], name="g4e")

        v3slots = [v3tok2, at("v3tok3", [128, 2 * 1024], BF16,
                              nc.lookup_mloc(selec).addr - RB)]
        c5_in = {}
        c5_readers = {0: [], 1: []}
        out_dmas = []
        for ch in range(16):
            cs = ch % 2
            if ch == 0:
                for cpre in range(2):
                    c5_in[cpre] = sc.dma("SYNC", lambda h, cpre=cpre: h.dma_start(
                        out=v3slots[cpre % 2][:, :], in_=V3H.ap()[cpre]),
                        deps=[c3_v3h[cpre], tok_f7, last_gbc_mm[0]] + stat_prev,
                        name=f"g5i{cpre}")
            rds = []
            for tt in range(2):
                jj = 2 * ch + tt
                op2 = jj % 2
                tok_n1 = sc.s(lambda h, jj=jj, tt=tt, cs=cs, op2=op2: h.activation(
                    out=outst2[op2][:, :], in_=v3slots[cs][:, tt * 1024:(tt + 1) * 1024],
                    func=AF.Identity, bias=fmu[:, jj:jj + 1]),
                    deps=[c5_in[ch], tok_f7] + out_dmas[-2:-1], name=f"g5n{ch}_{tt}")
                tok_n2 = sc.v(lambda h, jj=jj, op2=op2: h.tensor_scalar(
                    out=outst2[op2][:, :], in0=outst2[op2][:, :],
                    scalar1=fsq[:, jj:jj + 1], scalar2=None, op0=ALU.mult),
                    deps=[tok_n1], name=f"g5m{ch}_{tt}")
                rds.append(tok_n1)
                gtt = ch * 256 + tt * 128
                bloc2 = gtt // T
                trow = gtt % T
                out_dmas.append(sc.dma("SYNC", lambda h, bloc2=bloc2, trow=trow, op2=op2:
                                       h.dma_start(out=OUT[bloc2, trow:trow + 128, :],
                                                   in_=outst2[op2][:, :]),
                                       deps=[tok_n2], name=f"g5o{ch}_{tt}"))
            c5_readers[cs] = rds
            if ch + 2 < 16:
                c5_in[ch + 2] = sc.dma("SYNC", lambda h, ch=ch, cs=cs: h.dma_start(
                    out=v3slots[cs][:, :], in_=V3H.ap()[ch + 2]),
                    deps=[c3_v3h[ch + 2]] + rds, name=f"g5i{ch + 2}")

        block = e(nc.Block())

        @block.tensor
        def _(h):
            emit_engine_stream(sc, "T", h, sems)

        @block.vector
        def _(h):
            emit_engine_stream(sc, "V", h, sems)

        @block.scalar
        def _(h):
            emit_engine_stream(sc, "S", h, sems)

        @block.gpsimd
        def _(h):
            emit_engine_stream(sc, "G", h, sems)

        @block.sync
        def _(h):
            emit_engine_stream(sc, "SYNC", h, sems)
            for i in range(8):
                if sc.cnt[f"DS{i}"]:
                    h.wait_ge(sems[f"DS{i}"], sc.cnt[f"DS{i}"])
                if sc.cnt[f"DG{i}"]:
                    h.wait_ge(sems[f"DG{i}"], sc.cnt[f"DG{i}"])

    return nc, dbg_outs


def _in_maps(inputs):
    x = np.ascontiguousarray(inputs["L0_attn_x"])
    maps = []
    ren = {
        "norm_w": "L0_attn_norm_w", "in_proj_w": "L0_attn_in_proj_w", "conv_w": "L0_attn_conv_w",
        "conv_b": "L0_attn_conv_b", "dt_bias": "L0_attn_dt_bias", "A_log": "L0_attn_A_log",
        "Dp": "L0_attn_D", "ssm_norm_w": "L0_attn_ssm_norm_w", "out_proj_w": "L0_attn_out_proj_w",
        "scorer_w1": "L0_attn_scorer_w1", "scorer_w2": "L0_attn_scorer_w2",
        "summ_w": "L0_attn_summ_w", "q_w": "L0_attn_q_w", "k_w": "L0_attn_k_w",
        "v_w": "L0_attn_v_w", "gate_w": "L0_attn_gate_w",
        "ffn_gate_w": "L0_ffn_gate_w", "ffn_gate_b": "L0_ffn_gate_b",
        "shared_wg": "L0_ffn_shared_wg", "shared_wu": "L0_ffn_shared_wu",
        "shared_wd": "L0_ffn_shared_wd", "experts_wg": "L0_ffn_experts_wg",
        "experts_wu": "L0_ffn_experts_wu", "experts_wd": "L0_ffn_experts_wd",
    }
    nw = np.ascontiguousarray(inputs["L0_attn_norm_w"].reshape(8, 128).T)
    sw = np.ascontiguousarray(inputs["L0_attn_ssm_norm_w"].reshape(16, 128).T)
    cw = np.ascontiguousarray(inputs["L0_attn_conv_w"].reshape(18, 128, 4)
                              .transpose(1, 0, 2).reshape(128, 72))
    cb = np.ascontiguousarray(inputs["L0_attn_conv_b"].reshape(18, 128).T)
    dp = np.ascontiguousarray(np.repeat(inputs["L0_attn_D"].reshape(16, 2), 64, axis=1)
                              .reshape(16, 128).T)
    for r in range(NCORES):
        m = {"x": x[r * B_LOC:(r + 1) * B_LOC],
             "normw_t": nw, "ssmw_t": sw, "convw_t": cw, "convb_t": cb, "dp_rep": dp}
        for k, v in ren.items():
            m[k] = np.ascontiguousarray(inputs[v])
        for nm, arr in CONSTS.items():
            m["c_" + nm] = arr
        maps.append(m)
    return maps


def kernel(**inputs):
    dbg = tuple(x for x in DBG.split(",") if x)
    nc, dbg_outs = build_program(dbg)
    res = run_bass_kernel_spmd(nc, _in_maps(inputs), core_ids=list(range(NCORES)))
    out = np.concatenate([r["out"] for r in res.results], axis=0)
    if dbg:
        kernel.dbg = [{k: r["dbg_" + k] for k in dbg} for r in res.results]
    return out
